# version 09
# baseline (speedup 1.0000x reference)
import numpy as np
from contextlib import ExitStack

import concourse.bass as bass
import concourse.mybir as mybir
from concourse.bass_utils import run_bass_kernel_spmd

F32 = mybir.dt.float32
F16 = mybir.dt.float16
U8 = mybir.dt.uint8
AF = mybir.ActivationFunctionType
ALU = mybir.AluOpType
AX = mybir.AxisListType

B, T, D = 8, 2048, 1024
NCH, NTI, NTAU = 8, 16, 4
E, H, DH = 8, 16, 64
N = B * T


NSEM = 8


class Q:

    def __init__(self, name, dma=False):
        self.name, self.dma = name, dma
        self.ops, self.n, self.sem = [], 0, None
        if dma:
            self.sems = [None] * NSEM
            self.counts = [0] * NSEM
            self.i = 0

    def wait(self, other, val=None):
        if other.dma:
            if isinstance(val, tuple):
                s, v = val
                self.ops.append(lambda e, o=other, s=s, v=v: e.wait_ge(o.sems[s], v))
            else:
                for s in range(NSEM):
                    v = other.counts[s]
                    if v > 0:
                        self.ops.append(lambda e, o=other, s=s, v=v: e.wait_ge(o.sems[s], v))
        else:
            v = other.n if val is None else val
            if v > 0:
                self.ops.append(lambda e, o=other, vv=v: e.wait_ge(o.sem, vv))

    def op(self, fn, serialize=None):
        if self.dma:
            s = self.i % NSEM
            self.i += 1
            self.counts[s] += 16
            tgt = self.counts[s]

            def run(e, fn=fn, s=s, tgt=tgt, sh=self):
                if tgt > 16:
                    e.wait_ge(sh.sems[s], tgt - 16)
                instr = fn(e)
                instr.then_inc(sh.sems[s], 16)
            self.ops.append(run)
            self.n += 16
            return (s, tgt)
        if serialize is None:
            serialize = self.name in ("vector", "scalar")
        self.n += 1
        tgt = self.n

        def run(e, fn=fn, tgt=tgt, serialize=serialize, sh=self):
            instr = fn(e)
            instr.then_inc(sh.sem, 1)
            if serialize:
                e.wait_ge(sh.sem, tgt)
        self.ops.append(run)
        return tgt

    def emit(self, engine):
        for f in self.ops:
            f(engine)


def build_nc(debug=False):
    nc = bass.Bass(trn_type="TRN2", target_bir_lowering=False)
    dram = {}

    def din(name, shape, dt=F32):
        dram[name] = nc.dram_tensor(name, list(shape), dt, kind="ExternalInput").ap()
        return dram[name]

    def dscr(name, shape, dt=F32):
        kind = "ExternalOutput" if debug else "Internal"
        dram[name] = nc.dram_tensor(name, list(shape), dt, kind=kind).ap()
        return dram[name]

    x_tm = din("x_tm", (B, T, D))
    x_fm = din("x_fm", (B, D, T))
    xo_tm = din("xo_tm", (T, D))
    xo_fm = din("xo_fm", (D, T))
    WinT = din("WinT", (D, D)); WigT = din("WigT", (D, D))
    WrgT = din("WrgT", (D, D)); WoutT = din("WoutT", (D, D))
    Wr_fm = din("Wr_fm", (128, 64))
    br_row = din("br_row", (1, E))
    big_col = din("big_col", (128, 8)); brg_col = din("brg_col", (128, 8))
    lam_col = din("lam_col", (128, 8)); h0_col = din("h0_col", (128, 8))
    nw_col = din("nw_col", (128, 8))
    W1moe = din("W1moe", (E, D, D)); W2moe = din("W2moe", (E, D, D))
    b1_col = din("b1_col", (E, 128, 8))
    b2stat = din("b2stat", (E, 8, 128))
    Wq_i = din("Wq", (D, D)); Wk_i = din("Wk", (D, D)); Wv_i = din("Wv", (D, D))
    Wo_i = din("Wo", (D, D)); Wb_i = din("Wb", (D, H))
    bb_col = din("bb_col", (16, 1))
    k0r = din("k0r", (32, 128, 16, 128), U8)
    k1r = din("k1r", (32, 128, 32, 128), U8)
    k2r = din("k2r", (8, 128, 32, 128), U8)
    s_sc = din("s_sc", (1, 3))
    ident_c = din("ident_c", (128, 128))
    iota_e = din("iota_e", (128, 128))
    iota8c = din("iota8c", (8, 1))
    tril4 = din("tril4", (4, 128, 512))
    selq = din("selq", (NCH, 128, 16))
    hbq = din("hbq", (NCH, 16, 128))
    onesrow = din("onesrow", (1, 128))
    e8sel = din("e8sel", (8, 1024))
    coreoh = din("coreoh", (1, 8))

    out = nc.dram_tensor("out", [T, D], F32, kind="ExternalOutput").ap()
    dram["out"] = out

    xnT_s = dscr("xnT_s", (D, T))
    g0_s = dscr("g0_s", (D, T))
    s0_s = dscr("s0_s", (D, T))
    x1_s = dscr("x1_s", (D, T))
    xn1_s = dscr("xn1_s", (D, T))
    qn_s = dscr("qn_s", (D, T))
    kb_s = dscr("kb_s", (D, T))
    vT_s = dscr("vT_s", (T, D))
    ao_s = dscr("ao_s", (D, T))
    s1_s = dscr("s1_s", (D, T))
    wem_s = dscr("wem_s", (8, T))
    kept_s = dscr("kept_s", (8, N))

    qT, qV, qS = Q("tensor"), Q("vector"), Q("scalar")
    qD, qG = Q("sync", dma=True), Q("gpsimd", dma=True)
    ALLQ = (qT, qV, qS, qD, qG)

    es = ExitStack()
    with es:
        A = es.enter_context(nc.sbuf_tensor("A", [128, 16384], F32))
        Bp = es.enter_context(nc.sbuf_tensor("Bp", [128, 16384], F32))
        C = es.enter_context(nc.sbuf_tensor("C", [128, 8192], F32))
        cst = es.enter_context(nc.sbuf_tensor("cst", [128, 1024], F32))
        rowp = es.enter_context(nc.sbuf_tensor("rowp", [64, 6400], F32))
        wem = es.enter_context(nc.sbuf_tensor("wem", [8, T], F32))
        rsb = es.enter_context(nc.sbuf_tensor("rsb", [128, 512], F32))
        sc0 = es.enter_context(nc.sbuf_tensor("sc0", [128, 512], F32))
        sc1 = es.enter_context(nc.sbuf_tensor("sc1", [128, 512], F32))
        stats = es.enter_context(nc.sbuf_tensor("stats", [128, 256], F32))
        gates = es.enter_context(nc.sbuf_tensor("gates", [128, 152], F32))
        colb = es.enter_context(nc.sbuf_tensor("colb", [128, 64], F32))
        ltm = es.enter_context(nc.sbuf_tensor("ltm", [128, 352], F32))
        f16c = es.enter_context(nc.sbuf_tensor("f16c", [128, 256], F16))

        p = [es.enter_context(nc.psum_tensor("p%d" % i, [128, 512], F32))
             for i in range(8)]

        for q in ALLQ:
            if q.dma:
                q.sems = [es.enter_context(nc.semaphore("sem_%s_%d" % (q.name, i)))
                          for i in range(NSEM)]
            else:
                q.sem = es.enter_context(nc.semaphore("sem_" + q.name))

        sc2 = C[:, 7168:7680]
        A16 = A[:].bitcast(F16)
        B16 = Bp[:].bitcast(F16)
        C16 = C[:].bitcast(F16)

        emp = A[0:8, 0:16384]
        emp2 = Bp[0:8, 0:16384]

        def mm(ps, stat, mov, start, stop, tp=None):
            return qT.op(lambda t, a=ps, b=stat, c=mov, s1=start, s2=stop, tp=tp:
                         t.matmul(a, b, c, start=s1, stop=s2, tile_position=tp),
                         serialize=False)

        def tr(ps, src, idn=None):
            return qT.op(lambda t, a=ps, b=src, i=idn: t.transpose(
                a, b, i if i is not None else IDENT), serialize=False)

        def ld(dst, src, q=None):
            q = q or qD
            if not isinstance(dst, bass.AP):
                dst = dst[:]
            if not isinstance(src, bass.AP):
                src = src[:]
            return q.op(lambda e, d=dst, s=src: e.dma_start(out=d, in_=s),
                        serialize=False)

        def vop(fn):
            return qV.op(fn)

        def sop(fn):
            return qS.op(fn)

        def ch(pool, c, w=T):
            return pool[:, c * w:(c + 1) * w]

        def barrier():
            for q in ALLQ:
                for o in ALLQ:
                    if o is not q:
                        q.wait(o)

        ld(cst[:, 0:128], ident_c)
        ld(cst[:, 128:256], iota_e)
        for c in range(NCH):
            ld(cst[:, 256 + c * 16:256 + (c + 1) * 16], selq[c])
        ld(cst[0:1, 384:512], onesrow)
        ld(cst[:, 512:576], Wr_fm)
        ld(cst[0:8, 648:649], iota8c)
        ld(cst[0:1, 656:664], br_row)
        ld(cst[0:1, 664:672], coreoh)
        ld(cst[0:1, 672:675], s_sc)
        ld(colb[:, 0:8], big_col); ld(colb[:, 8:16], brg_col)
        ld(colb[:, 16:24], lam_col); ld(colb[:, 24:32], h0_col)
        ld(colb[:, 32:40], nw_col); ld(colb[0:16, 40:41], bb_col)
        n_cload, n_gload = qD.n, qG.n

        IDENT = cst[:, 0:128]
        IOTA3 = cst[:, 128:256].rearrange("p (i c) -> p i c", c=8)
        SEL = lambda c: cst[:, 256 + c * 16:256 + (c + 1) * 16]
        ONESROW = cst[0:1, 384:512]
        WRC = lambda k: cst[:, 512 + k * 8:512 + (k + 1) * 8]
        IOTA8C = cst[0:8, 648:649]
        BRROW = cst[0:1, 656:664]
        COREOH = cst[0:1, 664:672]
        SSC = cst[0:1, 672:675]
        TRIL = lambda j: C16[:, j * 512:(j + 1) * 512]

        qV.wait(qD)
        vop(lambda v: v.memset(cst[:, 676:677], 1.0))
        ONESCOL = cst[:, 676:677]
        vop(lambda v: v.memset(cst[:, 704:776], 1.0))
        for k in range(NCH):
            vop(lambda v, k=k: v.tensor_copy(cst[:, 704 + k * 9:704 + k * 9 + 8], WRC(k)))
        WRONES = lambda k: cst[:, 704 + k * 9:704 + k * 9 + 9]
        vop(lambda v: v.memset(f16c[0:1, 0:128], 1.0))
        vop(lambda v: v.memset(f16c[:, 128:129], -0.5))
        ONESROW16 = f16c[0:1, 0:128]
        NEGH16 = f16c[:, 128:129]
        n_vconst = qV.n

        qT.wait(qD)
        qT.wait(qV, n_vconst)
        mm(p[0][0:8, 0:8], ONESROW[0:1, 0:8], COREOH, True, True)
        mm(p[0][:, 8:16], ONESROW, COREOH, True, True)
        mm(p[0][:, 16:19], ONESROW, SSC, True, True)
        qV.wait(qT, qT.n)
        vop(lambda v: v.tensor_copy(gates[0:8, 128:136], p[0][0:8, 0:8]))
        vop(lambda v: v.tensor_copy(gates[:, 136:144], p[0][:, 8:16]))
        vop(lambda v: v.tensor_copy(colb[:, 44:47], p[0][:, 16:19]))
        COREOH8 = gates[0:8, 128:136]
        COREOHB = gates[:, 136:144]
        S0C, S1C, S2C = colb[:, 44:45], colb[:, 45:46], colb[:, 46:47]
        vop(lambda v: v.tensor_scalar(colb[:, 47:48], S0C, 8.0, None, ALU.mult, ALU.bypass))
        vop(lambda v: v.tensor_scalar(colb[:, 48:49], S1C, 4.0, None, ALU.mult, ALU.bypass))
        S0C8, S1C4 = colb[:, 47:48], colb[:, 48:49]
        vop(lambda v: v.memset(colb[:, 49:50], 1e-5))
        vop(lambda v: v.memset(colb[:, 50:51], 1e-6))
        vop(lambda v: v.memset(colb[:, 51:52], 1.0 + 1e-6))
        vop(lambda v: v.memset(colb[:, 52:53], -1.0))
        EPS5, EPS6, GEPS, NEG1 = (colb[:, 49:50], colb[:, 50:51],
                                  colb[:, 51:52], colb[:, 52:53])
        qS.wait(qD)
        sop(lambda s: s.activation(colb[:, 56:64], colb[:, 16:24], AF.Sigmoid))
        qV.wait(qS, qS.n)
        vop(lambda v: v.tensor_scalar(colb[:, 56:64], colb[:, 56:64], 1e-8, None, ALU.add, ALU.bypass))
        qS.wait(qV, qV.n)
        sop(lambda s: s.activation(colb[:, 56:64], colb[:, 56:64], AF.Ln))
        qV.wait(qS, qS.n)
        vop(lambda v: v.tensor_scalar(colb[:, 56:64], colb[:, 56:64], 8.0, None, ALU.mult, ALU.bypass))
        CLOGA = colb[:, 56:64]
        qT.wait(qV, qV.n)

        tm0, tm1 = C[:, 0:1024], C[:, 1024:2048]
        SINK = C[:, 2048:3072]
        TMP3 = C[:, 3072:3200].rearrange("p (i c) -> p i c", c=8)
        DIFF3 = C[:, 3200:3328].rearrange("p (i c) -> p i c", c=8)
        MB3 = C[:, 3328:3456].rearrange("p (i c) -> p i c", c=8)
        IDXROW = rowp[0:1, 2304:2304 + T]
        GATEROW = rowp[0:1, 2304:2304 + T]

        sq_tok = {}
        last_sq = [0, 0]

        def emit_prepass(b):
            for i in range(NTI):
                gi = b * NTI + i
                buf = tm0 if gi % 2 == 0 else tm1
                if last_sq[gi % 2] > 0:
                    qD.wait(qS, last_sq[gi % 2])
                t_ld = ld(buf, x_tm[b, i * 128:(i + 1) * 128, :], q=qD)
                qS.wait(qD, t_ld)
                sq_tok[b] = last_sq[gi % 2] = sop(lambda s, bu=buf, b=b, i=i: s.activation(
                    SINK, bu, AF.Square, accum_out=stats[:, b * 16 + i:b * 16 + i + 1]))
        LRAWp = (rowp[0:9, 0:2048], rowp[32:41, 0:2048])
        LTMp = (ltm[:, 0:176], ltm[:, 176:352])
        P_LOG = (p[0], p[4])
        P_TR = (p[1], p[5])
        P_BC = (p[2], p[6])
        DIFFp = (C[:, 3072:3200], C[:, 3456:3584])
        MBp = (C[:, 3200:3328], C[:, 3584:3712])
        TMPp = (C[:, 3328:3456], C[:, 3712:3840])
        IDXRp = (rowp[0:1, 2304:2304 + T], rowp[0:1, 4352:4352 + T])
        FLATp = (rowp[0:16, 2048:2176], rowp[32:48, 2048:2176])
        SBASE = (128, 192)
        log_evac = [0, 0]
        tr_evac = {}
        ltm_free = [0, 0]
        mm_log = [0, 0]
        memq_tok = [0, 0]
        for b in range(B):
            emit_prepass(b)
            par = b % 2
            LRAW = LRAWp[par]
            L3 = LTMp[par].rearrange("p (i c) -> p i c", c=11)
            pL, pT_, pBc = P_LOG[par], P_TR[par], P_BC[par]
            S0 = SBASE[par]
            TMP3 = TMPp[par].rearrange("p (i c) -> p i c", c=8)
            DIFF3 = DIFFp[par].rearrange("p (i c) -> p i c", c=8)
            MB3 = MBp[par].rearrange("p (i c) -> p i c", c=8)
            for tau in range(NTAU):
                for k in range(NCH):
                    buf = sc0 if k % 2 == 0 else sc1
                    nmm = qT.n
                    if b + tau + k > 0:
                        qD.wait(qT, nmm - 1 if nmm >= 2 else 0)
                    lk = ld(buf, x_fm[b, k * 128:(k + 1) * 128, tau * 512:(tau + 1) * 512])
                    qT.wait(qD, lk)
                    if k == 0 and log_evac[par] > 0:
                        qT.wait(qV, log_evac[par])
                    mm(pL[0:9, 0:512], WRONES(k), buf[:], k == 0, k == NCH - 1)
                qV.wait(qT, qT.n)
                log_evac[par] = vop(lambda v, tau=tau, LRAW=LRAW, pL=pL: v.tensor_copy(
                    LRAW[:, tau * 512:(tau + 1) * 512], pL[0:9, 0:512]))
            if b == 0:
                qT.wait(qV, qV.n)
                for k in range(NCH):
                    mm(p[3][0:1, 0:8], ONESCOL, WRC(k), k == 0, k == NCH - 1)
                qV.wait(qT, qT.n)
                vop(lambda v: v.tensor_copy(cst[0:1, 680:688], p[3][0:1, 0:8]))
                qT.wait(qV, qV.n)
                mm(p[3][:, 0:8], ONESROW, cst[0:1, 680:688], True, True)
                mm(p[3][:, 8:16], ONESROW, BRROW, True, True)
                qV.wait(qT, qT.n)
                vop(lambda v: v.tensor_copy(cst[:, 688:696], p[3][:, 0:8]))
                vop(lambda v: v.tensor_copy(cst[:, 696:704], p[3][:, 8:16]))
                qT.wait(qV, qV.n)
            COLSB = cst[:, 688:696].unsqueeze(1).broadcast_to((128, 16, 8))
            BRB = cst[:, 696:704].unsqueeze(1).broadcast_to((128, 16, 8))
            for g4 in range(4):
                ptr = pT_ if g4 % 2 == 0 else p[7]
                pid = (par, 0) if g4 % 2 == 0 else ("g7",)
                if tr_evac.get(pid, 0) > 0:
                    qT.wait(qV, tr_evac[pid])
                if g4 == 0:
                    qT.wait(qV, log_evac[par])
                    if ltm_free[par] > 0:
                        qT.wait(qV, ltm_free[par])
                t1 = 0
                for jj in range(4):
                    i = g4 * 4 + jj
                    t1 = tr(ptr[0:128, jj * 9:(jj + 1) * 9], LRAW[:, i * 128:(i + 1) * 128],
                            IDENT[0:9, 0:9] if par == 0 else IDENT[32:41, 32:41])
                qV.wait(qT, t1)
                tr_evac[pid] = vop(lambda v, g4=g4, ptr=ptr, L3=L3: v.tensor_copy(
                    L3[:, g4 * 4:(g4 + 1) * 4, 0:9],
                    ptr[:, 0:36].rearrange("p (a b) -> p a b", b=9)))
            LT = L3[:, :, 0:8]
            SUMR = L3[:, :, 8:9]
            qV.wait(qS, sq_tok[b])
            vop(lambda v, S0=S0, SUMR=SUMR: v.tensor_scalar(
                stats[:, S0:S0 + 16], SUMR.rearrange("p i c -> p (i c)"), 1.0 / D, None, ALU.mult, ALU.bypass))
            vop(lambda v, S0=S0, b=b: v.tensor_scalar(
                stats[:, S0 + 16:S0 + 32], stats[:, b * 16:(b + 1) * 16], 1.0 / D, None, ALU.mult, ALU.bypass))
            vop(lambda v, S0=S0: v.tensor_tensor(
                stats[:, S0 + 32:S0 + 48], stats[:, S0:S0 + 16], stats[:, S0:S0 + 16], ALU.mult))
            vv = vop(lambda v, S0=S0: v.tensor_tensor(
                stats[:, S0 + 16:S0 + 32], stats[:, S0 + 16:S0 + 32], stats[:, S0 + 32:S0 + 48], ALU.subtract))
            qS.wait(qV, vv)
            ss1 = sop(lambda s, S0=S0: s.activation(
                stats[:, S0 + 32:S0 + 48], stats[:, S0 + 16:S0 + 32], AF.Sqrt, bias=EPS5, scale=1.0))
            qV.wait(qS, ss1)
            vop(lambda v, S0=S0: v.reciprocal(stats[:, S0 + 48:S0 + 64], stats[:, S0 + 32:S0 + 48]))
            ISTD3 = stats[:, S0 + 48:S0 + 64].unsqueeze(2).broadcast_to((128, 16, 8))
            vop(lambda v, S0=S0: v.tensor_tensor(
                stats[:, S0:S0 + 16], stats[:, S0:S0 + 16], stats[:, S0 + 48:S0 + 64], ALU.mult))
            MISTD3 = stats[:, S0:S0 + 16].unsqueeze(2).broadcast_to((128, 16, 8))
            vop(lambda v, LT=LT, ISTD3=ISTD3: v.tensor_tensor(LT, LT, ISTD3, ALU.mult))
            vop(lambda v, TMP3=TMP3, MISTD3=MISTD3: v.tensor_copy(TMP3, MISTD3))
            vop(lambda v, TMP3=TMP3, COLSB=COLSB: v.tensor_tensor(TMP3, TMP3, COLSB, ALU.mult))
            vop(lambda v, LT=LT, TMP3=TMP3: v.tensor_tensor(LT, LT, TMP3, ALU.subtract))
            vop(lambda v, LT=LT, BRB=BRB: v.tensor_tensor(LT, LT, BRB, ALU.add))
            NL = L3[:, :, 9:10]
            vop(lambda v, NL=NL, LT=LT: v.tensor_reduce(NL, LT, axis=AX.X, op=ALU.max, negate=True))
            dv = vop(lambda v, DIFF3=DIFF3, LT=LT, NL=NL: v.tensor_tensor(
                DIFF3, LT, NL.broadcast_to((128, 16, 8)), ALU.add))
            vop(lambda v, MB3=MB3, DIFF3=DIFF3: v.tensor_scalar(MB3, DIFF3, 0.0, None, ALU.is_lt, ALU.bypass))
            vop(lambda v, MB3=MB3: v.tensor_scalar(MB3, MB3, 1024.0, None, ALU.mult, ALU.bypass))
            vop(lambda v, MB3=MB3: v.tensor_tensor(MB3, MB3, IOTA3, ALU.add))
            vop(lambda v, L3=L3, MB3=MB3: v.tensor_reduce(L3[:, :, 10:11], MB3, axis=AX.X, op=ALU.min))
            qS.wait(qV, qV.n)
            se = sop(lambda s, MB3=MB3, DIFF3=DIFF3: s.activation(MB3, DIFF3, AF.Exp))
            qV.wait(qS, se)
            vop(lambda v, L3=L3, MB3=MB3: v.tensor_reduce(L3[:, :, 9:10], MB3, axis=AX.X, op=ALU.add))
            vop(lambda v, b=b, L3=L3: v.reciprocal(gates[:, b * 16:(b + 1) * 16],
                                                   L3[:, :, 9:10].rearrange("p i c -> p (i c)")))
            ltm_free[par] = vop(lambda v, S0=S0, L3=L3: v.tensor_copy(
                stats[:, S0 + 16:S0 + 32], L3[:, :, 10:11].rearrange("p i c -> p (i c)")))
            qT.wait(qV, ltm_free[par])
            if tr_evac.get((par, 0), 0) > 0:
                qT.wait(qV, tr_evac[(par, 0)])
            t2 = tr(pT_[0:16, 0:128], stats[:, S0 + 16:S0 + 32])
            tr_evac[(par, 0)] = None
            qV.wait(qT, t2)
            fe = vop(lambda v, par=par, pT_=pT_: v.tensor_copy(FLATp[par], pT_[0:16, 0:128]))
            tr_evac[(par, 0)] = fe
            qG.wait(qV, fe)
            fl = ld(IDXRp[par], FLATp[par], q=qG)
            qT.wait(qG, fl)
            for tau in range(NTAU):
                pb2 = pBc if tau % 2 == 0 else p[3]
                key = (par, "m") if tau % 2 == 0 else ("m3",)
                if tr_evac.get(key, 0) > 0:
                    qT.wait(qV, tr_evac[key])
                mmv = mm(pb2[0:8, 0:512], ONESROW[0:1, 0:8],
                         IDXRp[par][0:1, tau * 512:(tau + 1) * 512], True, True)
                qV.wait(qT, mmv)
                tr_evac[key] = memq_tok[par] = vop(lambda v, b=b, tau=tau, pb2=pb2: v.tensor_scalar(
                    emp[:, b * T + tau * 512:b * T + (tau + 1) * 512],
                    pb2[0:8, 0:512], IOTA8C, None, ALU.is_equal, ALU.bypass))
        qD.wait(qV, qV.n)
        qG.wait(qV, qV.n)

        for j in range(4):
            init = 0.0 if j == 0 else emp2[:, j * 4096 - 1:j * 4096]
            vop(lambda v, j=j, init=init: v.tensor_tensor_scan(
                emp2[:, j * 4096:(j + 1) * 4096],
                ONESCOL[0:8, 0:1].broadcast_to((8, 4096)),
                emp[:, j * 4096:(j + 1) * 4096], init, ALU.mult, ALU.add))
        vop(lambda v: v.tensor_tensor(emp2[:], emp2[:], emp[:], ALU.mult))
        vop(lambda v: v.tensor_scalar(emp2[:], emp2[:], 2048.0, None, ALU.is_lt, ALU.bypass))
        vop(lambda v: v.tensor_tensor(emp2[:], emp2[:], emp[:], ALU.mult))
        if debug:
            qG.wait(qV, qV.n)
            ld(kept_s, emp2[:], q=qG)
        vop(lambda v: v.memset(stats[:, 0:16], 0.0))
        for b in range(B):
            vop(lambda v, b=b: v.tensor_scalar(stats[:, 16:32], gates[:, b * 16:(b + 1) * 16],
                                               COREOHB[:, b:b + 1], None, ALU.mult, ALU.bypass))
            vop(lambda v: v.tensor_tensor(stats[:, 0:16], stats[:, 0:16], stats[:, 16:32], ALU.add))
        qT.wait(qV, qV.n)
        t3 = tr(p[1][0:16, 0:128], stats[:, 0:16])
        qV.wait(qT, t3)
        vop(lambda v: v.tensor_copy(rowp[0:16, 2048:2176], p[1][0:16, 0:128]))
        qG.wait(qV, qV.n)
        fl2 = ld(GATEROW, rowp[0:16, 2048:2176], q=qG)
        vop(lambda v: v.memset(wem[:], 0.0))
        for b in range(B):
            vop(lambda v, b=b: v.tensor_scalar(emp[:, 0:T], emp2[:, b * T:(b + 1) * T],
                                               COREOH8[:, b:b + 1], None, ALU.mult, ALU.bypass))
            vop(lambda v: v.tensor_tensor(wem[:], wem[:], emp[:, 0:T], ALU.add))
        qT.wait(qG, fl2)
        qT.wait(qV, qV.n)
        for tau in range(NTAU):
            mmg = mm(p[2][0:8, 0:512], ONESROW[0:1, 0:8],
                     GATEROW[0:1, tau * 512:(tau + 1) * 512], True, True)
            qV.wait(qT, mmg)
            vop(lambda v, tau=tau: v.tensor_tensor(
                wem[:, tau * 512:(tau + 1) * 512], wem[:, tau * 512:(tau + 1) * 512],
                p[2][0:8, 0:512], ALU.mult))
            qT.wait(qV, qV.n)
        if debug:
            qG.wait(qV, qV.n)
            ld(wem_s, wem[:], q=qG)

        barrier()
        s_base = qS.n
        for i in range(NTI):
            buf = tm0 if i % 2 == 0 else tm1
            if i >= 2:
                qD.wait(qS, s_base + 2 * (i - 1))
            t_ld = ld(buf, xo_tm[i * 128:(i + 1) * 128, :])
            qS.wait(qD, t_ld)
            sop(lambda s, bu=buf, i=i: s.activation(
                SINK, bu, AF.Square, accum_out=stats[:, i:i + 1]))
            sop(lambda s, bu=buf, i=i: s.activation(
                SINK, bu, AF.Identity, accum_out=stats[:, 16 + i:17 + i]))
        qV.wait(qS, qS.n)
        vop(lambda v: v.tensor_scalar(stats[:, 16:32], stats[:, 16:32], 1.0 / D, None, ALU.mult, ALU.bypass))
        vop(lambda v: v.tensor_scalar(stats[:, 32:48], stats[:, 0:16], 1.0 / D, None, ALU.mult, ALU.bypass))
        vop(lambda v: v.tensor_tensor(stats[:, 48:64], stats[:, 16:32], stats[:, 16:32], ALU.mult))
        vop(lambda v: v.tensor_tensor(stats[:, 32:48], stats[:, 32:48], stats[:, 48:64], ALU.subtract))
        qS.wait(qV, qV.n)
        sop(lambda s: s.activation(stats[:, 48:64], stats[:, 32:48], AF.Sqrt, bias=EPS5, scale=1.0))
        qV.wait(qS, qS.n)
        vop(lambda v: v.reciprocal(stats[:, 0:16], stats[:, 48:64]))
        qT.wait(qV, qV.n)
        t4 = tr(p[1][0:32, 0:128], stats[:, 0:32])
        qV.wait(qT, t4)
        vop(lambda v: v.tensor_copy(rowp[0:32, 2048:2176], p[1][0:32, 0:128]))
        qG.wait(qV, qV.n)
        fl2a = ld(rowp[0:1, 0:T], rowp[0:16, 2048:2176], q=qG)
        fl3 = ld(rowp[0:1, 4352:4352 + T], rowp[16:32, 2048:2176], q=qG)
        ISTDR, MR = rowp[0:1, 0:T], rowp[0:1, 4352:4352 + T]
        qT.wait(qG, fl2a)
        qT.wait(qG, fl3)
        for c in range(NCH):
            for tau in range(NTAU):
                mm(p[1][:, 0:512], ONESROW, MR[0:1, tau * 512:(tau + 1) * 512], True, True)
                m2 = mm(p[2][:, 0:512], ONESROW, ISTDR[0:1, tau * 512:(tau + 1) * 512], True, True)
                lk = ld(sc0, xo_fm[c * 128:(c + 1) * 128, tau * 512:(tau + 1) * 512])
                qV.wait(qT, m2)
                qV.wait(qD, lk)
                dst = ch(A, c)[:, tau * 512:(tau + 1) * 512]
                vop(lambda v, dst=dst: v.tensor_tensor(dst, sc0[:], p[1][:, 0:512], ALU.subtract))
                vop(lambda v, dst=dst: v.tensor_tensor(dst, dst, p[2][:, 0:512], ALU.mult))
                qD.wait(qV, qV.n)
                qT.wait(qV, qV.n)
            qG.wait(qV, qV.n)
            ld(xnT_s[c * 128:(c + 1) * 128, :], ch(A, c), q=qG)

        barrier()
        ld(rowp[0:8, 2304:3328], b2stat.rearrange("e c f -> e (c f)"), q=qD)
        B2S = lambda c: rowp[0:8, 2304 + c * 128:2304 + (c + 1) * 128]
        WBp = (C[:, 2048:3072], C[:, 3072:4096])
        wbuf = (C[:, 0:1024], C[:, 1024:2048])
        ptok = [0] * 8
        ld(rowp[0:8, 3328:4352], e8sel, q=qD)
        ld(stats[:, 128:192].rearrange("p (e c) -> p e c", c=8), b1_col.rearrange("e p c -> p e c"), q=qD)
        E8S = lambda e: rowp[0:8, 3328 + e * 128:3328 + (e + 1) * 128]
        qT.wait(qD)
        qS.wait(qD)
        wb_done = {}
        def make_wb(half, e):
            hof2 = half * 1024
            ep2 = e % 2
            for t2 in range(2):
                if ptok[0] > 0:
                    qT.wait(qV, ptok[0])
                m3 = mm(p[0][:, 0:512], E8S(e),
                        wem[0:8, hof2 + t2 * 512:hof2 + (t2 + 1) * 512], True, True)
                qV.wait(qT, m3)
                ptok[0] = vop(lambda v, t2=t2, ep2=ep2: v.tensor_copy(
                    WBp[ep2][:, t2 * 512:(t2 + 1) * 512], p[0][:, 0:512]))
            wb_done[(half, e)] = ptok[0]
        for half in range(2):
            hof = half * 1024
            qV.wait(qG)
            qS.wait(qG)
            for e in range(E):
                ep = e % 2
                if (half, e) not in wb_done:
                    make_wb(half, e)
                B1C = stats[:, 128 + e * 8:128 + (e + 1) * 8]
                WB0 = WBp[ep][:, 0:512]
                WB1 = WBp[ep][:, 512:1024]
                if e + 1 < E:
                    make_wb(half, e + 1)
                for li, Wm in enumerate((W1moe, W2moe)):
                    if li == 1:
                        qT.wait(qV, qV.n)
                        qT.wait(qS, qS.n)
                    for t2 in range(2):
                        tslc = slice(hof + t2 * 512, hof + (t2 + 1) * 512)
                        for k in range(NCH):
                            wb = wbuf[k % 2]
                            qD.wait(qT, qT.n - 8 if k >= 2 else qT.n)
                            lw = ld(wb, Wm[e, k * 128:(k + 1) * 128, :])
                            qT.wait(qD, lw)
                            for f in range(NCH):
                                src = (ch(A, k)[:, tslc] if li == 0
                                       else Bp[:, k * 1024 + t2 * 512:k * 1024 + (t2 + 1) * 512])
                                if k == 0 and ptok[f] > 0:
                                    qT.wait(qV, ptok[f])
                                mm(p[f][:, 0:512], wb[:, f * 128:(f + 1) * 128], src,
                                   k == 0, k == NCH - 1)
                        qV.wait(qT, qT.n)
                        for f in range(NCH):
                            if li == 0:
                                dsth = Bp[:, f * 1024 + t2 * 512:f * 1024 + (t2 + 1) * 512]
                                ptok[f] = vop(lambda v, f=f, dsth=dsth, wb2=(WB0 if t2 == 0 else WB1):
                                              v.tensor_tensor(dsth, p[f][:, 0:512], wb2, ALU.mult))
                                qS.wait(qV, ptok[f])
                                stok = sop(lambda s, dsth=dsth, f=f, B1C=B1C: s.activation(
                                    dsth, dsth, AF.Relu, bias=B1C[:, f:f + 1], scale=1.0))
                                qV.wait(qS, stok)
                                vop(lambda v, dsth=dsth, wb2=(WB0 if t2 == 0 else WB1):
                                    v.tensor_tensor(dsth, dsth, wb2, ALU.mult))
                            else:
                                ydst = Bp[:, 8192 + f * 1024 + t2 * 512:8192 + f * 1024 + (t2 + 1) * 512]
                                if e == 0:
                                    ptok[f] = vop(lambda v, ydst=ydst, f=f: v.tensor_copy(ydst, p[f][:, 0:512]))
                                else:
                                    ptok[f] = vop(lambda v, ydst=ydst, f=f: v.tensor_tensor(
                                        ydst, ydst, p[f][:, 0:512], ALU.add))
            qT.wait(qV, qV.n)
            for f in range(NCH):
                for t2 in range(2):
                    if ptok[0] > 0:
                        qT.wait(qV, ptok[0])
                    m4 = mm(p[0][:, 0:512], B2S(f),
                            wem[0:8, hof + t2 * 512:hof + (t2 + 1) * 512], True, True)
                    qV.wait(qT, m4)
                    ydst = Bp[:, 8192 + f * 1024 + t2 * 512:8192 + f * 1024 + (t2 + 1) * 512]
                    ptok[0] = vop(lambda v, ydst=ydst: v.tensor_tensor(ydst, ydst, p[0][:, 0:512], ALU.add))
                qG.wait(qV, qV.n)
                ld(g0_s[f * 128:(f + 1) * 128, hof:hof + 1024],
                   Bp[:, 8192 + f * 1024:8192 + (f + 1) * 1024], q=qG)

        barrier()
        WPAN = C[:, 0:3072]
        XP = C[:, 3072:5120]
        R_ = C[:, 5120:7168]
        qD.wait(qG)
        for c in range(NCH):
            nw = []
            for oi, WT in enumerate((WinT, WigT, WrgT)):
                nw.append(ld(WPAN[:, oi * 1024:(oi + 1) * 1024].rearrange("p (k f) -> p k f", f=128),
                             WT[:, c * 128:(c + 1) * 128].rearrange("(k p) f -> p k f", p=128)))
            qT.wait(qD)
            I_ = ch(Bp, c)
            for tau in range(NTAU):
                for k in range(NCH):
                    for oi in range(3):
                        mm(p[oi][:, 0:512], WPAN[:, oi * 1024 + k * 128:oi * 1024 + (k + 1) * 128],
                           ch(A, k)[:, tau * 512:(tau + 1) * 512], k == 0, k == NCH - 1)
                qV.wait(qT, qT.n)
                qS.wait(qT, qT.n)
                sl = slice(tau * 512, (tau + 1) * 512)
                vop(lambda v, sl=sl: v.tensor_copy(XP[:, sl], p[0][:, 0:512]))
                sop(lambda s, sl=sl, c=c: s.activation(I_[:, sl], p[1][:, 0:512], AF.Sigmoid,
                                                       bias=colb[:, 0 + c:1 + c], scale=1.0))
                sop(lambda s, sl=sl, c=c: s.activation(R_[:, sl], p[2][:, 0:512], AF.Sigmoid,
                                                       bias=colb[:, 8 + c:9 + c], scale=1.0))
                qT.wait(qV, qV.n)
                qT.wait(qS, qS.n)
            qV.wait(qS, qS.n)
            vop(lambda v, c=c: v.tensor_scalar(R_[:], R_[:], CLOGA[:, c:c + 1], None, ALU.mult, ALU.bypass))
            qS.wait(qV, qV.n)
            sop(lambda s: s.activation(R_[:], R_[:], AF.Exp))
            qV.wait(qS, qS.n)
            vop(lambda v: v.tensor_tensor(XP[:], XP[:], I_[:], ALU.mult))
            vop(lambda v: v.tensor_tensor(I_[:], R_[:], R_[:], ALU.mult))
            qS.wait(qV, qV.n)
            sop(lambda s: s.activation(I_[:], I_[:], AF.Sqrt, bias=GEPS, scale=NEG1))
            qV.wait(qS, qS.n)
            vop(lambda v: v.tensor_tensor(XP[:], XP[:], I_[:], ALU.mult))
            vop(lambda v, c=c: v.tensor_tensor_scan(
                ch(Bp, c), R_[:], XP[:], colb[:, 24 + c:25 + c], ALU.mult, ALU.add))
            qD.wait(qV, qV.n)
        for c in range(NCH):
            vop(lambda v, c=c: v.tensor_tensor(ch(A, c), ch(Bp, c), ch(Bp, c), ALU.mult))
        qT.wait(qV, qV.n)
        for tau in range(NTAU):
            for c in range(NCH):
                mm(p[3][0:1, 0:512], ONESCOL, ch(A, c)[:, tau * 512:(tau + 1) * 512],
                   c == 0, c == NCH - 1)
            qV.wait(qT, qT.n)
            vop(lambda v, tau=tau: v.tensor_copy(rowp[0:1, tau * 512:(tau + 1) * 512], p[3][0:1, 0:512]))
            qT.wait(qV, qV.n)
        vop(lambda v: v.tensor_scalar(rowp[0:1, 0:T], rowp[0:1, 0:T], 1.0 / D, None, ALU.mult, ALU.bypass))
        qS.wait(qV, qV.n)
        sop(lambda s: s.activation(rowp[0:1, 0:T], rowp[0:1, 0:T], AF.Sqrt, bias=EPS6[0:1, :], scale=1.0))
        qV.wait(qS, qS.n)
        vop(lambda v: v.reciprocal(rowp[0:1, 2304:2304 + T], rowp[0:1, 0:T]))
        RMSR = rowp[0:1, 2304:2304 + T]
        qT.wait(qV, qV.n)
        for c in range(NCH):
            for tau in range(NTAU):
                m5 = mm(p[4][:, 0:512], ONESROW, RMSR[0:1, tau * 512:(tau + 1) * 512], True, True)
                qV.wait(qT, m5)
                dst = ch(A, c)[:, tau * 512:(tau + 1) * 512]
                vop(lambda v, dst=dst, c=c, tau=tau: v.tensor_tensor(
                    dst, ch(Bp, c)[:, tau * 512:(tau + 1) * 512], p[4][:, 0:512], ALU.mult))
                qT.wait(qV, qV.n)
            vop(lambda v, c=c: v.tensor_scalar(ch(A, c), ch(A, c), colb[:, 32 + c:33 + c],
                                               None, ALU.mult, ALU.bypass))
        def proj_stream(W_src, act_pool, out_to_Bp=True, transposed_w=False):
            qV.wait(qG)
            qT.wait(qV, qV.n)
            ptok2 = [0] * 8
            for tau in range(NTAU):
                for k in range(NCH):
                    wb = wbuf[k % 2]
                    qD.wait(qT, qT.n - 8 if (k >= 2 or tau > 0) else qT.n)
                    lw = ld(wb, W_src[k * 128:(k + 1) * 128, :])
                    qT.wait(qD, lw)
                    for f in range(NCH):
                        if k == 0 and ptok2[f] > 0:
                            qT.wait(qV, ptok2[f])
                        mm(p[f][:, 0:512], wb[:, f * 128:(f + 1) * 128],
                           ch(act_pool, k)[:, tau * 512:(tau + 1) * 512], k == 0, k == NCH - 1)
                qV.wait(qT, qT.n)
                for f in range(NCH):
                    ptok2[f] = vop(lambda v, f=f, tau=tau: v.tensor_copy(
                        ch(Bp, f)[:, tau * 512:(tau + 1) * 512], p[f][:, 0:512]))
            qT.wait(qV, qV.n)

        qT.wait(qV, qV.n)
        proj_stream(WoutT, A)
        for f in range(NCH):
            qG.wait(qV, qV.n)
            ld(s0_s[f * 128:(f + 1) * 128, :], ch(Bp, f), q=qG)

        barrier()
        qD.wait(qG)
        for tau in range(NTAU):
            for c in range(NCH):
                l1 = ld(sc0, xo_fm[c * 128:(c + 1) * 128, tau * 512:(tau + 1) * 512])
                l2_ = ld(sc1, g0_s[c * 128:(c + 1) * 128, tau * 512:(tau + 1) * 512])
                l3_ = ld(sc2, s0_s[c * 128:(c + 1) * 128, tau * 512:(tau + 1) * 512])
                qS.wait(qD, l2_)
                sop(lambda s: s.activation(sc1[:], sc1[:], AF.Silu))
                qV.wait(qS, qS.n)
                qV.wait(qD, l1)
                qV.wait(qD, l3_)
                dst = ch(Bp, c)[:, tau * 512:(tau + 1) * 512]
                vop(lambda v, dst=dst: v.tensor_tensor(dst, sc1[:], sc2, ALU.mult))
                vop(lambda v, dst=dst: v.tensor_tensor(dst, dst, sc0[:], ALU.add))
                vop(lambda v, dst=dst: v.tensor_tensor(sc0[:], dst, dst, ALU.mult))
                qT.wait(qV, qV.n)
                mm(p[5][0:1, 0:512], ONESCOL, dst, c == 0, c == NCH - 1)
                mm(p[6][0:1, 0:512], ONESCOL, sc0[:], c == 0, c == NCH - 1)
                qG.wait(qV, qV.n)
                ld(x1_s[c * 128:(c + 1) * 128, tau * 512:(tau + 1) * 512], dst, q=qG)
                qD.wait(qT, qT.n)
                qD.wait(qS, qS.n)
            qV.wait(qT, qT.n)
            vop(lambda v, tau=tau: v.tensor_copy(rowp[0:1, tau * 512:(tau + 1) * 512], p[5][0:1, 0:512]))
            vop(lambda v, tau=tau: v.tensor_copy(rowp[0:1, 2304 + tau * 512:2304 + (tau + 1) * 512], p[6][0:1, 0:512]))
            qT.wait(qV, qV.n)
        SU, SQ = rowp[0:1, 0:T], rowp[0:1, 2304:2304 + T]
        vop(lambda v: v.tensor_scalar(SU, SU, 1.0 / D, None, ALU.mult, ALU.bypass))
        vop(lambda v: v.tensor_scalar(SQ, SQ, 1.0 / D, None, ALU.mult, ALU.bypass))
        vop(lambda v: v.tensor_tensor(rowp[0:1, 4352:4352 + T], SU, SU, ALU.mult))
        vop(lambda v: v.tensor_tensor(SQ, SQ, rowp[0:1, 4352:4352 + T], ALU.subtract))
        qS.wait(qV, qV.n)
        sop(lambda s: s.activation(SQ, SQ, AF.Sqrt, bias=EPS5[0:1, :], scale=1.0))
        qV.wait(qS, qS.n)
        vop(lambda v: v.reciprocal(rowp[0:1, 4352:4352 + T], SQ))
        ISTD1 = rowp[0:1, 4352:4352 + T]
        qT.wait(qV, qV.n)
        for c in range(NCH):
            for tau in range(NTAU):
                mm(p[5][:, 0:512], ONESROW, SU[0:1, tau * 512:(tau + 1) * 512], True, True)
                m7 = mm(p[6][:, 0:512], ONESROW, ISTD1[0:1, tau * 512:(tau + 1) * 512], True, True)
                qV.wait(qT, m7)
                dst = ch(A, c)[:, tau * 512:(tau + 1) * 512]
                src = ch(Bp, c)[:, tau * 512:(tau + 1) * 512]
                vop(lambda v, dst=dst, src=src: v.tensor_tensor(dst, src, p[5][:, 0:512], ALU.subtract))
                vop(lambda v, dst=dst: v.tensor_tensor(dst, dst, p[6][:, 0:512], ALU.mult))
                qT.wait(qV, qV.n)
            qG.wait(qV, qV.n)
            ld(xn1_s[c * 128:(c + 1) * 128, :], ch(A, c), q=qG)

        barrier()
        n_wb = ld(C[:, 0:128].rearrange("p (k e) -> p k e", e=16),
                  Wb_i.rearrange("(k p) e -> p k e", p=128))
        for c in range(NCH):
            ld(rowp[0:16, 4352 + c * 128:4352 + (c + 1) * 128], hbq[c], q=qD)
        HB = lambda c: rowp[0:16, 4352 + c * 128:4352 + (c + 1) * 128]
        qT.wait(qD)
        BETA = rowp[0:16, 0:T]
        for tau in range(NTAU):
            for k in range(NCH):
                mm(p[7][0:16, 0:512], C[:, k * 16:(k + 1) * 16],
                   ch(A, k)[:, tau * 512:(tau + 1) * 512], k == 0, k == NCH - 1)
            qS.wait(qT, qT.n)
            sop(lambda s, tau=tau: s.activation(BETA[:, tau * 512:(tau + 1) * 512],
                                                p[7][0:16, 0:512], AF.Sigmoid,
                                                bias=colb[0:16, 40:41], scale=1.0))
            qT.wait(qS, qS.n)
        qT.wait(qS, qS.n)
        NRM = rowp[0:16, 2304:2304 + T]

        def l2_apply(out_scr, fold_beta):
            sqb = (sc0[:], sc1[:], sc2, C[:, 7680:8192])
            sqtok = [0, 0, 0, 0]
            mmtok = [0, 0, 0, 0]
            for tau in range(NTAU):
                for c in range(NCH):
                    src = ch(Bp, c)[:, tau * 512:(tau + 1) * 512]
                    bi = c % 4
                    if mmtok[bi] > 0:
                        qV.wait(qT, mmtok[bi])
                    sqtok[bi] = vop(lambda v, src=src, buf=sqb[bi]: v.tensor_tensor(
                        buf, src, src, ALU.mult))
                    qT.wait(qV, sqtok[bi])
                    mmtok[bi] = mm(p[7][0:16, 0:512], SEL(c), sqb[bi], c == 0, c == NCH - 1)
                qV.wait(qT, qT.n)
                vop(lambda v, tau=tau: v.tensor_copy(NRM[:, tau * 512:(tau + 1) * 512],
                                                     p[7][0:16, 0:512]))
            qS.wait(qV, qV.n)
            sop(lambda s: s.activation(NRM, NRM, AF.Sqrt))
            qV.wait(qS, qS.n)
            vop(lambda v: v.tensor_scalar(NRM, NRM, 1e-12, None, ALU.max, ALU.bypass))
            vop(lambda v: v.reciprocal(NRM, NRM))
            if fold_beta:
                vop(lambda v: v.tensor_tensor(NRM, NRM, BETA, ALU.mult))
            qT.wait(qV, qV.n)
            aptok = [0, 0]
            for c in range(NCH):
                for tau in range(NTAU):
                    pb = p[7] if (c * NTAU + tau) % 2 == 0 else p[3]
                    bi = (c * NTAU + tau) % 2
                    if aptok[bi] > 0:
                        qT.wait(qV, aptok[bi])
                    m8 = mm(pb[:, 0:512], HB(c), NRM[:, tau * 512:(tau + 1) * 512], True, True)
                    qV.wait(qT, m8)
                    dst = ch(Bp, c)[:, tau * 512:(tau + 1) * 512]
                    aptok[bi] = vop(lambda v, dst=dst, pb=pb: v.tensor_tensor(
                        dst, dst, pb[:, 0:512], ALU.mult))
                qG.wait(qV, qV.n)
                ld(out_scr[c * 128:(c + 1) * 128, :], ch(Bp, c), q=qG)

        proj_stream(Wq_i, A)
        l2_apply(qn_s, False)
        proj_stream(Wk_i, A)
        l2_apply(kb_s, True)
        proj_stream(Wv_i, A)
        qD.wait(qG)
        for c in range(NCH):
            for g4 in range(4):
                for jj in range(4):
                    i = g4 * 4 + jj
                    tr(p[7][:, jj * 128:(jj + 1) * 128], ch(Bp, c)[:, i * 128:(i + 1) * 128])
                qV.wait(qT, qT.n)
                qV.wait(qG)
                vop(lambda v: v.tensor_copy(sc0[:], p[7][:, 0:512]))
                qG.wait(qV, qV.n)
                ld(vT_s[g4 * 512:(g4 + 1) * 512, c * 128:(c + 1) * 128].rearrange(
                    "(i p) d -> p i d", p=128),
                   sc0[:].rearrange("p (i d) -> p i d", d=128), q=qG)
                qT.wait(qV, qV.n)

        barrier()
        QHp = A[:, 0:2048]
        KHp = A[:, 2048:4096]
        VTa = A[:, 4096:5120]
        VTb = A[:, 5120:6144]
        AOSp = A[:, 6144:8192]
        SC3 = C[:, 7680:8192]
        qD.wait(qG)
        qD.wait(qT, qT.n)
        qG.wait(qT, qT.n)
        qG.wait(qV, qV.n)
        ntl = [ld(TRIL(j), tril4[j], q=qG) for j in range(4)]
        qV.wait(qG, ntl[-1])
        for hp in range(H // 2):
            hA, hB = 2 * hp, 2 * hp + 1
            qV.wait(qG)
            qD.wait(qV, qV.n)
            ld(QHp[0:64, :], qn_s[hA * 64:(hA + 1) * 64, :])
            ld(QHp[64:128, :], qn_s[hB * 64:(hB + 1) * 64, :])
            ld(KHp[0:64, :], kb_s[hA * 64:(hA + 1) * 64, :])
            ld(KHp[64:128, :], kb_s[hB * 64:(hB + 1) * 64, :])
            ld(VTa.rearrange("p (i d) -> p i d", d=64),
               vT_s[:, hA * 64:(hA + 1) * 64].rearrange("(i p) d -> p i d", p=128))
            ld(VTb.rearrange("p (i d) -> p i d", d=64),
               vT_s[:, hB * 64:(hB + 1) * 64].rearrange("(i p) d -> p i d", p=128))
            qT.wait(qD)
            etok = [0, 0]
            for qtau in range(NTAU):
                jmax = qtau * 4 + 4

                def mm1p(j, qtau=qtau):
                    psA = p[0] if j % 2 == 0 else p[4]
                    psB = p[1] if j % 2 == 0 else p[5]
                    if etok[j % 2] > 0:
                        qT.wait(qV, etok[j % 2])
                    mm(psA[:, 0:512], KHp[0:64, j * 128:(j + 1) * 128],
                       QHp[0:64, qtau * 512:(qtau + 1) * 512], True, True, tp=(0, 0))
                    return mm(psB[:, 0:512], KHp[64:128, j * 128:(j + 1) * 128],
                              QHp[64:128, qtau * 512:(qtau + 1) * 512], True, True, tp=(64, 0))

                t1 = mm1p(0)
                for j in range(jmax):
                    psA = p[0] if j % 2 == 0 else p[4]
                    psB = p[1] if j % 2 == 0 else p[5]
                    scbA = sc0[:] if j % 2 == 0 else sc1[:]
                    scbB = sc2 if j % 2 == 0 else SC3
                    qV.wait(qT, t1)
                    o = j - qtau * 4
                    if o >= 0:
                        vop(lambda v, scbA=scbA, psA=psA, o=o: v.tensor_tensor(
                            scbA, psA[:, 0:512], TRIL(o), ALU.mult))
                        etok[j % 2] = vop(lambda v, scbB=scbB, psB=psB, o=o: v.tensor_tensor(
                            scbB, psB[:, 0:512], TRIL(o), ALU.mult))
                    else:
                        vop(lambda v, scbA=scbA, psA=psA: v.tensor_copy(
                            scbA, psA[:, 0:512]))
                        etok[j % 2] = vop(lambda v, scbB=scbB, psB=psB: v.tensor_copy(
                            scbB, psB[:, 0:512]))
                    if j + 1 < jmax:
                        t1 = mm1p(j + 1)
                    qT.wait(qV, etok[j % 2])
                    mm(p[2][0:64, 0:512], VTa[:, j * 64:(j + 1) * 64],
                       scbA,
                       j == 0, j == jmax - 1, tp=(0, 0))
                    mm(p[3][64:128, 0:512], VTb[:, j * 64:(j + 1) * 64],
                       scbB,
                       j == 0, j == jmax - 1, tp=(0, 64))
                qV.wait(qT, qT.n)
                vop(lambda v, qtau=qtau: v.tensor_copy(
                    AOSp[0:64, qtau * 512:(qtau + 1) * 512], p[2][0:64, 0:512]))
                vop(lambda v, qtau=qtau: v.tensor_copy(
                    AOSp[64:128, qtau * 512:(qtau + 1) * 512], p[3][64:128, 0:512]))
                qT.wait(qV, qV.n)
            qG.wait(qV, qV.n)
            ld(ao_s[hA * 64:(hA + 1) * 64, :], AOSp[0:64, :], q=qG)
            ld(ao_s[hB * 64:(hB + 1) * 64, :], AOSp[64:128, :], q=qG)
            qD.wait(qG)
        for k in range(NCH):
            ld(ch(A, k), ao_s[k * 128:(k + 1) * 128, :])
        qT.wait(qD)
        proj_stream(Wo_i, A)
        for f in range(NCH):
            qG.wait(qV, qV.n)
            ld(s1_s[f * 128:(f + 1) * 128, :], ch(Bp, f), q=qG)

        barrier()
        kst0 = C16[:, 0:2048]
        kst1 = C16[:, 2048:4096]
        keff = C16[:, 4096:6144]
        XNt = C[:, 3072:7168]
        OUTS = C[:, 3072:7168]
        GT = sc2
        XHt = A16[:, 0:4096]
        XLt = A16[:, 4096:8192]
        D1h = B16[:, 0:16384]
        D1l = B16[:, 16384:32768]
        D2h = A16[:, 0:16384]
        D2l = A16[:, 16384:32768]
        STAGE = A[:, 2048:3072]
        qD.wait(qG)
        qG.wait(qD)
        for tau in range(NTAU):
            qD.wait(qT, qT.n)
            qD.wait(qG)
            nl = []
            for c in range(NCH):
                nl.append(ld(XNt[:, c * 512:(c + 1) * 512],
                             xn1_s[c * 128:(c + 1) * 128, tau * 512:(tau + 1) * 512]))
            qV.wait(qD)
            for c in range(NCH):
                xs = slice(c * 512, (c + 1) * 512)
                vop(lambda v, xs=xs: v.tensor_copy(XHt[:, xs], XNt[:, xs]))
                vop(lambda v, xs=xs: v.tensor_tensor(sc0[:], XNt[:, xs], XHt[:, xs], ALU.subtract))
                vop(lambda v, xs=xs: v.tensor_copy(XLt[:, xs], sc0[:]))
            ft0 = [0, 0]
            splitdone0 = [0, 0]
            subtok = [0, 0]
            mmtok0 = [0, 0]
            ck0 = {}
            ck0[0] = ld(kst0[:, 0:2048], k0r[0].rearrange("k b f -> k (b f)"), q=qG)
            for fb in range(32):
                par = fb % 2
                kq = kst0 if par == 0 else kst1
                pp = p[3] if par == 0 else p[7]
                sa = sc0 if par == 0 else sc1
                sb = sc1 if par == 0 else sc0
                qV.wait(qG, ck0[fb])
                if mmtok0[par] > 0:
                    qV.wait(qT, mmtok0[par])
                ke = keff[:, 0:1024] if par == 0 else keff[:, 1024:2048]
                subtok[par] = vop(lambda v, kq=kq, ke=ke: v.tensor_tensor(
                    ke, kq[:, 0:1024], kq[:, 1024:2048], ALU.subtract))
                if fb + 1 < 32:
                    npar = (fb + 1) % 2
                    nkq = kst0 if npar == 0 else kst1
                    if subtok[npar] > 0:
                        qG.wait(qV, subtok[npar])
                    ck0[fb + 1] = ld(nkq[:, 0:2048],
                                     k0r[fb + 1].rearrange("k b f -> k (b f)"), q=qG)
                qT.wait(qV, subtok[par])
                if ft0[par] > 0:
                    qT.wait(qS, ft0[par])
                for kb in range(8):
                    stat = ke[:, kb * 128:(kb + 1) * 128]
                    mm(pp[:, 0:512], stat, XHt[:, kb * 512:(kb + 1) * 512], kb == 0, False)
                    mm(pp[:, 0:512], stat, XLt[:, kb * 512:(kb + 1) * 512], False, kb == 7)
                mmtok0[par] = qT.n
                qS.wait(qT, qT.n)
                if splitdone0[par] > 0:
                    qS.wait(qV, splitdone0[par])
                ft0[par] = sop(lambda s, pp=pp, sa=sa: s.activation(
                    sa[:], pp[:, 0:512], AF.Relu, bias=0.0, scale=S0C8))
                qV.wait(qS, ft0[par])
                fs = slice(fb * 512, (fb + 1) * 512)
                vop(lambda v, fs=fs, sa=sa: v.tensor_copy(D1h[:, fs], sa[:]))
                vop(lambda v, fs=fs, sa=sa, sb=sb: v.tensor_tensor(sb[:], sa[:], D1h[:, fs], ALU.subtract))
                splitdone0[par] = vop(lambda v, fs=fs, sb=sb: v.tensor_copy(D1l[:, fs], sb[:]))

            def rowsum_neg_half(Dh, Dl, nkb):
                qT.wait(qV, qV.n)
                qT.wait(qS, qS.n)
                for kb in range(nkb):
                    mm(p[4][0:1, 0:512], NEGH16, Dh[:, kb * 512:(kb + 1) * 512], kb == 0, False)
                    mm(p[4][0:1, 0:512], NEGH16, Dl[:, kb * 512:(kb + 1) * 512], False, kb == nkb - 1)
                qV.wait(qT, qT.n)
                vop(lambda v: v.tensor_copy(rowp[0:1, 0:512], p[4][0:1, 0:512]))
                qT.wait(qV, qV.n)
                m9 = mm(p[5][:, 0:512], ONESROW, rowp[0:1, 0:512], True, True)
                qV.wait(qT, m9)
                vop(lambda v: v.tensor_copy(rsb[:], p[5][:, 0:512]))
                qT.wait(qV, qV.n)

            rowsum_neg_half(D1h, D1l, 32)
            ft = [0, 0]
            kbufs = (kst0, kst1, keff)
            mm_done = [0, 0, 0]
            ck_tok = {}
            nhalf = 64
            qG.wait(qT, qT.n)
            qG.wait(qV, qV.n)
            for hidx in range(2):
                bslot = hidx % 3
                ck_tok[hidx] = ld(kbufs[bslot][:, 0:2048],
                                  k1r[0, :, hidx * 16:(hidx + 1) * 16, :].rearrange("k b f -> k (b f)"), q=qG)
            for fb in range(32):
                pp = p[3] if fb % 2 == 0 else p[7]
                sa = sc0 if fb % 2 == 0 else sc1
                sb = sc1 if fb % 2 == 0 else sc0
                if ft[fb % 2] > 0:
                    qT.wait(qV, ft[fb % 2])
                for half2 in range(2):
                    hidx = fb * 2 + half2
                    bslot = hidx % 3
                    kq = kbufs[bslot]
                    qT.wait(qG, ck_tok[hidx])
                    for kb in range(16):
                        gkb = half2 * 16 + kb
                        stat = kq[:, kb * 128:(kb + 1) * 128]
                        mm(pp[:, 0:512], stat, D1h[:, gkb * 512:(gkb + 1) * 512],
                           gkb == 0, False)
                        mm(pp[:, 0:512], stat, D1l[:, gkb * 512:(gkb + 1) * 512],
                           False, gkb == 31)
                    mm_done[bslot] = qT.n
                    nxt = hidx + 2
                    if nxt < nhalf:
                        nslot = nxt % 3
                        qG.wait(qT, mm_done[nslot])
                        ck_tok[nxt] = ld(kbufs[nslot][:, 0:2048],
                                         k1r[nxt // 2, :, (nxt % 2) * 16:(nxt % 2 + 1) * 16, :]
                                         .rearrange("k b f -> k (b f)"), q=qG)
                qV.wait(qT, qT.n)
                ft[fb % 2] = vop(lambda v, pp=pp, sa=sa: v.tensor_tensor(sa[:], pp[:, 0:512], rsb[:], ALU.add))
                qS.wait(qV, ft[fb % 2])
                stok = sop(lambda s, sa=sa: s.activation(sa[:], sa[:], AF.Relu, bias=0.0, scale=S1C4))
                qV.wait(qS, stok)
                fs = slice(fb * 512, (fb + 1) * 512)
                vop(lambda v, fs=fs, sa=sa: v.tensor_copy(D2h[:, fs], sa[:]))
                vop(lambda v, fs=fs, sa=sa, sb=sb: v.tensor_tensor(sb[:], sa[:], D2h[:, fs], ALU.subtract))
                vop(lambda v, fs=fs, sb=sb: v.tensor_copy(D2l[:, fs], sb[:]))
            rowsum_neg_half(D2h, D2l, 32)
            mm_done = [0, 0, 0]
            ck_tok = {}
            nhalf2 = 16
            qG.wait(qT, qT.n)
            for hidx in range(2):
                ck_tok[hidx] = ld(kbufs[hidx % 3][:, 0:2048],
                                  k2r[0, :, hidx * 16:(hidx + 1) * 16, :].rearrange("k b f -> k (b f)"), q=qG)
            for fb in range(8):
                pp = p[3] if fb % 2 == 0 else p[7]
                for half2 in range(2):
                    hidx = fb * 2 + half2
                    bslot = hidx % 3
                    kq = kbufs[bslot]
                    qT.wait(qG, ck_tok[hidx])
                    for kb in range(16):
                        gkb = half2 * 16 + kb
                        stat = kq[:, kb * 128:(kb + 1) * 128]
                        mm(pp[:, 0:512], stat, D2h[:, gkb * 512:(gkb + 1) * 512],
                           gkb == 0, False)
                        mm(pp[:, 0:512], stat, D2l[:, gkb * 512:(gkb + 1) * 512],
                           False, gkb == 31)
                    mm_done[bslot] = qT.n
                    nxt = hidx + 2
                    if nxt < nhalf2:
                        nslot = nxt % 3
                        qG.wait(qT, mm_done[nslot])
                        ck_tok[nxt] = ld(kbufs[nslot][:, 0:2048],
                                         k2r[nxt // 2, :, (nxt % 2) * 16:(nxt % 2 + 1) * 16, :]
                                         .rearrange("k b f -> k (b f)"), q=qG)
                qV.wait(qT, qT.n)
                vop(lambda v, pp=pp: v.tensor_tensor(sc0[:], pp[:, 0:512], rsb[:], ALU.add))
                qS.wait(qV, qV.n)
                sop(lambda s: s.activation(GT, sc0[:], AF.Silu, bias=0.0, scale=S2C))
                qD.wait(qV, qV.n)
                qD.wait(qS, qS.n)
                lx = ld(sc0, x1_s[fb * 128:(fb + 1) * 128, tau * 512:(tau + 1) * 512])
                ls = ld(sc1, s1_s[fb * 128:(fb + 1) * 128, tau * 512:(tau + 1) * 512])
                qV.wait(qS, qS.n)
                qV.wait(qD, lx)
                qV.wait(qD, ls)
                vop(lambda v: v.tensor_tensor(GT, GT, sc1[:], ALU.mult))
                vop(lambda v: v.tensor_tensor(GT, GT, sc0[:], ALU.add))
                vop(lambda v, fb=fb: v.tensor_copy(OUTS[:, fb * 512:(fb + 1) * 512], GT))
                qD.wait(qV, qV.n)
                qT.wait(qV, qV.n)
            for tt_i in range(4):
                qV.wait(qG)
                for c in range(NCH):
                    tr(p[6][:, (c % 4) * 128:((c % 4) + 1) * 128],
                       OUTS[:, c * 512 + tt_i * 128:c * 512 + (tt_i + 1) * 128])
                    if c % 4 == 3:
                        qV.wait(qT, qT.n)
                        vop(lambda v, c=c, tt_i=tt_i: v.tensor_copy(
                            STAGE[:, (c - 3) * 128:(c + 1) * 128], p[6][:, 0:512]))
                        qT.wait(qV, qV.n)
                qG.wait(qV, qV.n)
                ld(out[tau * 512 + tt_i * 128:tau * 512 + (tt_i + 1) * 128, :],
                   STAGE, q=qG)

        final_g = qG.n
        qD.wait(qG)
        qV.wait(qG)

        with nc.Block() as block:
            @block.tensor
            def _(t):
                qT.emit(t)

            @block.vector
            def _(v):
                qV.emit(v)

            @block.scalar
            def _(s):
                qS.emit(s)

            @block.sync
            def _(d):
                qD.emit(d)

            @block.gpsimd
            def _(g):
                qG.emit(g)

    return nc, dram


_NC_CACHE = {}


def _consts():
    ident = np.eye(128, dtype=np.float32)
    iota = np.broadcast_to(np.arange(8, dtype=np.float32), (128, 16, 8)).reshape(128, 128).copy()
    iota8 = np.arange(8, dtype=np.float32).reshape(8, 1)
    tril = np.zeros((4, 128, 512), np.float32)
    for o in range(4):
        kk = o * 128 + np.arange(128)[:, None]
        qq = np.arange(512)[None, :]
        tril[o] = (kk <= qq).astype(np.float32)
    sel = np.zeros((8, 128, 16), np.float32)
    for c in range(8):
        sel[c, 0:64, 2 * c] = 1.0
        sel[c, 64:128, 2 * c + 1] = 1.0
    hb = np.zeros((8, 16, 128), np.float32)
    for c in range(8):
        hb[c, 2 * c, 0:64] = 1.0
        hb[c, 2 * c + 1, 64:128] = 1.0
    ones = np.ones((1, 128), np.float32)
    return ident, iota, iota8, tril, sel, hb, ones


def _shared_inputs(inp):
    x = np.ascontiguousarray(inp["L0_attn_x"].astype(np.float32, copy=False))
    ident, iota, iota8, tril, sel, hb, ones = _consts()
    return x, {
        "x_tm": x,
        "x_fm": np.ascontiguousarray(x.transpose(0, 2, 1)),
        "WinT": np.ascontiguousarray(inp["L0_attn_W_in"].T),
        "WigT": np.ascontiguousarray(inp["L0_attn_W_ig"].T),
        "WrgT": np.ascontiguousarray(inp["L0_attn_W_rg"].T),
        "WoutT": np.ascontiguousarray(inp["L0_attn_W_out"].T),
        "Wr_fm": np.ascontiguousarray(
            inp["L0_ffn_Wr"].reshape(8, 128, 8).transpose(1, 0, 2).reshape(128, 64)),
        "br_row": np.ascontiguousarray(inp["L0_ffn_br"].reshape(1, 8)),
        "big_col": np.ascontiguousarray(inp["L0_attn_b_ig"].reshape(8, 128).T),
        "brg_col": np.ascontiguousarray(inp["L0_attn_b_rg"].reshape(8, 128).T),
        "lam_col": np.ascontiguousarray(inp["L0_attn_lam"].reshape(8, 128).T),
        "h0_col": np.ascontiguousarray(inp["L0_attn_h0"].reshape(8, 128).T),
        "nw_col": np.ascontiguousarray(inp["L0_attn_norm_w"].reshape(8, 128).T),
        "W1moe": np.ascontiguousarray(inp["L0_ffn_W1"].astype(np.float32, copy=False)),
        "W2moe": np.ascontiguousarray(inp["L0_ffn_W2"].astype(np.float32, copy=False)),
        "b1_col": np.ascontiguousarray(inp["L0_ffn_b1"].reshape(8, 8, 128).transpose(0, 2, 1)),
        "b2stat": np.ascontiguousarray(inp["L0_ffn_b2"].reshape(8, 8, 128)),
        "Wq": np.ascontiguousarray(inp["L1_attn_Wq"]),
        "Wk": np.ascontiguousarray(inp["L1_attn_Wk"]),
        "Wv": np.ascontiguousarray(inp["L1_attn_Wv"]),
        "Wo": np.ascontiguousarray(inp["L1_attn_Wo"]),
        "Wb": np.ascontiguousarray(inp["L1_attn_Wb"]),
        "bb_col": np.ascontiguousarray(inp["L1_attn_bb"].reshape(16, 1)),
        "k0r": np.ascontiguousarray(inp["L1_ffn_k0"].astype(np.uint8)
                                    .reshape(16, 128, 32, 128).transpose(2, 1, 0, 3)),
        "k1r": np.ascontiguousarray(inp["L1_ffn_k1"].astype(np.uint8)
                                    .reshape(32, 128, 32, 128).transpose(2, 1, 0, 3)),
        "k2r": np.ascontiguousarray(inp["L1_ffn_k2"].astype(np.uint8)
                                    .reshape(32, 128, 8, 128).transpose(2, 1, 0, 3)),
        "s_sc": np.array([[float(inp["L1_ffn_s0"]), float(inp["L1_ffn_s1"]),
                           float(inp["L1_ffn_s2"])]], np.float32),
        "ident_c": ident, "iota_e": iota, "iota8c": iota8, "tril4": tril,
        "selq": sel, "hbq": hb, "onesrow": ones,
        "e8sel": np.ascontiguousarray(
            np.stack([np.where(np.arange(8)[:, None] == e, 1.0, 0.0)
                      .repeat(128, axis=1)[:, :128] for e in range(8)], axis=1)
            .reshape(8, 1024).astype(np.float32)),
    }


def kernel(**inputs):
    inp = {k: np.asarray(v) for k, v in inputs.items()}
    x, shared = _shared_inputs(inp)

    if "nc" not in _NC_CACHE:
        _NC_CACHE["nc"] = build_nc(debug=False)
    nc, dram = _NC_CACHE["nc"]

    in_maps = []
    for b in range(B):
        m = dict(shared)
        m["xo_tm"] = np.ascontiguousarray(x[b])
        m["xo_fm"] = np.ascontiguousarray(x[b].T)
        oh = np.zeros((1, 8), np.float32)
        oh[0, b] = 1.0
        m["coreoh"] = oh
        in_maps.append(m)

    res = run_bass_kernel_spmd(nc, in_maps, core_ids=list(range(8)))
    outs = np.stack([res.results[b]["out"] for b in range(B)], axis=0)
    return outs


# version 10
# speedup vs baseline: 1.0122x; 1.0122x over previous
import numpy as np
from contextlib import ExitStack

import concourse.bass as bass
import concourse.mybir as mybir
from concourse.bass_utils import run_bass_kernel_spmd

F32 = mybir.dt.float32
F16 = mybir.dt.float16
U8 = mybir.dt.uint8
AF = mybir.ActivationFunctionType
ALU = mybir.AluOpType
AX = mybir.AxisListType

B, T, D = 8, 2048, 1024
NCH, NTI, NTAU = 8, 16, 4
E, H, DH = 8, 16, 64
N = B * T


NSEM = 8


class Q:

    def __init__(self, name, dma=False):
        self.name, self.dma = name, dma
        self.ops, self.n, self.sem = [], 0, None
        if dma:
            self.sems = [None] * NSEM
            self.counts = [0] * NSEM
            self.i = 0

    def wait(self, other, val=None):
        if other.dma:
            if isinstance(val, tuple):
                s, v = val
                self.ops.append(lambda e, o=other, s=s, v=v: e.wait_ge(o.sems[s], v))
            else:
                for s in range(NSEM):
                    v = other.counts[s]
                    if v > 0:
                        self.ops.append(lambda e, o=other, s=s, v=v: e.wait_ge(o.sems[s], v))
        else:
            v = other.n if val is None else val
            if v > 0:
                self.ops.append(lambda e, o=other, vv=v: e.wait_ge(o.sem, vv))

    def op(self, fn, serialize=None):
        if self.dma:
            s = self.i % NSEM
            self.i += 1
            self.counts[s] += 16
            tgt = self.counts[s]

            def run(e, fn=fn, s=s, tgt=tgt, sh=self):
                if tgt > 16:
                    e.wait_ge(sh.sems[s], tgt - 16)
                instr = fn(e)
                instr.then_inc(sh.sems[s], 16)
            self.ops.append(run)
            self.n += 16
            return (s, tgt)
        if serialize is None:
            serialize = self.name in ("vector", "scalar")
        self.n += 1
        tgt = self.n

        def run(e, fn=fn, tgt=tgt, serialize=serialize, sh=self):
            instr = fn(e)
            instr.then_inc(sh.sem, 1)
            if serialize:
                e.wait_ge(sh.sem, tgt)
        self.ops.append(run)
        return tgt

    def emit(self, engine):
        for f in self.ops:
            f(engine)


def build_nc(debug=False):
    nc = bass.Bass(trn_type="TRN2", target_bir_lowering=False)
    dram = {}

    def din(name, shape, dt=F32):
        dram[name] = nc.dram_tensor(name, list(shape), dt, kind="ExternalInput").ap()
        return dram[name]

    def dscr(name, shape, dt=F32):
        kind = "ExternalOutput" if debug else "Internal"
        dram[name] = nc.dram_tensor(name, list(shape), dt, kind=kind).ap()
        return dram[name]

    x_tm = din("x_tm", (B, T, D))
    x_fm = din("x_fm", (B, D, T))
    xo_tm = din("xo_tm", (T, D))
    xo_fm = din("xo_fm", (D, T))
    WinT = din("WinT", (D, D)); WigT = din("WigT", (D, D))
    WrgT = din("WrgT", (D, D)); WoutT = din("WoutT", (D, D))
    Wr_fm = din("Wr_fm", (128, 64))
    br_row = din("br_row", (1, E))
    big_col = din("big_col", (128, 8)); brg_col = din("brg_col", (128, 8))
    lam_col = din("lam_col", (128, 8)); h0_col = din("h0_col", (128, 8))
    nw_col = din("nw_col", (128, 8))
    W1moe = din("W1moe", (E, D, D)); W2moe = din("W2moe", (E, D, D))
    b1_col = din("b1_col", (E, 128, 8))
    b2stat = din("b2stat", (E, 8, 128))
    Wq_i = din("Wq", (D, D)); Wk_i = din("Wk", (D, D)); Wv_i = din("Wv", (D, D))
    Wo_i = din("Wo", (D, D)); Wb_i = din("Wb", (D, H))
    bb_col = din("bb_col", (16, 1))
    k0r = din("k0r", (32, 128, 16, 128), U8)
    k1r = din("k1r", (32, 128, 32, 128), U8)
    k2r = din("k2r", (8, 128, 32, 128), U8)
    s_sc = din("s_sc", (1, 3))
    ident_c = din("ident_c", (128, 128))
    iota_e = din("iota_e", (128, 128))
    iota8c = din("iota8c", (8, 1))
    tril4 = din("tril4", (4, 128, 512))
    selq = din("selq", (NCH, 128, 16))
    hbq = din("hbq", (NCH, 16, 128))
    onesrow = din("onesrow", (1, 128))
    e8sel = din("e8sel", (8, 1024))
    coreoh = din("coreoh", (1, 8))

    out = nc.dram_tensor("out", [T, D], F32, kind="ExternalOutput").ap()
    dram["out"] = out

    xnT_s = dscr("xnT_s", (D, T))
    g0_s = dscr("g0_s", (D, T))
    s0_s = dscr("s0_s", (D, T))
    x1_s = dscr("x1_s", (D, T))
    xn1_s = dscr("xn1_s", (D, T))
    qn_s = dscr("qn_s", (D, T))
    kb_s = dscr("kb_s", (D, T))
    vT_s = dscr("vT_s", (T, D))
    ao_s = dscr("ao_s", (D, T))
    s1_s = dscr("s1_s", (D, T))
    wem_s = dscr("wem_s", (8, T))
    kept_s = dscr("kept_s", (8, N))

    qT, qV, qS = Q("tensor"), Q("vector"), Q("scalar")
    qD, qG = Q("sync", dma=True), Q("gpsimd", dma=True)
    ALLQ = (qT, qV, qS, qD, qG)

    es = ExitStack()
    with es:
        A = es.enter_context(nc.sbuf_tensor("A", [128, 16384], F32))
        Bp = es.enter_context(nc.sbuf_tensor("Bp", [128, 16384], F32))
        C = es.enter_context(nc.sbuf_tensor("C", [128, 8192], F32))
        cst = es.enter_context(nc.sbuf_tensor("cst", [128, 1024], F32))
        rowp = es.enter_context(nc.sbuf_tensor("rowp", [64, 6400], F32))
        wem = es.enter_context(nc.sbuf_tensor("wem", [8, T], F32))
        rsb = es.enter_context(nc.sbuf_tensor("rsb", [128, 512], F32))
        sc0 = es.enter_context(nc.sbuf_tensor("sc0", [128, 512], F32))
        sc1 = es.enter_context(nc.sbuf_tensor("sc1", [128, 512], F32))
        stats = es.enter_context(nc.sbuf_tensor("stats", [128, 256], F32))
        gates = es.enter_context(nc.sbuf_tensor("gates", [128, 152], F32))
        colb = es.enter_context(nc.sbuf_tensor("colb", [128, 64], F32))
        ltm = es.enter_context(nc.sbuf_tensor("ltm", [128, 352], F32))
        f16c = es.enter_context(nc.sbuf_tensor("f16c", [128, 256], F16))

        p = [es.enter_context(nc.psum_tensor("p%d" % i, [128, 512], F32))
             for i in range(8)]

        for q in ALLQ:
            if q.dma:
                q.sems = [es.enter_context(nc.semaphore("sem_%s_%d" % (q.name, i)))
                          for i in range(NSEM)]
            else:
                q.sem = es.enter_context(nc.semaphore("sem_" + q.name))

        sc2 = C[:, 7168:7680]
        A16 = A[:].bitcast(F16)
        B16 = Bp[:].bitcast(F16)
        C16 = C[:].bitcast(F16)

        emp = A[0:8, 0:16384]
        emp2 = Bp[0:8, 0:16384]

        def mm(ps, stat, mov, start, stop, tp=None):
            return qT.op(lambda t, a=ps, b=stat, c=mov, s1=start, s2=stop, tp=tp:
                         t.matmul(a, b, c, start=s1, stop=s2, tile_position=tp),
                         serialize=False)

        def tr(ps, src, idn=None):
            return qT.op(lambda t, a=ps, b=src, i=idn: t.transpose(
                a, b, i if i is not None else IDENT), serialize=False)

        def ld(dst, src, q=None):
            q = q or qD
            if not isinstance(dst, bass.AP):
                dst = dst[:]
            if not isinstance(src, bass.AP):
                src = src[:]
            return q.op(lambda e, d=dst, s=src: e.dma_start(out=d, in_=s),
                        serialize=False)

        def vop(fn):
            return qV.op(fn)

        def sop(fn):
            return qS.op(fn)

        def ch(pool, c, w=T):
            return pool[:, c * w:(c + 1) * w]

        def barrier():
            for q in ALLQ:
                for o in ALLQ:
                    if o is not q:
                        q.wait(o)

        ld(cst[:, 0:128], ident_c)
        ld(cst[:, 128:256], iota_e)
        for c in range(NCH):
            ld(cst[:, 256 + c * 16:256 + (c + 1) * 16], selq[c])
        ld(cst[0:1, 384:512], onesrow)
        ld(cst[:, 512:576], Wr_fm)
        ld(cst[0:8, 648:649], iota8c)
        ld(cst[0:1, 656:664], br_row)
        ld(cst[0:1, 664:672], coreoh)
        ld(cst[0:1, 672:675], s_sc)
        ld(colb[:, 0:8], big_col); ld(colb[:, 8:16], brg_col)
        ld(colb[:, 16:24], lam_col); ld(colb[:, 24:32], h0_col)
        ld(colb[:, 32:40], nw_col); ld(colb[0:16, 40:41], bb_col)
        n_cload, n_gload = qD.n, qG.n

        IDENT = cst[:, 0:128]
        IOTA3 = cst[:, 128:256].rearrange("p (i c) -> p i c", c=8)
        SEL = lambda c: cst[:, 256 + c * 16:256 + (c + 1) * 16]
        ONESROW = cst[0:1, 384:512]
        WRC = lambda k: cst[:, 512 + k * 8:512 + (k + 1) * 8]
        IOTA8C = cst[0:8, 648:649]
        BRROW = cst[0:1, 656:664]
        COREOH = cst[0:1, 664:672]
        SSC = cst[0:1, 672:675]
        TRIL = lambda j: C16[:, j * 512:(j + 1) * 512]

        qV.wait(qD)
        vop(lambda v: v.memset(cst[:, 676:677], 1.0))
        ONESCOL = cst[:, 676:677]
        vop(lambda v: v.memset(cst[:, 704:776], 1.0))
        for k in range(NCH):
            vop(lambda v, k=k: v.tensor_copy(cst[:, 704 + k * 9:704 + k * 9 + 8], WRC(k)))
        WRONES = lambda k: cst[:, 704 + k * 9:704 + k * 9 + 9]
        vop(lambda v: v.memset(f16c[0:1, 0:128], 1.0))
        vop(lambda v: v.memset(f16c[:, 128:129], -0.5))
        ONESROW16 = f16c[0:1, 0:128]
        NEGH16 = f16c[:, 128:129]
        n_vconst = qV.n

        qT.wait(qD)
        qT.wait(qV, n_vconst)
        mm(p[0][0:8, 0:8], ONESROW[0:1, 0:8], COREOH, True, True)
        mm(p[0][:, 8:16], ONESROW, COREOH, True, True)
        mm(p[0][:, 16:19], ONESROW, SSC, True, True)
        qV.wait(qT, qT.n)
        vop(lambda v: v.tensor_copy(gates[0:8, 128:136], p[0][0:8, 0:8]))
        vop(lambda v: v.tensor_copy(gates[:, 136:144], p[0][:, 8:16]))
        vop(lambda v: v.tensor_copy(colb[:, 44:47], p[0][:, 16:19]))
        COREOH8 = gates[0:8, 128:136]
        COREOHB = gates[:, 136:144]
        S0C, S1C, S2C = colb[:, 44:45], colb[:, 45:46], colb[:, 46:47]
        vop(lambda v: v.tensor_scalar(colb[:, 47:48], S0C, 8.0, None, ALU.mult, ALU.bypass))
        vop(lambda v: v.tensor_scalar(colb[:, 48:49], S1C, 4.0, None, ALU.mult, ALU.bypass))
        S0C8, S1C4 = colb[:, 47:48], colb[:, 48:49]
        vop(lambda v: v.memset(colb[:, 49:50], 1e-5))
        vop(lambda v: v.memset(colb[:, 50:51], 1e-6))
        vop(lambda v: v.memset(colb[:, 51:52], 1.0 + 1e-6))
        vop(lambda v: v.memset(colb[:, 52:53], -1.0))
        EPS5, EPS6, GEPS, NEG1 = (colb[:, 49:50], colb[:, 50:51],
                                  colb[:, 51:52], colb[:, 52:53])
        qS.wait(qD)
        sop(lambda s: s.activation(colb[:, 56:64], colb[:, 16:24], AF.Sigmoid))
        qV.wait(qS, qS.n)
        vop(lambda v: v.tensor_scalar(colb[:, 56:64], colb[:, 56:64], 1e-8, None, ALU.add, ALU.bypass))
        qS.wait(qV, qV.n)
        sop(lambda s: s.activation(colb[:, 56:64], colb[:, 56:64], AF.Ln))
        qV.wait(qS, qS.n)
        vop(lambda v: v.tensor_scalar(colb[:, 56:64], colb[:, 56:64], 8.0, None, ALU.mult, ALU.bypass))
        CLOGA = colb[:, 56:64]
        qT.wait(qV, qV.n)

        tm0, tm1 = C[:, 0:1024], C[:, 1024:2048]
        SINK = C[:, 2048:3072]
        TMP3 = C[:, 3072:3200].rearrange("p (i c) -> p i c", c=8)
        DIFF3 = C[:, 3200:3328].rearrange("p (i c) -> p i c", c=8)
        MB3 = C[:, 3328:3456].rearrange("p (i c) -> p i c", c=8)
        IDXROW = rowp[0:1, 2304:2304 + T]
        GATEROW = rowp[0:1, 2304:2304 + T]

        sq_tok = {}
        last_sq = [0, 0]
        for b in range(B):
            for i in range(NTI):
                gi = b * NTI + i
                buf = tm0 if gi % 2 == 0 else tm1
                if last_sq[gi % 2] > 0:
                    qG.wait(qS, last_sq[gi % 2])
                t_ld = ld(buf, x_tm[b, i * 128:(i + 1) * 128, :], q=qG)
                qS.wait(qG, t_ld)
                sq_tok[b] = last_sq[gi % 2] = sop(lambda s, bu=buf, b=b, i=i: s.activation(
                    SINK, bu, AF.Square, accum_out=stats[:, b * 16 + i:b * 16 + i + 1]))
        LRAWp = (rowp[0:9, 0:2048], rowp[32:41, 0:2048])
        LTMp = (ltm[:, 0:176], ltm[:, 176:352])
        P_LOG = (p[0], p[4])
        P_TR = (p[1], p[5])
        P_BC = (p[2], p[6])
        DIFFp = (C[:, 3072:3200], C[:, 3456:3584])
        MBp = (C[:, 3200:3328], C[:, 3584:3712])
        TMPp = (C[:, 3328:3456], C[:, 3712:3840])
        IDXRp = (rowp[0:1, 2304:2304 + T], rowp[0:1, 4352:4352 + T])
        FLATp = (rowp[0:16, 2048:2176], rowp[32:48, 2048:2176])
        SBASE = (128, 192)
        log_evac = [0, 0]
        tr_evac = {}
        ltm_free = [0, 0]
        mm_log = [0, 0]
        memq_tok = [0, 0]
        for b in range(B):
            par = b % 2
            LRAW = LRAWp[par]
            L3 = LTMp[par].rearrange("p (i c) -> p i c", c=11)
            pL, pT_, pBc = P_LOG[par], P_TR[par], P_BC[par]
            S0 = SBASE[par]
            TMP3 = TMPp[par].rearrange("p (i c) -> p i c", c=8)
            DIFF3 = DIFFp[par].rearrange("p (i c) -> p i c", c=8)
            MB3 = MBp[par].rearrange("p (i c) -> p i c", c=8)
            for tau in range(NTAU):
                for k in range(NCH):
                    buf = sc0 if k % 2 == 0 else sc1
                    nmm = qT.n
                    if b + tau + k > 0:
                        qD.wait(qT, nmm - 1 if nmm >= 2 else 0)
                    lk = ld(buf, x_fm[b, k * 128:(k + 1) * 128, tau * 512:(tau + 1) * 512])
                    qT.wait(qD, lk)
                    if k == 0 and log_evac[par] > 0:
                        qT.wait(qV, log_evac[par])
                    mm(pL[0:9, 0:512], WRONES(k), buf[:], k == 0, k == NCH - 1)
                qV.wait(qT, qT.n)
                log_evac[par] = vop(lambda v, tau=tau, LRAW=LRAW, pL=pL: v.tensor_copy(
                    LRAW[:, tau * 512:(tau + 1) * 512], pL[0:9, 0:512]))
            if b == 0:
                qT.wait(qV, qV.n)
                for k in range(NCH):
                    mm(p[3][0:1, 0:8], ONESCOL, WRC(k), k == 0, k == NCH - 1)
                qV.wait(qT, qT.n)
                vop(lambda v: v.tensor_copy(cst[0:1, 680:688], p[3][0:1, 0:8]))
                qT.wait(qV, qV.n)
                mm(p[3][:, 0:8], ONESROW, cst[0:1, 680:688], True, True)
                mm(p[3][:, 8:16], ONESROW, BRROW, True, True)
                qV.wait(qT, qT.n)
                vop(lambda v: v.tensor_copy(cst[:, 688:696], p[3][:, 0:8]))
                vop(lambda v: v.tensor_copy(cst[:, 696:704], p[3][:, 8:16]))
                qT.wait(qV, qV.n)
            COLSB = cst[:, 688:696].unsqueeze(1).broadcast_to((128, 16, 8))
            BRB = cst[:, 696:704].unsqueeze(1).broadcast_to((128, 16, 8))
            for g4 in range(4):
                ptr = pT_ if g4 % 2 == 0 else p[7]
                pid = (par, 0) if g4 % 2 == 0 else ("g7",)
                if tr_evac.get(pid, 0) > 0:
                    qT.wait(qV, tr_evac[pid])
                if g4 == 0:
                    qT.wait(qV, log_evac[par])
                    if ltm_free[par] > 0:
                        qT.wait(qV, ltm_free[par])
                t1 = 0
                for jj in range(4):
                    i = g4 * 4 + jj
                    t1 = tr(ptr[0:128, jj * 9:(jj + 1) * 9], LRAW[:, i * 128:(i + 1) * 128],
                            IDENT[0:9, 0:9] if par == 0 else IDENT[32:41, 32:41])
                qV.wait(qT, t1)
                tr_evac[pid] = vop(lambda v, g4=g4, ptr=ptr, L3=L3: v.tensor_copy(
                    L3[:, g4 * 4:(g4 + 1) * 4, 0:9],
                    ptr[:, 0:36].rearrange("p (a b) -> p a b", b=9)))
            LT = L3[:, :, 0:8]
            SUMR = L3[:, :, 8:9]
            qV.wait(qS, sq_tok[b])
            vop(lambda v, S0=S0, SUMR=SUMR: v.tensor_scalar(
                stats[:, S0:S0 + 16], SUMR.rearrange("p i c -> p (i c)"), 1.0 / D, None, ALU.mult, ALU.bypass))
            vop(lambda v, S0=S0, b=b: v.tensor_scalar(
                stats[:, S0 + 16:S0 + 32], stats[:, b * 16:(b + 1) * 16], 1.0 / D, None, ALU.mult, ALU.bypass))
            vop(lambda v, S0=S0: v.tensor_tensor(
                stats[:, S0 + 32:S0 + 48], stats[:, S0:S0 + 16], stats[:, S0:S0 + 16], ALU.mult))
            vv = vop(lambda v, S0=S0: v.tensor_tensor(
                stats[:, S0 + 16:S0 + 32], stats[:, S0 + 16:S0 + 32], stats[:, S0 + 32:S0 + 48], ALU.subtract))
            qS.wait(qV, vv)
            ss1 = sop(lambda s, S0=S0: s.activation(
                stats[:, S0 + 32:S0 + 48], stats[:, S0 + 16:S0 + 32], AF.Sqrt, bias=EPS5, scale=1.0))
            qV.wait(qS, ss1)
            vop(lambda v, S0=S0: v.reciprocal(stats[:, S0 + 48:S0 + 64], stats[:, S0 + 32:S0 + 48]))
            ISTD3 = stats[:, S0 + 48:S0 + 64].unsqueeze(2).broadcast_to((128, 16, 8))
            vop(lambda v, S0=S0: v.tensor_tensor(
                stats[:, S0:S0 + 16], stats[:, S0:S0 + 16], stats[:, S0 + 48:S0 + 64], ALU.mult))
            MISTD3 = stats[:, S0:S0 + 16].unsqueeze(2).broadcast_to((128, 16, 8))
            vop(lambda v, LT=LT, ISTD3=ISTD3: v.tensor_tensor(LT, LT, ISTD3, ALU.mult))
            vop(lambda v, TMP3=TMP3, MISTD3=MISTD3: v.tensor_copy(TMP3, MISTD3))
            vop(lambda v, TMP3=TMP3, COLSB=COLSB: v.tensor_tensor(TMP3, TMP3, COLSB, ALU.mult))
            vop(lambda v, LT=LT, TMP3=TMP3: v.tensor_tensor(LT, LT, TMP3, ALU.subtract))
            vop(lambda v, LT=LT, BRB=BRB: v.tensor_tensor(LT, LT, BRB, ALU.add))
            NL = L3[:, :, 9:10]
            vop(lambda v, NL=NL, LT=LT: v.tensor_reduce(NL, LT, axis=AX.X, op=ALU.max, negate=True))
            dv = vop(lambda v, DIFF3=DIFF3, LT=LT, NL=NL: v.tensor_tensor(
                DIFF3, LT, NL.broadcast_to((128, 16, 8)), ALU.add))
            vop(lambda v, MB3=MB3, DIFF3=DIFF3: v.tensor_scalar(MB3, DIFF3, 0.0, None, ALU.is_lt, ALU.bypass))
            vop(lambda v, MB3=MB3: v.tensor_scalar(MB3, MB3, 1024.0, None, ALU.mult, ALU.bypass))
            vop(lambda v, MB3=MB3: v.tensor_tensor(MB3, MB3, IOTA3, ALU.add))
            vop(lambda v, L3=L3, MB3=MB3: v.tensor_reduce(L3[:, :, 10:11], MB3, axis=AX.X, op=ALU.min))
            qS.wait(qV, qV.n)
            se = sop(lambda s, MB3=MB3, DIFF3=DIFF3: s.activation(MB3, DIFF3, AF.Exp))
            qV.wait(qS, se)
            vop(lambda v, L3=L3, MB3=MB3: v.tensor_reduce(L3[:, :, 9:10], MB3, axis=AX.X, op=ALU.add))
            vop(lambda v, b=b, L3=L3: v.reciprocal(gates[:, b * 16:(b + 1) * 16],
                                                   L3[:, :, 9:10].rearrange("p i c -> p (i c)")))
            ltm_free[par] = vop(lambda v, S0=S0, L3=L3: v.tensor_copy(
                stats[:, S0 + 16:S0 + 32], L3[:, :, 10:11].rearrange("p i c -> p (i c)")))
            qT.wait(qV, ltm_free[par])
            if tr_evac.get((par, 0), 0) > 0:
                qT.wait(qV, tr_evac[(par, 0)])
            t2 = tr(pT_[0:16, 0:128], stats[:, S0 + 16:S0 + 32])
            tr_evac[(par, 0)] = None
            qV.wait(qT, t2)
            fe = vop(lambda v, par=par, pT_=pT_: v.tensor_copy(FLATp[par], pT_[0:16, 0:128]))
            tr_evac[(par, 0)] = fe
            qG.wait(qV, fe)
            fl = ld(IDXRp[par], FLATp[par], q=qG)
            qT.wait(qG, fl)
            for tau in range(NTAU):
                pb2 = pBc if tau % 2 == 0 else p[3]
                key = (par, "m") if tau % 2 == 0 else ("m3",)
                if tr_evac.get(key, 0) > 0:
                    qT.wait(qV, tr_evac[key])
                mmv = mm(pb2[0:8, 0:512], ONESROW[0:1, 0:8],
                         IDXRp[par][0:1, tau * 512:(tau + 1) * 512], True, True)
                qV.wait(qT, mmv)
                tr_evac[key] = memq_tok[par] = vop(lambda v, b=b, tau=tau, pb2=pb2: v.tensor_scalar(
                    emp[:, b * T + tau * 512:b * T + (tau + 1) * 512],
                    pb2[0:8, 0:512], IOTA8C, None, ALU.is_equal, ALU.bypass))
        qD.wait(qV, qV.n)
        qG.wait(qV, qV.n)

        for j in range(4):
            init = 0.0 if j == 0 else emp2[:, j * 4096 - 1:j * 4096]
            vop(lambda v, j=j, init=init: v.tensor_tensor_scan(
                emp2[:, j * 4096:(j + 1) * 4096],
                ONESCOL[0:8, 0:1].broadcast_to((8, 4096)),
                emp[:, j * 4096:(j + 1) * 4096], init, ALU.mult, ALU.add))
        vop(lambda v: v.tensor_tensor(emp2[:], emp2[:], emp[:], ALU.mult))
        vop(lambda v: v.tensor_scalar(emp2[:], emp2[:], 2048.0, None, ALU.is_lt, ALU.bypass))
        vop(lambda v: v.tensor_tensor(emp2[:], emp2[:], emp[:], ALU.mult))
        if debug:
            qG.wait(qV, qV.n)
            ld(kept_s, emp2[:], q=qG)
        vop(lambda v: v.memset(stats[:, 0:16], 0.0))
        for b in range(B):
            vop(lambda v, b=b: v.tensor_scalar(stats[:, 16:32], gates[:, b * 16:(b + 1) * 16],
                                               COREOHB[:, b:b + 1], None, ALU.mult, ALU.bypass))
            vop(lambda v: v.tensor_tensor(stats[:, 0:16], stats[:, 0:16], stats[:, 16:32], ALU.add))
        qT.wait(qV, qV.n)
        t3 = tr(p[1][0:16, 0:128], stats[:, 0:16])
        qV.wait(qT, t3)
        vop(lambda v: v.tensor_copy(rowp[0:16, 2048:2176], p[1][0:16, 0:128]))
        qG.wait(qV, qV.n)
        fl2 = ld(GATEROW, rowp[0:16, 2048:2176], q=qG)
        vop(lambda v: v.memset(wem[:], 0.0))
        for b in range(B):
            vop(lambda v, b=b: v.tensor_scalar(emp[:, 0:T], emp2[:, b * T:(b + 1) * T],
                                               COREOH8[:, b:b + 1], None, ALU.mult, ALU.bypass))
            vop(lambda v: v.tensor_tensor(wem[:], wem[:], emp[:, 0:T], ALU.add))
        qT.wait(qG, fl2)
        qT.wait(qV, qV.n)
        for tau in range(NTAU):
            mmg = mm(p[2][0:8, 0:512], ONESROW[0:1, 0:8],
                     GATEROW[0:1, tau * 512:(tau + 1) * 512], True, True)
            qV.wait(qT, mmg)
            vop(lambda v, tau=tau: v.tensor_tensor(
                wem[:, tau * 512:(tau + 1) * 512], wem[:, tau * 512:(tau + 1) * 512],
                p[2][0:8, 0:512], ALU.mult))
            qT.wait(qV, qV.n)
        if debug:
            qG.wait(qV, qV.n)
            ld(wem_s, wem[:], q=qG)

        barrier()
        s_base = qS.n
        for i in range(NTI):
            buf = tm0 if i % 2 == 0 else tm1
            if i >= 2:
                qD.wait(qS, s_base + 2 * (i - 1))
            t_ld = ld(buf, xo_tm[i * 128:(i + 1) * 128, :])
            qS.wait(qD, t_ld)
            sop(lambda s, bu=buf, i=i: s.activation(
                SINK, bu, AF.Square, accum_out=stats[:, i:i + 1]))
            sop(lambda s, bu=buf, i=i: s.activation(
                SINK, bu, AF.Identity, accum_out=stats[:, 16 + i:17 + i]))
        qV.wait(qS, qS.n)
        vop(lambda v: v.tensor_scalar(stats[:, 16:32], stats[:, 16:32], 1.0 / D, None, ALU.mult, ALU.bypass))
        vop(lambda v: v.tensor_scalar(stats[:, 32:48], stats[:, 0:16], 1.0 / D, None, ALU.mult, ALU.bypass))
        vop(lambda v: v.tensor_tensor(stats[:, 48:64], stats[:, 16:32], stats[:, 16:32], ALU.mult))
        vop(lambda v: v.tensor_tensor(stats[:, 32:48], stats[:, 32:48], stats[:, 48:64], ALU.subtract))
        qS.wait(qV, qV.n)
        sop(lambda s: s.activation(stats[:, 48:64], stats[:, 32:48], AF.Sqrt, bias=EPS5, scale=1.0))
        qV.wait(qS, qS.n)
        vop(lambda v: v.reciprocal(stats[:, 0:16], stats[:, 48:64]))
        qT.wait(qV, qV.n)
        t4 = tr(p[1][0:32, 0:128], stats[:, 0:32])
        qV.wait(qT, t4)
        vop(lambda v: v.tensor_copy(rowp[0:32, 2048:2176], p[1][0:32, 0:128]))
        qG.wait(qV, qV.n)
        fl2a = ld(rowp[0:1, 0:T], rowp[0:16, 2048:2176], q=qG)
        fl3 = ld(rowp[0:1, 4352:4352 + T], rowp[16:32, 2048:2176], q=qG)
        ISTDR, MR = rowp[0:1, 0:T], rowp[0:1, 4352:4352 + T]
        qT.wait(qG, fl2a)
        qT.wait(qG, fl3)
        for c in range(NCH):
            for tau in range(NTAU):
                mm(p[1][:, 0:512], ONESROW, MR[0:1, tau * 512:(tau + 1) * 512], True, True)
                m2 = mm(p[2][:, 0:512], ONESROW, ISTDR[0:1, tau * 512:(tau + 1) * 512], True, True)
                lk = ld(sc0, xo_fm[c * 128:(c + 1) * 128, tau * 512:(tau + 1) * 512])
                qV.wait(qT, m2)
                qV.wait(qD, lk)
                dst = ch(A, c)[:, tau * 512:(tau + 1) * 512]
                vop(lambda v, dst=dst: v.tensor_tensor(dst, sc0[:], p[1][:, 0:512], ALU.subtract))
                vop(lambda v, dst=dst: v.tensor_tensor(dst, dst, p[2][:, 0:512], ALU.mult))
                qD.wait(qV, qV.n)
                qT.wait(qV, qV.n)
            qG.wait(qV, qV.n)
            ld(xnT_s[c * 128:(c + 1) * 128, :], ch(A, c), q=qG)

        barrier()
        ld(rowp[0:8, 2304:3328], b2stat.rearrange("e c f -> e (c f)"), q=qD)
        B2S = lambda c: rowp[0:8, 2304 + c * 128:2304 + (c + 1) * 128]
        WBp = (C[:, 2048:3072], C[:, 3072:4096])
        wbuf = (C[:, 0:1024], C[:, 1024:2048])
        wbuf4 = (C[:, 0:1024], C[:, 1024:2048], C[:, 4096:5120], C[:, 5120:6144])
        wslot, wtok = [0], [0, 0, 0, 0]
        ptok = [0] * 8
        ld(rowp[0:8, 3328:4352], e8sel, q=qD)
        ld(stats[:, 128:192].rearrange("p (e c) -> p e c", c=8), b1_col.rearrange("e p c -> p e c"), q=qD)
        E8S = lambda e: rowp[0:8, 3328 + e * 128:3328 + (e + 1) * 128]
        qT.wait(qD)
        qS.wait(qD)
        wb_done = {}
        def make_wb(half, e):
            hof2 = half * 1024
            ep2 = e % 2
            for t2 in range(2):
                if ptok[0] > 0:
                    qT.wait(qV, ptok[0])
                m3 = mm(p[0][:, 0:512], E8S(e),
                        wem[0:8, hof2 + t2 * 512:hof2 + (t2 + 1) * 512], True, True)
                qV.wait(qT, m3)
                ptok[0] = vop(lambda v, t2=t2, ep2=ep2: v.tensor_copy(
                    WBp[ep2][:, t2 * 512:(t2 + 1) * 512], p[0][:, 0:512]))
            wb_done[(half, e)] = ptok[0]
        for half in range(2):
            hof = half * 1024
            qV.wait(qG)
            qS.wait(qG)
            for e in range(E):
                ep = e % 2
                if (half, e) not in wb_done:
                    make_wb(half, e)
                B1C = stats[:, 128 + e * 8:128 + (e + 1) * 8]
                WB0 = WBp[ep][:, 0:512]
                WB1 = WBp[ep][:, 512:1024]
                if e + 1 < E:
                    make_wb(half, e + 1)
                for li, Wm in enumerate((W1moe, W2moe)):
                    if li == 1:
                        qT.wait(qV, qV.n)
                        qT.wait(qS, qS.n)
                    for t2 in range(2):
                        tslc = slice(hof + t2 * 512, hof + (t2 + 1) * 512)
                        for k in range(NCH):
                            slot = wslot[0] % 4
                            wslot[0] += 1
                            wb = wbuf4[slot]
                            if wtok[slot] > 0:
                                qD.wait(qT, wtok[slot])
                            lw = ld(wb, Wm[e, k * 128:(k + 1) * 128, :])
                            qT.wait(qD, lw)
                            for f in range(NCH):
                                src = (ch(A, k)[:, tslc] if li == 0
                                       else Bp[:, k * 1024 + t2 * 512:k * 1024 + (t2 + 1) * 512])
                                if k == 0 and ptok[f] > 0:
                                    qT.wait(qV, ptok[f])
                                wtok[slot] = mm(p[f][:, 0:512], wb[:, f * 128:(f + 1) * 128], src,
                                                k == 0, k == NCH - 1)
                        qV.wait(qT, qT.n)
                        for f in range(NCH):
                            if li == 0:
                                dsth = Bp[:, f * 1024 + t2 * 512:f * 1024 + (t2 + 1) * 512]
                                ptok[f] = vop(lambda v, f=f, dsth=dsth, wb2=(WB0 if t2 == 0 else WB1):
                                              v.tensor_tensor(dsth, p[f][:, 0:512], wb2, ALU.mult))
                                qS.wait(qV, ptok[f])
                                stok = sop(lambda s, dsth=dsth, f=f, B1C=B1C: s.activation(
                                    dsth, dsth, AF.Relu, bias=B1C[:, f:f + 1], scale=1.0))
                                qV.wait(qS, stok)
                                vop(lambda v, dsth=dsth, wb2=(WB0 if t2 == 0 else WB1):
                                    v.tensor_tensor(dsth, dsth, wb2, ALU.mult))
                            else:
                                ydst = Bp[:, 8192 + f * 1024 + t2 * 512:8192 + f * 1024 + (t2 + 1) * 512]
                                if e == 0:
                                    ptok[f] = vop(lambda v, ydst=ydst, f=f: v.tensor_copy(ydst, p[f][:, 0:512]))
                                else:
                                    ptok[f] = vop(lambda v, ydst=ydst, f=f: v.tensor_tensor(
                                        ydst, ydst, p[f][:, 0:512], ALU.add))
            qT.wait(qV, qV.n)
            for f in range(NCH):
                for t2 in range(2):
                    if ptok[0] > 0:
                        qT.wait(qV, ptok[0])
                    m4 = mm(p[0][:, 0:512], B2S(f),
                            wem[0:8, hof + t2 * 512:hof + (t2 + 1) * 512], True, True)
                    qV.wait(qT, m4)
                    ydst = Bp[:, 8192 + f * 1024 + t2 * 512:8192 + f * 1024 + (t2 + 1) * 512]
                    ptok[0] = vop(lambda v, ydst=ydst: v.tensor_tensor(ydst, ydst, p[0][:, 0:512], ALU.add))
                qG.wait(qV, qV.n)
                ld(g0_s[f * 128:(f + 1) * 128, hof:hof + 1024],
                   Bp[:, 8192 + f * 1024:8192 + (f + 1) * 1024], q=qG)

        barrier()
        WPAN = C[:, 0:3072]
        XP = C[:, 3072:5120]
        R_ = C[:, 5120:7168]
        qD.wait(qG)
        for c in range(NCH):
            nw = []
            for oi, WT in enumerate((WinT, WigT, WrgT)):
                nw.append(ld(WPAN[:, oi * 1024:(oi + 1) * 1024].rearrange("p (k f) -> p k f", f=128),
                             WT[:, c * 128:(c + 1) * 128].rearrange("(k p) f -> p k f", p=128)))
            qT.wait(qD)
            I_ = ch(Bp, c)
            for tau in range(NTAU):
                for k in range(NCH):
                    for oi in range(3):
                        mm(p[oi][:, 0:512], WPAN[:, oi * 1024 + k * 128:oi * 1024 + (k + 1) * 128],
                           ch(A, k)[:, tau * 512:(tau + 1) * 512], k == 0, k == NCH - 1)
                qV.wait(qT, qT.n)
                qS.wait(qT, qT.n)
                sl = slice(tau * 512, (tau + 1) * 512)
                vop(lambda v, sl=sl: v.tensor_copy(XP[:, sl], p[0][:, 0:512]))
                sop(lambda s, sl=sl, c=c: s.activation(I_[:, sl], p[1][:, 0:512], AF.Sigmoid,
                                                       bias=colb[:, 0 + c:1 + c], scale=1.0))
                sop(lambda s, sl=sl, c=c: s.activation(R_[:, sl], p[2][:, 0:512], AF.Sigmoid,
                                                       bias=colb[:, 8 + c:9 + c], scale=1.0))
                qT.wait(qV, qV.n)
                qT.wait(qS, qS.n)
            qV.wait(qS, qS.n)
            vop(lambda v, c=c: v.tensor_scalar(R_[:], R_[:], CLOGA[:, c:c + 1], None, ALU.mult, ALU.bypass))
            qS.wait(qV, qV.n)
            sop(lambda s: s.activation(R_[:], R_[:], AF.Exp))
            qV.wait(qS, qS.n)
            vop(lambda v: v.tensor_tensor(XP[:], XP[:], I_[:], ALU.mult))
            vop(lambda v: v.tensor_tensor(I_[:], R_[:], R_[:], ALU.mult))
            qS.wait(qV, qV.n)
            sop(lambda s: s.activation(I_[:], I_[:], AF.Sqrt, bias=GEPS, scale=NEG1))
            qV.wait(qS, qS.n)
            vop(lambda v: v.tensor_tensor(XP[:], XP[:], I_[:], ALU.mult))
            vop(lambda v, c=c: v.tensor_tensor_scan(
                ch(Bp, c), R_[:], XP[:], colb[:, 24 + c:25 + c], ALU.mult, ALU.add))
            qD.wait(qV, qV.n)
        for c in range(NCH):
            vop(lambda v, c=c: v.tensor_tensor(ch(A, c), ch(Bp, c), ch(Bp, c), ALU.mult))
        qT.wait(qV, qV.n)
        for tau in range(NTAU):
            for c in range(NCH):
                mm(p[3][0:1, 0:512], ONESCOL, ch(A, c)[:, tau * 512:(tau + 1) * 512],
                   c == 0, c == NCH - 1)
            qV.wait(qT, qT.n)
            vop(lambda v, tau=tau: v.tensor_copy(rowp[0:1, tau * 512:(tau + 1) * 512], p[3][0:1, 0:512]))
            qT.wait(qV, qV.n)
        vop(lambda v: v.tensor_scalar(rowp[0:1, 0:T], rowp[0:1, 0:T], 1.0 / D, None, ALU.mult, ALU.bypass))
        qS.wait(qV, qV.n)
        sop(lambda s: s.activation(rowp[0:1, 0:T], rowp[0:1, 0:T], AF.Sqrt, bias=EPS6[0:1, :], scale=1.0))
        qV.wait(qS, qS.n)
        vop(lambda v: v.reciprocal(rowp[0:1, 2304:2304 + T], rowp[0:1, 0:T]))
        RMSR = rowp[0:1, 2304:2304 + T]
        qT.wait(qV, qV.n)
        for c in range(NCH):
            for tau in range(NTAU):
                m5 = mm(p[4][:, 0:512], ONESROW, RMSR[0:1, tau * 512:(tau + 1) * 512], True, True)
                qV.wait(qT, m5)
                dst = ch(A, c)[:, tau * 512:(tau + 1) * 512]
                vop(lambda v, dst=dst, c=c, tau=tau: v.tensor_tensor(
                    dst, ch(Bp, c)[:, tau * 512:(tau + 1) * 512], p[4][:, 0:512], ALU.mult))
                qT.wait(qV, qV.n)
            vop(lambda v, c=c: v.tensor_scalar(ch(A, c), ch(A, c), colb[:, 32 + c:33 + c],
                                               None, ALU.mult, ALU.bypass))
        def proj_stream(W_src, act_pool, out_to_Bp=True, transposed_w=False):
            qV.wait(qG)
            qT.wait(qV, qV.n)
            qD.wait(qT, qT.n)
            ptok2 = [0] * 8
            ps_buf = (C[:, 0:1024], C[:, 1024:2048], C[:, 4096:5120], C[:, 5120:6144])
            ps_tok = [0, 0, 0, 0]
            gslot = [0]
            for tau in range(NTAU):
                for k in range(NCH):
                    slot = gslot[0] % 4
                    gslot[0] += 1
                    wb = ps_buf[slot]
                    if ps_tok[slot] > 0:
                        qD.wait(qT, ps_tok[slot])
                    lw = ld(wb, W_src[k * 128:(k + 1) * 128, :])
                    qT.wait(qD, lw)
                    for f in range(NCH):
                        if k == 0 and ptok2[f] > 0:
                            qT.wait(qV, ptok2[f])
                        ps_tok[slot] = mm(p[f][:, 0:512], wb[:, f * 128:(f + 1) * 128],
                                          ch(act_pool, k)[:, tau * 512:(tau + 1) * 512],
                                          k == 0, k == NCH - 1)
                qV.wait(qT, qT.n)
                for f in range(NCH):
                    ptok2[f] = vop(lambda v, f=f, tau=tau: v.tensor_copy(
                        ch(Bp, f)[:, tau * 512:(tau + 1) * 512], p[f][:, 0:512]))
            qT.wait(qV, qV.n)

        qT.wait(qV, qV.n)
        proj_stream(WoutT, A)
        for f in range(NCH):
            qG.wait(qV, qV.n)
            ld(s0_s[f * 128:(f + 1) * 128, :], ch(Bp, f), q=qG)

        barrier()
        qD.wait(qG)
        for tau in range(NTAU):
            for c in range(NCH):
                l1 = ld(sc0, xo_fm[c * 128:(c + 1) * 128, tau * 512:(tau + 1) * 512])
                l2_ = ld(sc1, g0_s[c * 128:(c + 1) * 128, tau * 512:(tau + 1) * 512])
                l3_ = ld(sc2, s0_s[c * 128:(c + 1) * 128, tau * 512:(tau + 1) * 512])
                qS.wait(qD, l2_)
                sop(lambda s: s.activation(sc1[:], sc1[:], AF.Silu))
                qV.wait(qS, qS.n)
                qV.wait(qD, l1)
                qV.wait(qD, l3_)
                dst = ch(Bp, c)[:, tau * 512:(tau + 1) * 512]
                vop(lambda v, dst=dst: v.tensor_tensor(dst, sc1[:], sc2, ALU.mult))
                vop(lambda v, dst=dst: v.tensor_tensor(dst, dst, sc0[:], ALU.add))
                vop(lambda v, dst=dst: v.tensor_tensor(sc0[:], dst, dst, ALU.mult))
                qT.wait(qV, qV.n)
                mm(p[5][0:1, 0:512], ONESCOL, dst, c == 0, c == NCH - 1)
                mm(p[6][0:1, 0:512], ONESCOL, sc0[:], c == 0, c == NCH - 1)
                qG.wait(qV, qV.n)
                ld(x1_s[c * 128:(c + 1) * 128, tau * 512:(tau + 1) * 512], dst, q=qG)
                qD.wait(qT, qT.n)
                qD.wait(qS, qS.n)
            qV.wait(qT, qT.n)
            vop(lambda v, tau=tau: v.tensor_copy(rowp[0:1, tau * 512:(tau + 1) * 512], p[5][0:1, 0:512]))
            vop(lambda v, tau=tau: v.tensor_copy(rowp[0:1, 2304 + tau * 512:2304 + (tau + 1) * 512], p[6][0:1, 0:512]))
            qT.wait(qV, qV.n)
        SU, SQ = rowp[0:1, 0:T], rowp[0:1, 2304:2304 + T]
        vop(lambda v: v.tensor_scalar(SU, SU, 1.0 / D, None, ALU.mult, ALU.bypass))
        vop(lambda v: v.tensor_scalar(SQ, SQ, 1.0 / D, None, ALU.mult, ALU.bypass))
        vop(lambda v: v.tensor_tensor(rowp[0:1, 4352:4352 + T], SU, SU, ALU.mult))
        vop(lambda v: v.tensor_tensor(SQ, SQ, rowp[0:1, 4352:4352 + T], ALU.subtract))
        qS.wait(qV, qV.n)
        sop(lambda s: s.activation(SQ, SQ, AF.Sqrt, bias=EPS5[0:1, :], scale=1.0))
        qV.wait(qS, qS.n)
        vop(lambda v: v.reciprocal(rowp[0:1, 4352:4352 + T], SQ))
        ISTD1 = rowp[0:1, 4352:4352 + T]
        qT.wait(qV, qV.n)
        for c in range(NCH):
            for tau in range(NTAU):
                mm(p[5][:, 0:512], ONESROW, SU[0:1, tau * 512:(tau + 1) * 512], True, True)
                m7 = mm(p[6][:, 0:512], ONESROW, ISTD1[0:1, tau * 512:(tau + 1) * 512], True, True)
                qV.wait(qT, m7)
                dst = ch(A, c)[:, tau * 512:(tau + 1) * 512]
                src = ch(Bp, c)[:, tau * 512:(tau + 1) * 512]
                vop(lambda v, dst=dst, src=src: v.tensor_tensor(dst, src, p[5][:, 0:512], ALU.subtract))
                vop(lambda v, dst=dst: v.tensor_tensor(dst, dst, p[6][:, 0:512], ALU.mult))
                qT.wait(qV, qV.n)
            qG.wait(qV, qV.n)
            ld(xn1_s[c * 128:(c + 1) * 128, :], ch(A, c), q=qG)

        barrier()
        n_wb = ld(C[:, 0:128].rearrange("p (k e) -> p k e", e=16),
                  Wb_i.rearrange("(k p) e -> p k e", p=128))
        for c in range(NCH):
            ld(rowp[0:16, 4352 + c * 128:4352 + (c + 1) * 128], hbq[c], q=qD)
        HB = lambda c: rowp[0:16, 4352 + c * 128:4352 + (c + 1) * 128]
        qT.wait(qD)
        BETA = rowp[0:16, 0:T]
        for tau in range(NTAU):
            for k in range(NCH):
                mm(p[7][0:16, 0:512], C[:, k * 16:(k + 1) * 16],
                   ch(A, k)[:, tau * 512:(tau + 1) * 512], k == 0, k == NCH - 1)
            qS.wait(qT, qT.n)
            sop(lambda s, tau=tau: s.activation(BETA[:, tau * 512:(tau + 1) * 512],
                                                p[7][0:16, 0:512], AF.Sigmoid,
                                                bias=colb[0:16, 40:41], scale=1.0))
            qT.wait(qS, qS.n)
        qT.wait(qS, qS.n)
        NRM = rowp[0:16, 2304:2304 + T]

        def l2_apply(out_scr, fold_beta):
            for tau in range(NTAU):
                for c in range(NCH):
                    src = ch(Bp, c)[:, tau * 512:(tau + 1) * 512]
                    buf = sc0 if c % 2 == 0 else sc1
                    qV.wait(qT, qT.n)
                    vop(lambda v, src=src, buf=buf: v.tensor_tensor(buf[:], src, src, ALU.mult))
                    qT.wait(qV, qV.n)
                    mm(p[7][0:16, 0:512], SEL(c), buf[:], c == 0, c == NCH - 1)
                qV.wait(qT, qT.n)
                vop(lambda v, tau=tau: v.tensor_copy(NRM[:, tau * 512:(tau + 1) * 512],
                                                     p[7][0:16, 0:512]))
            qS.wait(qV, qV.n)
            sop(lambda s: s.activation(NRM, NRM, AF.Sqrt))
            qV.wait(qS, qS.n)
            vop(lambda v: v.tensor_scalar(NRM, NRM, 1e-12, None, ALU.max, ALU.bypass))
            vop(lambda v: v.reciprocal(NRM, NRM))
            if fold_beta:
                vop(lambda v: v.tensor_tensor(NRM, NRM, BETA, ALU.mult))
            qT.wait(qV, qV.n)
            for c in range(NCH):
                for tau in range(NTAU):
                    m8 = mm(p[7][:, 0:512], HB(c), NRM[:, tau * 512:(tau + 1) * 512], True, True)
                    qV.wait(qT, m8)
                    dst = ch(Bp, c)[:, tau * 512:(tau + 1) * 512]
                    vop(lambda v, dst=dst: v.tensor_tensor(dst, dst, p[7][:, 0:512], ALU.mult))
                    qT.wait(qV, qV.n)
                qG.wait(qV, qV.n)
                ld(out_scr[c * 128:(c + 1) * 128, :], ch(Bp, c), q=qG)

        proj_stream(Wq_i, A)
        l2_apply(qn_s, False)
        proj_stream(Wk_i, A)
        l2_apply(kb_s, True)
        proj_stream(Wv_i, A)
        qD.wait(qG)
        for c in range(NCH):
            for g4 in range(4):
                for jj in range(4):
                    i = g4 * 4 + jj
                    tr(p[7][:, jj * 128:(jj + 1) * 128], ch(Bp, c)[:, i * 128:(i + 1) * 128])
                qV.wait(qT, qT.n)
                qV.wait(qG)
                vop(lambda v: v.tensor_copy(sc0[:], p[7][:, 0:512]))
                qG.wait(qV, qV.n)
                ld(vT_s[g4 * 512:(g4 + 1) * 512, c * 128:(c + 1) * 128].rearrange(
                    "(i p) d -> p i d", p=128),
                   sc0[:].rearrange("p (i d) -> p i d", d=128), q=qG)
                qT.wait(qV, qV.n)

        barrier()
        QHp = A[:, 0:2048]
        KHp = A[:, 2048:4096]
        VTa = A[:, 4096:5120]
        VTb = A[:, 5120:6144]
        AOSp = A[:, 6144:8192]
        SC3 = C[:, 7680:8192]
        qD.wait(qG)
        qD.wait(qT, qT.n)
        qG.wait(qT, qT.n)
        qG.wait(qV, qV.n)
        ntl = [ld(TRIL(j), tril4[j], q=qG) for j in range(4)]
        qV.wait(qG, ntl[-1])
        for hp in range(H // 2):
            hA, hB = 2 * hp, 2 * hp + 1
            qV.wait(qG)
            qD.wait(qV, qV.n)
            ld(QHp[0:64, :], qn_s[hA * 64:(hA + 1) * 64, :])
            ld(QHp[64:128, :], qn_s[hB * 64:(hB + 1) * 64, :])
            ld(KHp[0:64, :], kb_s[hA * 64:(hA + 1) * 64, :])
            ld(KHp[64:128, :], kb_s[hB * 64:(hB + 1) * 64, :])
            ld(VTa.rearrange("p (i d) -> p i d", d=64),
               vT_s[:, hA * 64:(hA + 1) * 64].rearrange("(i p) d -> p i d", p=128))
            ld(VTb.rearrange("p (i d) -> p i d", d=64),
               vT_s[:, hB * 64:(hB + 1) * 64].rearrange("(i p) d -> p i d", p=128))
            qT.wait(qD)
            etok = [0, 0]
            for qtau in range(NTAU):
                jmax = qtau * 4 + 4

                def mm1p(j, qtau=qtau):
                    psA = p[0] if j % 2 == 0 else p[4]
                    psB = p[1] if j % 2 == 0 else p[5]
                    if etok[j % 2] > 0:
                        qT.wait(qV, etok[j % 2])
                    mm(psA[:, 0:512], KHp[0:64, j * 128:(j + 1) * 128],
                       QHp[0:64, qtau * 512:(qtau + 1) * 512], True, True, tp=(0, 0))
                    return mm(psB[:, 0:512], KHp[64:128, j * 128:(j + 1) * 128],
                              QHp[64:128, qtau * 512:(qtau + 1) * 512], True, True, tp=(64, 0))

                t1 = mm1p(0)
                for j in range(jmax):
                    psA = p[0] if j % 2 == 0 else p[4]
                    psB = p[1] if j % 2 == 0 else p[5]
                    scbA = sc0[:] if j % 2 == 0 else sc1[:]
                    scbB = sc2 if j % 2 == 0 else SC3
                    qV.wait(qT, t1)
                    o = j - qtau * 4
                    if o >= 0:
                        vop(lambda v, scbA=scbA, psA=psA, o=o: v.tensor_tensor(
                            scbA, psA[:, 0:512], TRIL(o), ALU.mult))
                        etok[j % 2] = vop(lambda v, scbB=scbB, psB=psB, o=o: v.tensor_tensor(
                            scbB, psB[:, 0:512], TRIL(o), ALU.mult))
                    else:
                        vop(lambda v, scbA=scbA, psA=psA: v.tensor_copy(
                            scbA, psA[:, 0:512]))
                        etok[j % 2] = vop(lambda v, scbB=scbB, psB=psB: v.tensor_copy(
                            scbB, psB[:, 0:512]))
                    if j + 1 < jmax:
                        t1 = mm1p(j + 1)
                    qT.wait(qV, etok[j % 2])
                    mm(p[2][0:64, 0:512], VTa[:, j * 64:(j + 1) * 64],
                       scbA,
                       j == 0, j == jmax - 1, tp=(0, 0))
                    mm(p[3][64:128, 0:512], VTb[:, j * 64:(j + 1) * 64],
                       scbB,
                       j == 0, j == jmax - 1, tp=(0, 64))
                qV.wait(qT, qT.n)
                vop(lambda v, qtau=qtau: v.tensor_copy(
                    AOSp[0:64, qtau * 512:(qtau + 1) * 512], p[2][0:64, 0:512]))
                vop(lambda v, qtau=qtau: v.tensor_copy(
                    AOSp[64:128, qtau * 512:(qtau + 1) * 512], p[3][64:128, 0:512]))
                qT.wait(qV, qV.n)
            qG.wait(qV, qV.n)
            ld(ao_s[hA * 64:(hA + 1) * 64, :], AOSp[0:64, :], q=qG)
            ld(ao_s[hB * 64:(hB + 1) * 64, :], AOSp[64:128, :], q=qG)
            qD.wait(qG)
        for k in range(NCH):
            ld(ch(A, k), ao_s[k * 128:(k + 1) * 128, :])
        qT.wait(qD)
        proj_stream(Wo_i, A)
        for f in range(NCH):
            qG.wait(qV, qV.n)
            ld(s1_s[f * 128:(f + 1) * 128, :], ch(Bp, f), q=qG)

        barrier()
        kst0 = C16[:, 0:2048]
        kst1 = C16[:, 2048:4096]
        keff = C16[:, 4096:6144]
        XNt = C[:, 3072:7168]
        OUTS = C[:, 3072:7168]
        GT = sc2
        XHt = A16[:, 0:4096]
        XLt = A16[:, 4096:8192]
        D1h = B16[:, 0:16384]
        D1l = B16[:, 16384:32768]
        D2h = A16[:, 0:16384]
        D2l = A16[:, 16384:32768]
        STAGE = A[:, 2048:3072]
        qD.wait(qG)
        qG.wait(qD)
        for tau in range(NTAU):
            qD.wait(qT, qT.n)
            qD.wait(qG)
            nl = []
            for c in range(NCH):
                nl.append(ld(XNt[:, c * 512:(c + 1) * 512],
                             xn1_s[c * 128:(c + 1) * 128, tau * 512:(tau + 1) * 512]))
            qV.wait(qD)
            for c in range(NCH):
                xs = slice(c * 512, (c + 1) * 512)
                vop(lambda v, xs=xs: v.tensor_copy(XHt[:, xs], XNt[:, xs]))
                vop(lambda v, xs=xs: v.tensor_tensor(sc0[:], XNt[:, xs], XHt[:, xs], ALU.subtract))
                vop(lambda v, xs=xs: v.tensor_copy(XLt[:, xs], sc0[:]))
            ft0 = [0, 0]
            splitdone0 = [0, 0]
            subtok = [0, 0]
            mmtok0 = [0, 0]
            ck0 = {}
            ck0[0] = ld(kst0[:, 0:2048], k0r[0].rearrange("k b f -> k (b f)"), q=qG)
            for fb in range(32):
                par = fb % 2
                kq = kst0 if par == 0 else kst1
                pp = p[3] if par == 0 else p[7]
                sa = sc0 if par == 0 else sc1
                sb = sc1 if par == 0 else sc0
                qV.wait(qG, ck0[fb])
                if mmtok0[par] > 0:
                    qV.wait(qT, mmtok0[par])
                ke = keff[:, 0:1024] if par == 0 else keff[:, 1024:2048]
                subtok[par] = vop(lambda v, kq=kq, ke=ke: v.tensor_tensor(
                    ke, kq[:, 0:1024], kq[:, 1024:2048], ALU.subtract))
                if fb + 1 < 32:
                    npar = (fb + 1) % 2
                    nkq = kst0 if npar == 0 else kst1
                    if subtok[npar] > 0:
                        qG.wait(qV, subtok[npar])
                    ck0[fb + 1] = ld(nkq[:, 0:2048],
                                     k0r[fb + 1].rearrange("k b f -> k (b f)"), q=qG)
                qT.wait(qV, subtok[par])
                if ft0[par] > 0:
                    qT.wait(qS, ft0[par])
                for kb in range(8):
                    stat = ke[:, kb * 128:(kb + 1) * 128]
                    mm(pp[:, 0:512], stat, XHt[:, kb * 512:(kb + 1) * 512], kb == 0, False)
                    mm(pp[:, 0:512], stat, XLt[:, kb * 512:(kb + 1) * 512], False, kb == 7)
                mmtok0[par] = qT.n
                qS.wait(qT, qT.n)
                if splitdone0[par] > 0:
                    qS.wait(qV, splitdone0[par])
                ft0[par] = sop(lambda s, pp=pp, sa=sa: s.activation(
                    sa[:], pp[:, 0:512], AF.Relu, bias=0.0, scale=S0C8))
                qV.wait(qS, ft0[par])
                fs = slice(fb * 512, (fb + 1) * 512)
                vop(lambda v, fs=fs, sa=sa: v.tensor_copy(D1h[:, fs], sa[:]))
                vop(lambda v, fs=fs, sa=sa, sb=sb: v.tensor_tensor(sb[:], sa[:], D1h[:, fs], ALU.subtract))
                splitdone0[par] = vop(lambda v, fs=fs, sb=sb: v.tensor_copy(D1l[:, fs], sb[:]))

            def rowsum_neg_half(Dh, Dl, nkb):
                qT.wait(qV, qV.n)
                qT.wait(qS, qS.n)
                for kb in range(nkb):
                    mm(p[4][0:1, 0:512], NEGH16, Dh[:, kb * 512:(kb + 1) * 512], kb == 0, False)
                    mm(p[4][0:1, 0:512], NEGH16, Dl[:, kb * 512:(kb + 1) * 512], False, kb == nkb - 1)
                qV.wait(qT, qT.n)
                vop(lambda v: v.tensor_copy(rowp[0:1, 0:512], p[4][0:1, 0:512]))
                qT.wait(qV, qV.n)
                m9 = mm(p[5][:, 0:512], ONESROW, rowp[0:1, 0:512], True, True)
                qV.wait(qT, m9)
                vop(lambda v: v.tensor_copy(rsb[:], p[5][:, 0:512]))
                qT.wait(qV, qV.n)

            rowsum_neg_half(D1h, D1l, 32)
            ft = [0, 0]
            kbufs = (kst0, kst1, keff)
            mm_done = [0, 0, 0]
            ck_tok = {}
            nhalf = 64
            qG.wait(qT, qT.n)
            qG.wait(qV, qV.n)
            for hidx in range(2):
                bslot = hidx % 3
                ck_tok[hidx] = ld(kbufs[bslot][:, 0:2048],
                                  k1r[0, :, hidx * 16:(hidx + 1) * 16, :].rearrange("k b f -> k (b f)"), q=qG)
            for fb in range(32):
                pp = p[3] if fb % 2 == 0 else p[7]
                sa = sc0 if fb % 2 == 0 else sc1
                sb = sc1 if fb % 2 == 0 else sc0
                if ft[fb % 2] > 0:
                    qT.wait(qV, ft[fb % 2])
                for half2 in range(2):
                    hidx = fb * 2 + half2
                    bslot = hidx % 3
                    kq = kbufs[bslot]
                    qT.wait(qG, ck_tok[hidx])
                    for kb in range(16):
                        gkb = half2 * 16 + kb
                        stat = kq[:, kb * 128:(kb + 1) * 128]
                        mm(pp[:, 0:512], stat, D1h[:, gkb * 512:(gkb + 1) * 512],
                           gkb == 0, False)
                        mm(pp[:, 0:512], stat, D1l[:, gkb * 512:(gkb + 1) * 512],
                           False, gkb == 31)
                    mm_done[bslot] = qT.n
                    nxt = hidx + 2
                    if nxt < nhalf:
                        nslot = nxt % 3
                        qG.wait(qT, mm_done[nslot])
                        ck_tok[nxt] = ld(kbufs[nslot][:, 0:2048],
                                         k1r[nxt // 2, :, (nxt % 2) * 16:(nxt % 2 + 1) * 16, :]
                                         .rearrange("k b f -> k (b f)"), q=qG)
                qV.wait(qT, qT.n)
                ft[fb % 2] = vop(lambda v, pp=pp, sa=sa: v.tensor_tensor(sa[:], pp[:, 0:512], rsb[:], ALU.add))
                qS.wait(qV, ft[fb % 2])
                stok = sop(lambda s, sa=sa: s.activation(sa[:], sa[:], AF.Relu, bias=0.0, scale=S1C4))
                qV.wait(qS, stok)
                fs = slice(fb * 512, (fb + 1) * 512)
                vop(lambda v, fs=fs, sa=sa: v.tensor_copy(D2h[:, fs], sa[:]))
                vop(lambda v, fs=fs, sa=sa, sb=sb: v.tensor_tensor(sb[:], sa[:], D2h[:, fs], ALU.subtract))
                vop(lambda v, fs=fs, sb=sb: v.tensor_copy(D2l[:, fs], sb[:]))
            rowsum_neg_half(D2h, D2l, 32)
            mm_done = [0, 0, 0]
            ck_tok = {}
            nhalf2 = 16
            qG.wait(qT, qT.n)
            for hidx in range(2):
                ck_tok[hidx] = ld(kbufs[hidx % 3][:, 0:2048],
                                  k2r[0, :, hidx * 16:(hidx + 1) * 16, :].rearrange("k b f -> k (b f)"), q=qG)
            for fb in range(8):
                pp = p[3] if fb % 2 == 0 else p[7]
                for half2 in range(2):
                    hidx = fb * 2 + half2
                    bslot = hidx % 3
                    kq = kbufs[bslot]
                    qT.wait(qG, ck_tok[hidx])
                    for kb in range(16):
                        gkb = half2 * 16 + kb
                        stat = kq[:, kb * 128:(kb + 1) * 128]
                        mm(pp[:, 0:512], stat, D2h[:, gkb * 512:(gkb + 1) * 512],
                           gkb == 0, False)
                        mm(pp[:, 0:512], stat, D2l[:, gkb * 512:(gkb + 1) * 512],
                           False, gkb == 31)
                    mm_done[bslot] = qT.n
                    nxt = hidx + 2
                    if nxt < nhalf2:
                        nslot = nxt % 3
                        qG.wait(qT, mm_done[nslot])
                        ck_tok[nxt] = ld(kbufs[nslot][:, 0:2048],
                                         k2r[nxt // 2, :, (nxt % 2) * 16:(nxt % 2 + 1) * 16, :]
                                         .rearrange("k b f -> k (b f)"), q=qG)
                qV.wait(qT, qT.n)
                vop(lambda v, pp=pp: v.tensor_tensor(sc0[:], pp[:, 0:512], rsb[:], ALU.add))
                qS.wait(qV, qV.n)
                sop(lambda s: s.activation(GT, sc0[:], AF.Silu, bias=0.0, scale=S2C))
                qD.wait(qV, qV.n)
                qD.wait(qS, qS.n)
                lx = ld(sc0, x1_s[fb * 128:(fb + 1) * 128, tau * 512:(tau + 1) * 512])
                ls = ld(sc1, s1_s[fb * 128:(fb + 1) * 128, tau * 512:(tau + 1) * 512])
                qV.wait(qS, qS.n)
                qV.wait(qD, lx)
                qV.wait(qD, ls)
                vop(lambda v: v.tensor_tensor(GT, GT, sc1[:], ALU.mult))
                vop(lambda v: v.tensor_tensor(GT, GT, sc0[:], ALU.add))
                vop(lambda v, fb=fb: v.tensor_copy(OUTS[:, fb * 512:(fb + 1) * 512], GT))
                qD.wait(qV, qV.n)
                qT.wait(qV, qV.n)
            for tt_i in range(4):
                qV.wait(qG)
                for c in range(NCH):
                    tr(p[6][:, (c % 4) * 128:((c % 4) + 1) * 128],
                       OUTS[:, c * 512 + tt_i * 128:c * 512 + (tt_i + 1) * 128])
                    if c % 4 == 3:
                        qV.wait(qT, qT.n)
                        vop(lambda v, c=c, tt_i=tt_i: v.tensor_copy(
                            STAGE[:, (c - 3) * 128:(c + 1) * 128], p[6][:, 0:512]))
                        qT.wait(qV, qV.n)
                qG.wait(qV, qV.n)
                ld(out[tau * 512 + tt_i * 128:tau * 512 + (tt_i + 1) * 128, :],
                   STAGE, q=qG)

        final_g = qG.n
        qD.wait(qG)
        qV.wait(qG)

        with nc.Block() as block:
            @block.tensor
            def _(t):
                qT.emit(t)

            @block.vector
            def _(v):
                qV.emit(v)

            @block.scalar
            def _(s):
                qS.emit(s)

            @block.sync
            def _(d):
                qD.emit(d)

            @block.gpsimd
            def _(g):
                qG.emit(g)

    return nc, dram


_NC_CACHE = {}


def _consts():
    ident = np.eye(128, dtype=np.float32)
    iota = np.broadcast_to(np.arange(8, dtype=np.float32), (128, 16, 8)).reshape(128, 128).copy()
    iota8 = np.arange(8, dtype=np.float32).reshape(8, 1)
    tril = np.zeros((4, 128, 512), np.float32)
    for o in range(4):
        kk = o * 128 + np.arange(128)[:, None]
        qq = np.arange(512)[None, :]
        tril[o] = (kk <= qq).astype(np.float32)
    sel = np.zeros((8, 128, 16), np.float32)
    for c in range(8):
        sel[c, 0:64, 2 * c] = 1.0
        sel[c, 64:128, 2 * c + 1] = 1.0
    hb = np.zeros((8, 16, 128), np.float32)
    for c in range(8):
        hb[c, 2 * c, 0:64] = 1.0
        hb[c, 2 * c + 1, 64:128] = 1.0
    ones = np.ones((1, 128), np.float32)
    return ident, iota, iota8, tril, sel, hb, ones


def _shared_inputs(inp):
    x = np.ascontiguousarray(inp["L0_attn_x"].astype(np.float32, copy=False))
    ident, iota, iota8, tril, sel, hb, ones = _consts()
    return x, {
        "x_tm": x,
        "x_fm": np.ascontiguousarray(x.transpose(0, 2, 1)),
        "WinT": np.ascontiguousarray(inp["L0_attn_W_in"].T),
        "WigT": np.ascontiguousarray(inp["L0_attn_W_ig"].T),
        "WrgT": np.ascontiguousarray(inp["L0_attn_W_rg"].T),
        "WoutT": np.ascontiguousarray(inp["L0_attn_W_out"].T),
        "Wr_fm": np.ascontiguousarray(
            inp["L0_ffn_Wr"].reshape(8, 128, 8).transpose(1, 0, 2).reshape(128, 64)),
        "br_row": np.ascontiguousarray(inp["L0_ffn_br"].reshape(1, 8)),
        "big_col": np.ascontiguousarray(inp["L0_attn_b_ig"].reshape(8, 128).T),
        "brg_col": np.ascontiguousarray(inp["L0_attn_b_rg"].reshape(8, 128).T),
        "lam_col": np.ascontiguousarray(inp["L0_attn_lam"].reshape(8, 128).T),
        "h0_col": np.ascontiguousarray(inp["L0_attn_h0"].reshape(8, 128).T),
        "nw_col": np.ascontiguousarray(inp["L0_attn_norm_w"].reshape(8, 128).T),
        "W1moe": np.ascontiguousarray(inp["L0_ffn_W1"].astype(np.float32, copy=False)),
        "W2moe": np.ascontiguousarray(inp["L0_ffn_W2"].astype(np.float32, copy=False)),
        "b1_col": np.ascontiguousarray(inp["L0_ffn_b1"].reshape(8, 8, 128).transpose(0, 2, 1)),
        "b2stat": np.ascontiguousarray(inp["L0_ffn_b2"].reshape(8, 8, 128)),
        "Wq": np.ascontiguousarray(inp["L1_attn_Wq"]),
        "Wk": np.ascontiguousarray(inp["L1_attn_Wk"]),
        "Wv": np.ascontiguousarray(inp["L1_attn_Wv"]),
        "Wo": np.ascontiguousarray(inp["L1_attn_Wo"]),
        "Wb": np.ascontiguousarray(inp["L1_attn_Wb"]),
        "bb_col": np.ascontiguousarray(inp["L1_attn_bb"].reshape(16, 1)),
        "k0r": np.ascontiguousarray(inp["L1_ffn_k0"].astype(np.uint8)
                                    .reshape(16, 128, 32, 128).transpose(2, 1, 0, 3)),
        "k1r": np.ascontiguousarray(inp["L1_ffn_k1"].astype(np.uint8)
                                    .reshape(32, 128, 32, 128).transpose(2, 1, 0, 3)),
        "k2r": np.ascontiguousarray(inp["L1_ffn_k2"].astype(np.uint8)
                                    .reshape(32, 128, 8, 128).transpose(2, 1, 0, 3)),
        "s_sc": np.array([[float(inp["L1_ffn_s0"]), float(inp["L1_ffn_s1"]),
                           float(inp["L1_ffn_s2"])]], np.float32),
        "ident_c": ident, "iota_e": iota, "iota8c": iota8, "tril4": tril,
        "selq": sel, "hbq": hb, "onesrow": ones,
        "e8sel": np.ascontiguousarray(
            np.stack([np.where(np.arange(8)[:, None] == e, 1.0, 0.0)
                      .repeat(128, axis=1)[:, :128] for e in range(8)], axis=1)
            .reshape(8, 1024).astype(np.float32)),
    }


def kernel(**inputs):
    inp = {k: np.asarray(v) for k, v in inputs.items()}
    x, shared = _shared_inputs(inp)

    if "nc" not in _NC_CACHE:
        _NC_CACHE["nc"] = build_nc(debug=False)
    nc, dram = _NC_CACHE["nc"]

    in_maps = []
    for b in range(B):
        m = dict(shared)
        m["xo_tm"] = np.ascontiguousarray(x[b])
        m["xo_fm"] = np.ascontiguousarray(x[b].T)
        oh = np.zeros((1, 8), np.float32)
        oh[0, b] = 1.0
        m["coreoh"] = oh
        in_maps.append(m)

    res = run_bass_kernel_spmd(nc, in_maps, core_ids=list(range(8)))
    outs = np.stack([res.results[b]["out"] for b in range(B)], axis=0)
    return outs


# version 11 (final)
# speedup vs baseline: 1.0727x; 1.0598x over previous
import numpy as np
from contextlib import ExitStack

import concourse.bass as bass
import concourse.mybir as mybir
from concourse.bass_utils import run_bass_kernel_spmd

F32 = mybir.dt.float32
F16 = mybir.dt.float16
U8 = mybir.dt.uint8
AF = mybir.ActivationFunctionType
ALU = mybir.AluOpType
AX = mybir.AxisListType

B, T, D = 8, 2048, 1024
NCH, NTI, NTAU = 8, 16, 4
E, H, DH = 8, 16, 64
N = B * T


NSEM = 8


class Q:

    def __init__(self, name, dma=False):
        self.name, self.dma = name, dma
        self.ops, self.n, self.sem = [], 0, None
        if dma:
            self.sems = [None] * NSEM
            self.counts = [0] * NSEM
            self.i = 0

    def wait(self, other, val=None):
        if other.dma:
            if isinstance(val, tuple):
                s, v = val
                self.ops.append(lambda e, o=other, s=s, v=v: e.wait_ge(o.sems[s], v))
            else:
                for s in range(NSEM):
                    v = other.counts[s]
                    if v > 0:
                        self.ops.append(lambda e, o=other, s=s, v=v: e.wait_ge(o.sems[s], v))
        else:
            v = other.n if val is None else val
            if v > 0:
                self.ops.append(lambda e, o=other, vv=v: e.wait_ge(o.sem, vv))

    def op(self, fn, serialize=None):
        if self.dma:
            s = self.i % NSEM
            self.i += 1
            self.counts[s] += 16
            tgt = self.counts[s]

            def run(e, fn=fn, s=s, tgt=tgt, sh=self):
                if tgt > 16:
                    e.wait_ge(sh.sems[s], tgt - 16)
                instr = fn(e)
                instr.then_inc(sh.sems[s], 16)
            self.ops.append(run)
            self.n += 16
            return (s, tgt)
        if serialize is None:
            serialize = self.name in ("vector", "scalar")
        self.n += 1
        tgt = self.n

        def run(e, fn=fn, tgt=tgt, serialize=serialize, sh=self):
            instr = fn(e)
            instr.then_inc(sh.sem, 1)
            if serialize:
                e.wait_ge(sh.sem, tgt)
        self.ops.append(run)
        return tgt

    def emit(self, engine):
        for f in self.ops:
            f(engine)


def build_nc(debug=False):
    nc = bass.Bass(trn_type="TRN2", target_bir_lowering=False)
    dram = {}

    def din(name, shape, dt=F32):
        dram[name] = nc.dram_tensor(name, list(shape), dt, kind="ExternalInput").ap()
        return dram[name]

    def dscr(name, shape, dt=F32):
        kind = "ExternalOutput" if debug else "Internal"
        dram[name] = nc.dram_tensor(name, list(shape), dt, kind=kind).ap()
        return dram[name]

    x_tm = din("x_tm", (B, T, D))
    x_fm = din("x_fm", (B, D, T))
    xo_tm = din("xo_tm", (T, D))
    xo_fm = din("xo_fm", (D, T))
    WinT = din("WinT", (D, D)); WigT = din("WigT", (D, D))
    WrgT = din("WrgT", (D, D)); WoutT = din("WoutT", (D, D))
    Wr_fm = din("Wr_fm", (128, 64))
    br_row = din("br_row", (1, E))
    big_col = din("big_col", (128, 8)); brg_col = din("brg_col", (128, 8))
    lam_col = din("lam_col", (128, 8)); h0_col = din("h0_col", (128, 8))
    nw_col = din("nw_col", (128, 8))
    W1moe = din("W1moe", (E, D, D)); W2moe = din("W2moe", (E, D, D))
    b1_col = din("b1_col", (E, 128, 8))
    b2stat = din("b2stat", (E, 8, 128))
    Wq_i = din("Wq", (D, D)); Wk_i = din("Wk", (D, D)); Wv_i = din("Wv", (D, D))
    Wo_i = din("Wo", (D, D)); Wb_i = din("Wb", (D, H))
    bb_col = din("bb_col", (16, 1))
    k0r = din("k0r", (32, 128, 16, 128), U8)
    k1r = din("k1r", (32, 128, 32, 128), U8)
    k2r = din("k2r", (8, 128, 32, 128), U8)
    s_sc = din("s_sc", (1, 3))
    ident_c = din("ident_c", (128, 128))
    iota_e = din("iota_e", (128, 128))
    iota8c = din("iota8c", (8, 1))
    tril4 = din("tril4", (4, 128, 512))
    selq = din("selq", (NCH, 128, 16))
    hbq = din("hbq", (NCH, 16, 128))
    onesrow = din("onesrow", (1, 128))
    e8sel = din("e8sel", (8, 1024))
    coreoh = din("coreoh", (1, 8))

    out = nc.dram_tensor("out", [T, D], F32, kind="ExternalOutput").ap()
    dram["out"] = out

    xnT_s = dscr("xnT_s", (D, T))
    g0_s = dscr("g0_s", (D, T))
    s0_s = dscr("s0_s", (D, T))
    x1_s = dscr("x1_s", (D, T))
    xn1_s = dscr("xn1_s", (D, T))
    qn_s = dscr("qn_s", (D, T))
    kb_s = dscr("kb_s", (D, T))
    vT_s = dscr("vT_s", (T, D))
    ao_s = dscr("ao_s", (D, T))
    s1_s = dscr("s1_s", (D, T))
    wem_s = dscr("wem_s", (8, T))
    kept_s = dscr("kept_s", (8, N))

    qT, qV, qS = Q("tensor"), Q("vector"), Q("scalar")
    qD, qG = Q("sync", dma=True), Q("gpsimd", dma=True)
    ALLQ = (qT, qV, qS, qD, qG)

    es = ExitStack()
    with es:
        A = es.enter_context(nc.sbuf_tensor("A", [128, 16384], F32))
        Bp = es.enter_context(nc.sbuf_tensor("Bp", [128, 16384], F32))
        C = es.enter_context(nc.sbuf_tensor("C", [128, 8192], F32))
        cst = es.enter_context(nc.sbuf_tensor("cst", [128, 1024], F32))
        rowp = es.enter_context(nc.sbuf_tensor("rowp", [64, 6400], F32))
        wem = es.enter_context(nc.sbuf_tensor("wem", [8, T], F32))
        rsb = es.enter_context(nc.sbuf_tensor("rsb", [128, 512], F32))
        sc0 = es.enter_context(nc.sbuf_tensor("sc0", [128, 512], F32))
        sc1 = es.enter_context(nc.sbuf_tensor("sc1", [128, 512], F32))
        stats = es.enter_context(nc.sbuf_tensor("stats", [128, 256], F32))
        gates = es.enter_context(nc.sbuf_tensor("gates", [128, 152], F32))
        colb = es.enter_context(nc.sbuf_tensor("colb", [128, 64], F32))
        ltm = es.enter_context(nc.sbuf_tensor("ltm", [128, 352], F32))
        f16c = es.enter_context(nc.sbuf_tensor("f16c", [128, 256], F16))

        p = [es.enter_context(nc.psum_tensor("p%d" % i, [128, 512], F32))
             for i in range(8)]

        for q in ALLQ:
            if q.dma:
                q.sems = [es.enter_context(nc.semaphore("sem_%s_%d" % (q.name, i)))
                          for i in range(NSEM)]
            else:
                q.sem = es.enter_context(nc.semaphore("sem_" + q.name))

        sc2 = C[:, 7168:7680]
        A16 = A[:].bitcast(F16)
        B16 = Bp[:].bitcast(F16)
        C16 = C[:].bitcast(F16)

        emp = A[0:8, 0:16384]
        emp2 = Bp[0:8, 0:16384]

        def mm(ps, stat, mov, start, stop, tp=None):
            return qT.op(lambda t, a=ps, b=stat, c=mov, s1=start, s2=stop, tp=tp:
                         t.matmul(a, b, c, start=s1, stop=s2, tile_position=tp),
                         serialize=False)

        def tr(ps, src, idn=None):
            return qT.op(lambda t, a=ps, b=src, i=idn: t.transpose(
                a, b, i if i is not None else IDENT), serialize=False)

        def ld(dst, src, q=None):
            q = q or qD
            if not isinstance(dst, bass.AP):
                dst = dst[:]
            if not isinstance(src, bass.AP):
                src = src[:]
            return q.op(lambda e, d=dst, s=src: e.dma_start(out=d, in_=s),
                        serialize=False)

        def vop(fn):
            return qV.op(fn)

        def sop(fn):
            return qS.op(fn)

        def ch(pool, c, w=T):
            return pool[:, c * w:(c + 1) * w]

        def barrier():
            for q in ALLQ:
                for o in ALLQ:
                    if o is not q:
                        q.wait(o)

        ld(cst[:, 0:128], ident_c)
        ld(cst[:, 128:256], iota_e)
        for c in range(NCH):
            ld(cst[:, 256 + c * 16:256 + (c + 1) * 16], selq[c])
        ld(cst[0:1, 384:512], onesrow)
        ld(cst[:, 512:576], Wr_fm)
        ld(cst[0:8, 648:649], iota8c)
        ld(cst[0:1, 656:664], br_row)
        ld(cst[0:1, 664:672], coreoh)
        ld(cst[0:1, 672:675], s_sc)
        ld(colb[:, 0:8], big_col); ld(colb[:, 8:16], brg_col)
        ld(colb[:, 16:24], lam_col); ld(colb[:, 24:32], h0_col)
        ld(colb[:, 32:40], nw_col); ld(colb[0:16, 40:41], bb_col)
        n_cload, n_gload = qD.n, qG.n

        IDENT = cst[:, 0:128]
        IOTA3 = cst[:, 128:256].rearrange("p (i c) -> p i c", c=8)
        SEL = lambda c: cst[:, 256 + c * 16:256 + (c + 1) * 16]
        ONESROW = cst[0:1, 384:512]
        WRC = lambda k: cst[:, 512 + k * 8:512 + (k + 1) * 8]
        IOTA8C = cst[0:8, 648:649]
        BRROW = cst[0:1, 656:664]
        COREOH = cst[0:1, 664:672]
        SSC = cst[0:1, 672:675]
        TRIL = lambda j: C16[:, j * 512:(j + 1) * 512]

        qV.wait(qD)
        vop(lambda v: v.memset(cst[:, 676:677], 1.0))
        ONESCOL = cst[:, 676:677]
        vop(lambda v: v.memset(cst[:, 704:776], 1.0))
        for k in range(NCH):
            vop(lambda v, k=k: v.tensor_copy(cst[:, 704 + k * 9:704 + k * 9 + 8], WRC(k)))
        WRONES = lambda k: cst[:, 704 + k * 9:704 + k * 9 + 9]
        vop(lambda v: v.memset(f16c[0:1, 0:128], 1.0))
        vop(lambda v: v.memset(f16c[:, 128:129], -0.5))
        ONESROW16 = f16c[0:1, 0:128]
        NEGH16 = f16c[:, 128:129]
        n_vconst = qV.n

        qT.wait(qD)
        qT.wait(qV, n_vconst)
        mm(p[0][0:8, 0:8], ONESROW[0:1, 0:8], COREOH, True, True)
        mm(p[0][:, 8:16], ONESROW, COREOH, True, True)
        mm(p[0][:, 16:19], ONESROW, SSC, True, True)
        qV.wait(qT, qT.n)
        vop(lambda v: v.tensor_copy(gates[0:8, 128:136], p[0][0:8, 0:8]))
        vop(lambda v: v.tensor_copy(gates[:, 136:144], p[0][:, 8:16]))
        vop(lambda v: v.tensor_copy(colb[:, 44:47], p[0][:, 16:19]))
        COREOH8 = gates[0:8, 128:136]
        COREOHB = gates[:, 136:144]
        S0C, S1C, S2C = colb[:, 44:45], colb[:, 45:46], colb[:, 46:47]
        vop(lambda v: v.tensor_scalar(colb[:, 47:48], S0C, 8.0, None, ALU.mult, ALU.bypass))
        vop(lambda v: v.tensor_scalar(colb[:, 48:49], S1C, 4.0, None, ALU.mult, ALU.bypass))
        S0C8, S1C4 = colb[:, 47:48], colb[:, 48:49]
        vop(lambda v: v.memset(colb[:, 49:50], 1e-5))
        vop(lambda v: v.memset(colb[:, 50:51], 1e-6))
        vop(lambda v: v.memset(colb[:, 51:52], 1.0 + 1e-6))
        vop(lambda v: v.memset(colb[:, 52:53], -1.0))
        EPS5, EPS6, GEPS, NEG1 = (colb[:, 49:50], colb[:, 50:51],
                                  colb[:, 51:52], colb[:, 52:53])
        qS.wait(qD)
        sop(lambda s: s.activation(colb[:, 56:64], colb[:, 16:24], AF.Sigmoid))
        qV.wait(qS, qS.n)
        vop(lambda v: v.tensor_scalar(colb[:, 56:64], colb[:, 56:64], 1e-8, None, ALU.add, ALU.bypass))
        qS.wait(qV, qV.n)
        sop(lambda s: s.activation(colb[:, 56:64], colb[:, 56:64], AF.Ln))
        qV.wait(qS, qS.n)
        vop(lambda v: v.tensor_scalar(colb[:, 56:64], colb[:, 56:64], 8.0, None, ALU.mult, ALU.bypass))
        CLOGA = colb[:, 56:64]
        qT.wait(qV, qV.n)

        tm0, tm1 = C[:, 0:1024], C[:, 1024:2048]
        SINK = C[:, 2048:3072]
        TMP3 = C[:, 3072:3200].rearrange("p (i c) -> p i c", c=8)
        DIFF3 = C[:, 3200:3328].rearrange("p (i c) -> p i c", c=8)
        MB3 = C[:, 3328:3456].rearrange("p (i c) -> p i c", c=8)
        IDXROW = rowp[0:1, 2304:2304 + T]
        GATEROW = rowp[0:1, 2304:2304 + T]

        sq_tok = {}
        last_sq = [0, 0]
        for b in range(B):
            for i in range(NTI):
                gi = b * NTI + i
                buf = tm0 if gi % 2 == 0 else tm1
                if last_sq[gi % 2] > 0:
                    qG.wait(qS, last_sq[gi % 2])
                t_ld = ld(buf, x_tm[b, i * 128:(i + 1) * 128, :], q=qG)
                qS.wait(qG, t_ld)
                sq_tok[b] = last_sq[gi % 2] = sop(lambda s, bu=buf, b=b, i=i: s.activation(
                    SINK, bu, AF.Square, accum_out=stats[:, b * 16 + i:b * 16 + i + 1]))
        LRAWp = (rowp[0:9, 0:2048], rowp[32:41, 0:2048])
        LTMp = (ltm[:, 0:176], ltm[:, 176:352])
        P_LOG = (p[0], p[4])
        P_TR = (p[1], p[5])
        P_BC = (p[2], p[6])
        DIFFp = (C[:, 3072:3200], C[:, 3456:3584])
        MBp = (C[:, 3200:3328], C[:, 3584:3712])
        TMPp = (C[:, 3328:3456], C[:, 3712:3840])
        IDXRp = (rowp[0:1, 2304:2304 + T], rowp[0:1, 4352:4352 + T])
        FLATp = (rowp[0:16, 2048:2176], rowp[32:48, 2048:2176])
        SBASE = (128, 192)
        log_evac = [0, 0]
        tr_evac = {}
        ltm_free = [0, 0]
        mm_log = [0, 0]
        memq_tok = [0, 0]
        for b in range(B):
            par = b % 2
            LRAW = LRAWp[par]
            L3 = LTMp[par].rearrange("p (i c) -> p i c", c=11)
            pL, pT_, pBc = P_LOG[par], P_TR[par], P_BC[par]
            S0 = SBASE[par]
            TMP3 = TMPp[par].rearrange("p (i c) -> p i c", c=8)
            DIFF3 = DIFFp[par].rearrange("p (i c) -> p i c", c=8)
            MB3 = MBp[par].rearrange("p (i c) -> p i c", c=8)
            for tau in range(NTAU):
                for k in range(NCH):
                    buf = sc0 if k % 2 == 0 else sc1
                    nmm = qT.n
                    if b + tau + k > 0:
                        qD.wait(qT, nmm - 1 if nmm >= 2 else 0)
                    lk = ld(buf, x_fm[b, k * 128:(k + 1) * 128, tau * 512:(tau + 1) * 512])
                    qT.wait(qD, lk)
                    if k == 0 and log_evac[par] > 0:
                        qT.wait(qV, log_evac[par])
                    mm(pL[0:9, 0:512], WRONES(k), buf[:], k == 0, k == NCH - 1)
                qV.wait(qT, qT.n)
                log_evac[par] = vop(lambda v, tau=tau, LRAW=LRAW, pL=pL: v.tensor_copy(
                    LRAW[:, tau * 512:(tau + 1) * 512], pL[0:9, 0:512]))
            if b == 0:
                qT.wait(qV, qV.n)
                for k in range(NCH):
                    mm(p[3][0:1, 0:8], ONESCOL, WRC(k), k == 0, k == NCH - 1)
                qV.wait(qT, qT.n)
                vop(lambda v: v.tensor_copy(cst[0:1, 680:688], p[3][0:1, 0:8]))
                qT.wait(qV, qV.n)
                mm(p[3][:, 0:8], ONESROW, cst[0:1, 680:688], True, True)
                mm(p[3][:, 8:16], ONESROW, BRROW, True, True)
                qV.wait(qT, qT.n)
                vop(lambda v: v.tensor_copy(cst[:, 688:696], p[3][:, 0:8]))
                vop(lambda v: v.tensor_copy(cst[:, 696:704], p[3][:, 8:16]))
                qT.wait(qV, qV.n)
            COLSB = cst[:, 688:696].unsqueeze(1).broadcast_to((128, 16, 8))
            BRB = cst[:, 696:704].unsqueeze(1).broadcast_to((128, 16, 8))
            for g4 in range(4):
                ptr = pT_ if g4 % 2 == 0 else p[7]
                pid = (par, 0) if g4 % 2 == 0 else ("g7",)
                if tr_evac.get(pid, 0) > 0:
                    qT.wait(qV, tr_evac[pid])
                if g4 == 0:
                    qT.wait(qV, log_evac[par])
                    if ltm_free[par] > 0:
                        qT.wait(qV, ltm_free[par])
                t1 = 0
                for jj in range(4):
                    i = g4 * 4 + jj
                    t1 = tr(ptr[0:128, jj * 9:(jj + 1) * 9], LRAW[:, i * 128:(i + 1) * 128],
                            IDENT[0:9, 0:9] if par == 0 else IDENT[32:41, 32:41])
                qV.wait(qT, t1)
                tr_evac[pid] = vop(lambda v, g4=g4, ptr=ptr, L3=L3: v.tensor_copy(
                    L3[:, g4 * 4:(g4 + 1) * 4, 0:9],
                    ptr[:, 0:36].rearrange("p (a b) -> p a b", b=9)))
            LT = L3[:, :, 0:8]
            SUMR = L3[:, :, 8:9]
            qV.wait(qS, sq_tok[b])
            vop(lambda v, S0=S0, SUMR=SUMR: v.tensor_scalar(
                stats[:, S0:S0 + 16], SUMR.rearrange("p i c -> p (i c)"), 1.0 / D, None, ALU.mult, ALU.bypass))
            vop(lambda v, S0=S0, b=b: v.tensor_scalar(
                stats[:, S0 + 16:S0 + 32], stats[:, b * 16:(b + 1) * 16], 1.0 / D, None, ALU.mult, ALU.bypass))
            vop(lambda v, S0=S0: v.tensor_tensor(
                stats[:, S0 + 32:S0 + 48], stats[:, S0:S0 + 16], stats[:, S0:S0 + 16], ALU.mult))
            vv = vop(lambda v, S0=S0: v.tensor_tensor(
                stats[:, S0 + 16:S0 + 32], stats[:, S0 + 16:S0 + 32], stats[:, S0 + 32:S0 + 48], ALU.subtract))
            qS.wait(qV, vv)
            ss1 = sop(lambda s, S0=S0: s.activation(
                stats[:, S0 + 32:S0 + 48], stats[:, S0 + 16:S0 + 32], AF.Sqrt, bias=EPS5, scale=1.0))
            qV.wait(qS, ss1)
            vop(lambda v, S0=S0: v.reciprocal(stats[:, S0 + 48:S0 + 64], stats[:, S0 + 32:S0 + 48]))
            ISTD3 = stats[:, S0 + 48:S0 + 64].unsqueeze(2).broadcast_to((128, 16, 8))
            vop(lambda v, S0=S0: v.tensor_tensor(
                stats[:, S0:S0 + 16], stats[:, S0:S0 + 16], stats[:, S0 + 48:S0 + 64], ALU.mult))
            MISTD3 = stats[:, S0:S0 + 16].unsqueeze(2).broadcast_to((128, 16, 8))
            vop(lambda v, LT=LT, ISTD3=ISTD3: v.tensor_tensor(LT, LT, ISTD3, ALU.mult))
            vop(lambda v, TMP3=TMP3, MISTD3=MISTD3: v.tensor_copy(TMP3, MISTD3))
            vop(lambda v, TMP3=TMP3, COLSB=COLSB: v.tensor_tensor(TMP3, TMP3, COLSB, ALU.mult))
            vop(lambda v, LT=LT, TMP3=TMP3: v.tensor_tensor(LT, LT, TMP3, ALU.subtract))
            vop(lambda v, LT=LT, BRB=BRB: v.tensor_tensor(LT, LT, BRB, ALU.add))
            NL = L3[:, :, 9:10]
            vop(lambda v, NL=NL, LT=LT: v.tensor_reduce(NL, LT, axis=AX.X, op=ALU.max, negate=True))
            dv = vop(lambda v, DIFF3=DIFF3, LT=LT, NL=NL: v.tensor_tensor(
                DIFF3, LT, NL.broadcast_to((128, 16, 8)), ALU.add))
            vop(lambda v, MB3=MB3, DIFF3=DIFF3: v.tensor_scalar(MB3, DIFF3, 0.0, None, ALU.is_lt, ALU.bypass))
            vop(lambda v, MB3=MB3: v.tensor_scalar(MB3, MB3, 1024.0, None, ALU.mult, ALU.bypass))
            vop(lambda v, MB3=MB3: v.tensor_tensor(MB3, MB3, IOTA3, ALU.add))
            vop(lambda v, L3=L3, MB3=MB3: v.tensor_reduce(L3[:, :, 10:11], MB3, axis=AX.X, op=ALU.min))
            qS.wait(qV, qV.n)
            se = sop(lambda s, MB3=MB3, DIFF3=DIFF3: s.activation(MB3, DIFF3, AF.Exp))
            qV.wait(qS, se)
            vop(lambda v, L3=L3, MB3=MB3: v.tensor_reduce(L3[:, :, 9:10], MB3, axis=AX.X, op=ALU.add))
            vop(lambda v, b=b, L3=L3: v.reciprocal(gates[:, b * 16:(b + 1) * 16],
                                                   L3[:, :, 9:10].rearrange("p i c -> p (i c)")))
            ltm_free[par] = vop(lambda v, S0=S0, L3=L3: v.tensor_copy(
                stats[:, S0 + 16:S0 + 32], L3[:, :, 10:11].rearrange("p i c -> p (i c)")))
            qT.wait(qV, ltm_free[par])
            if tr_evac.get((par, 0), 0) > 0:
                qT.wait(qV, tr_evac[(par, 0)])
            t2 = tr(pT_[0:16, 0:128], stats[:, S0 + 16:S0 + 32])
            tr_evac[(par, 0)] = None
            qV.wait(qT, t2)
            fe = vop(lambda v, par=par, pT_=pT_: v.tensor_copy(FLATp[par], pT_[0:16, 0:128]))
            tr_evac[(par, 0)] = fe
            qG.wait(qV, fe)
            fl = ld(IDXRp[par], FLATp[par], q=qG)
            qT.wait(qG, fl)
            for tau in range(NTAU):
                pb2 = pBc if tau % 2 == 0 else p[3]
                key = (par, "m") if tau % 2 == 0 else ("m3",)
                if tr_evac.get(key, 0) > 0:
                    qT.wait(qV, tr_evac[key])
                mmv = mm(pb2[0:8, 0:512], ONESROW[0:1, 0:8],
                         IDXRp[par][0:1, tau * 512:(tau + 1) * 512], True, True)
                qV.wait(qT, mmv)
                tr_evac[key] = memq_tok[par] = vop(lambda v, b=b, tau=tau, pb2=pb2: v.tensor_scalar(
                    emp[:, b * T + tau * 512:b * T + (tau + 1) * 512],
                    pb2[0:8, 0:512], IOTA8C, None, ALU.is_equal, ALU.bypass))
        qD.wait(qV, qV.n)
        qG.wait(qV, qV.n)

        for j in range(4):
            init = 0.0 if j == 0 else emp2[:, j * 4096 - 1:j * 4096]
            vop(lambda v, j=j, init=init: v.tensor_tensor_scan(
                emp2[:, j * 4096:(j + 1) * 4096],
                ONESCOL[0:8, 0:1].broadcast_to((8, 4096)),
                emp[:, j * 4096:(j + 1) * 4096], init, ALU.mult, ALU.add))
        vop(lambda v: v.tensor_tensor(emp2[:], emp2[:], emp[:], ALU.mult))
        vop(lambda v: v.tensor_scalar(emp2[:], emp2[:], 2048.0, None, ALU.is_lt, ALU.bypass))
        vop(lambda v: v.tensor_tensor(emp2[:], emp2[:], emp[:], ALU.mult))
        if debug:
            qG.wait(qV, qV.n)
            ld(kept_s, emp2[:], q=qG)
        vop(lambda v: v.memset(stats[:, 0:16], 0.0))
        for b in range(B):
            vop(lambda v, b=b: v.tensor_scalar(stats[:, 16:32], gates[:, b * 16:(b + 1) * 16],
                                               COREOHB[:, b:b + 1], None, ALU.mult, ALU.bypass))
            vop(lambda v: v.tensor_tensor(stats[:, 0:16], stats[:, 0:16], stats[:, 16:32], ALU.add))
        qT.wait(qV, qV.n)
        t3 = tr(p[1][0:16, 0:128], stats[:, 0:16])
        qV.wait(qT, t3)
        vop(lambda v: v.tensor_copy(rowp[0:16, 2048:2176], p[1][0:16, 0:128]))
        qG.wait(qV, qV.n)
        fl2 = ld(GATEROW, rowp[0:16, 2048:2176], q=qG)
        vop(lambda v: v.memset(wem[:], 0.0))
        for b in range(B):
            vop(lambda v, b=b: v.tensor_scalar(emp[:, 0:T], emp2[:, b * T:(b + 1) * T],
                                               COREOH8[:, b:b + 1], None, ALU.mult, ALU.bypass))
            vop(lambda v: v.tensor_tensor(wem[:], wem[:], emp[:, 0:T], ALU.add))
        qT.wait(qG, fl2)
        qT.wait(qV, qV.n)
        for tau in range(NTAU):
            mmg = mm(p[2][0:8, 0:512], ONESROW[0:1, 0:8],
                     GATEROW[0:1, tau * 512:(tau + 1) * 512], True, True)
            qV.wait(qT, mmg)
            vop(lambda v, tau=tau: v.tensor_tensor(
                wem[:, tau * 512:(tau + 1) * 512], wem[:, tau * 512:(tau + 1) * 512],
                p[2][0:8, 0:512], ALU.mult))
            qT.wait(qV, qV.n)
        if debug:
            qG.wait(qV, qV.n)
            ld(wem_s, wem[:], q=qG)

        barrier()
        s_base = qS.n
        for i in range(NTI):
            buf = tm0 if i % 2 == 0 else tm1
            if i >= 2:
                qD.wait(qS, s_base + 2 * (i - 1))
            t_ld = ld(buf, xo_tm[i * 128:(i + 1) * 128, :])
            qS.wait(qD, t_ld)
            sop(lambda s, bu=buf, i=i: s.activation(
                SINK, bu, AF.Square, accum_out=stats[:, i:i + 1]))
            sop(lambda s, bu=buf, i=i: s.activation(
                SINK, bu, AF.Identity, accum_out=stats[:, 16 + i:17 + i]))
        qV.wait(qS, qS.n)
        vop(lambda v: v.tensor_scalar(stats[:, 16:32], stats[:, 16:32], 1.0 / D, None, ALU.mult, ALU.bypass))
        vop(lambda v: v.tensor_scalar(stats[:, 32:48], stats[:, 0:16], 1.0 / D, None, ALU.mult, ALU.bypass))
        vop(lambda v: v.tensor_tensor(stats[:, 48:64], stats[:, 16:32], stats[:, 16:32], ALU.mult))
        vop(lambda v: v.tensor_tensor(stats[:, 32:48], stats[:, 32:48], stats[:, 48:64], ALU.subtract))
        qS.wait(qV, qV.n)
        sop(lambda s: s.activation(stats[:, 48:64], stats[:, 32:48], AF.Sqrt, bias=EPS5, scale=1.0))
        qV.wait(qS, qS.n)
        vop(lambda v: v.reciprocal(stats[:, 0:16], stats[:, 48:64]))
        qT.wait(qV, qV.n)
        t4 = tr(p[1][0:32, 0:128], stats[:, 0:32])
        qV.wait(qT, t4)
        vop(lambda v: v.tensor_copy(rowp[0:32, 2048:2176], p[1][0:32, 0:128]))
        qG.wait(qV, qV.n)
        fl2a = ld(rowp[0:1, 0:T], rowp[0:16, 2048:2176], q=qG)
        fl3 = ld(rowp[0:1, 4352:4352 + T], rowp[16:32, 2048:2176], q=qG)
        ISTDR, MR = rowp[0:1, 0:T], rowp[0:1, 4352:4352 + T]
        qT.wait(qG, fl2a)
        qT.wait(qG, fl3)
        npair = ((p[1], p[2]), (p[5], p[6]))
        ntok = [0, 0]
        nbuf = (sc0[:], sc1[:])
        for c in range(NCH):
            for tau in range(NTAU):
                bi = (c * NTAU + tau) % 2
                pa_, pb_ = npair[bi]
                if ntok[bi] > 0:
                    qT.wait(qV, ntok[bi])
                    qD.wait(qV, ntok[bi])
                mm(pa_[:, 0:512], ONESROW, MR[0:1, tau * 512:(tau + 1) * 512], True, True)
                m2 = mm(pb_[:, 0:512], ONESROW, ISTDR[0:1, tau * 512:(tau + 1) * 512], True, True)
                lk = ld(nbuf[bi], xo_fm[c * 128:(c + 1) * 128, tau * 512:(tau + 1) * 512])
                qV.wait(qT, m2)
                qV.wait(qD, lk)
                dst = ch(A, c)[:, tau * 512:(tau + 1) * 512]
                vop(lambda v, dst=dst, buf=nbuf[bi], pa_=pa_: v.tensor_tensor(
                    dst, buf, pa_[:, 0:512], ALU.subtract))
                ntok[bi] = vop(lambda v, dst=dst, pb_=pb_: v.tensor_tensor(
                    dst, dst, pb_[:, 0:512], ALU.mult))
            qG.wait(qV, qV.n)
            ld(xnT_s[c * 128:(c + 1) * 128, :], ch(A, c), q=qG)
        qT.wait(qV, qV.n)
        qD.wait(qV, qV.n)

        barrier()
        ld(rowp[0:8, 2304:3328], b2stat.rearrange("e c f -> e (c f)"), q=qD)
        B2S = lambda c: rowp[0:8, 2304 + c * 128:2304 + (c + 1) * 128]
        WBp = (C[:, 2048:3072], C[:, 3072:4096])
        wbuf = (C[:, 0:1024], C[:, 1024:2048])
        wbuf4 = (C[:, 0:1024], C[:, 1024:2048], C[:, 4096:5120], C[:, 5120:6144])
        wslot, wtok = [0], [0, 0, 0, 0]
        ptok = [0] * 8
        ld(rowp[0:8, 3328:4352], e8sel, q=qD)
        ld(stats[:, 128:192].rearrange("p (e c) -> p e c", c=8), b1_col.rearrange("e p c -> p e c"), q=qD)
        E8S = lambda e: rowp[0:8, 3328 + e * 128:3328 + (e + 1) * 128]
        qT.wait(qD)
        qS.wait(qD)
        wb_done = {}
        def make_wb(half, e):
            hof2 = half * 1024
            ep2 = e % 2
            for t2 in range(2):
                if ptok[0] > 0:
                    qT.wait(qV, ptok[0])
                m3 = mm(p[0][:, 0:512], E8S(e),
                        wem[0:8, hof2 + t2 * 512:hof2 + (t2 + 1) * 512], True, True)
                qV.wait(qT, m3)
                ptok[0] = vop(lambda v, t2=t2, ep2=ep2: v.tensor_copy(
                    WBp[ep2][:, t2 * 512:(t2 + 1) * 512], p[0][:, 0:512]))
            wb_done[(half, e)] = ptok[0]
        for half in range(2):
            hof = half * 1024
            qV.wait(qG)
            qS.wait(qG)
            for e in range(E):
                ep = e % 2
                if (half, e) not in wb_done:
                    make_wb(half, e)
                B1C = stats[:, 128 + e * 8:128 + (e + 1) * 8]
                WB0 = WBp[ep][:, 0:512]
                WB1 = WBp[ep][:, 512:1024]
                if e + 1 < E:
                    make_wb(half, e + 1)
                for li, Wm in enumerate((W1moe, W2moe)):
                    if li == 1:
                        qT.wait(qV, qV.n)
                        qT.wait(qS, qS.n)
                    for t2 in range(2):
                        tslc = slice(hof + t2 * 512, hof + (t2 + 1) * 512)
                        for k in range(NCH):
                            slot = wslot[0] % 4
                            wslot[0] += 1
                            wb = wbuf4[slot]
                            if wtok[slot] > 0:
                                qD.wait(qT, wtok[slot])
                            lw = ld(wb, Wm[e, k * 128:(k + 1) * 128, :])
                            qT.wait(qD, lw)
                            for f in range(NCH):
                                src = (ch(A, k)[:, tslc] if li == 0
                                       else Bp[:, k * 1024 + t2 * 512:k * 1024 + (t2 + 1) * 512])
                                if k == 0 and ptok[f] > 0:
                                    qT.wait(qV, ptok[f])
                                wtok[slot] = mm(p[f][:, 0:512], wb[:, f * 128:(f + 1) * 128], src,
                                                k == 0, k == NCH - 1)
                        qV.wait(qT, qT.n)
                        for f in range(NCH):
                            if li == 0:
                                dsth = Bp[:, f * 1024 + t2 * 512:f * 1024 + (t2 + 1) * 512]
                                ptok[f] = vop(lambda v, f=f, dsth=dsth, wb2=(WB0 if t2 == 0 else WB1):
                                              v.tensor_tensor(dsth, p[f][:, 0:512], wb2, ALU.mult))
                                qS.wait(qV, ptok[f])
                                stok = sop(lambda s, dsth=dsth, f=f, B1C=B1C: s.activation(
                                    dsth, dsth, AF.Relu, bias=B1C[:, f:f + 1], scale=1.0))
                                qV.wait(qS, stok)
                                vop(lambda v, dsth=dsth, wb2=(WB0 if t2 == 0 else WB1):
                                    v.tensor_tensor(dsth, dsth, wb2, ALU.mult))
                            else:
                                ydst = Bp[:, 8192 + f * 1024 + t2 * 512:8192 + f * 1024 + (t2 + 1) * 512]
                                if e == 0:
                                    ptok[f] = vop(lambda v, ydst=ydst, f=f: v.tensor_copy(ydst, p[f][:, 0:512]))
                                else:
                                    ptok[f] = vop(lambda v, ydst=ydst, f=f: v.tensor_tensor(
                                        ydst, ydst, p[f][:, 0:512], ALU.add))
            qT.wait(qV, qV.n)
            for f in range(NCH):
                for t2 in range(2):
                    if ptok[0] > 0:
                        qT.wait(qV, ptok[0])
                    m4 = mm(p[0][:, 0:512], B2S(f),
                            wem[0:8, hof + t2 * 512:hof + (t2 + 1) * 512], True, True)
                    qV.wait(qT, m4)
                    ydst = Bp[:, 8192 + f * 1024 + t2 * 512:8192 + f * 1024 + (t2 + 1) * 512]
                    ptok[0] = vop(lambda v, ydst=ydst: v.tensor_tensor(ydst, ydst, p[0][:, 0:512], ALU.add))
                qG.wait(qV, qV.n)
                ld(g0_s[f * 128:(f + 1) * 128, hof:hof + 1024],
                   Bp[:, 8192 + f * 1024:8192 + (f + 1) * 1024], q=qG)

        barrier()
        WPAN = C[:, 0:3072]
        XP = C[:, 3072:5120]
        R_ = C[:, 5120:7168]
        qD.wait(qG)
        for c in range(NCH):
            nw = []
            for oi, WT in enumerate((WinT, WigT, WrgT)):
                nw.append(ld(WPAN[:, oi * 1024:(oi + 1) * 1024].rearrange("p (k f) -> p k f", f=128),
                             WT[:, c * 128:(c + 1) * 128].rearrange("(k p) f -> p k f", p=128)))
            qT.wait(qD)
            I_ = ch(Bp, c)
            for tau in range(NTAU):
                for k in range(NCH):
                    for oi in range(3):
                        mm(p[oi][:, 0:512], WPAN[:, oi * 1024 + k * 128:oi * 1024 + (k + 1) * 128],
                           ch(A, k)[:, tau * 512:(tau + 1) * 512], k == 0, k == NCH - 1)
                qV.wait(qT, qT.n)
                qS.wait(qT, qT.n)
                sl = slice(tau * 512, (tau + 1) * 512)
                vop(lambda v, sl=sl: v.tensor_copy(XP[:, sl], p[0][:, 0:512]))
                sop(lambda s, sl=sl, c=c: s.activation(I_[:, sl], p[1][:, 0:512], AF.Sigmoid,
                                                       bias=colb[:, 0 + c:1 + c], scale=1.0))
                sop(lambda s, sl=sl, c=c: s.activation(R_[:, sl], p[2][:, 0:512], AF.Sigmoid,
                                                       bias=colb[:, 8 + c:9 + c], scale=1.0))
                qT.wait(qV, qV.n)
                qT.wait(qS, qS.n)
            qV.wait(qS, qS.n)
            vop(lambda v, c=c: v.tensor_scalar(R_[:], R_[:], CLOGA[:, c:c + 1], None, ALU.mult, ALU.bypass))
            qS.wait(qV, qV.n)
            sop(lambda s: s.activation(R_[:], R_[:], AF.Exp))
            qV.wait(qS, qS.n)
            vop(lambda v: v.tensor_tensor(XP[:], XP[:], I_[:], ALU.mult))
            vop(lambda v: v.tensor_tensor(I_[:], R_[:], R_[:], ALU.mult))
            qS.wait(qV, qV.n)
            sop(lambda s: s.activation(I_[:], I_[:], AF.Sqrt, bias=GEPS, scale=NEG1))
            qV.wait(qS, qS.n)
            vop(lambda v: v.tensor_tensor(XP[:], XP[:], I_[:], ALU.mult))
            vop(lambda v, c=c: v.tensor_tensor_scan(
                ch(Bp, c), R_[:], XP[:], colb[:, 24 + c:25 + c], ALU.mult, ALU.add))
            qD.wait(qV, qV.n)
        for c in range(NCH):
            vop(lambda v, c=c: v.tensor_tensor(ch(A, c), ch(Bp, c), ch(Bp, c), ALU.mult))
        qT.wait(qV, qV.n)
        for tau in range(NTAU):
            for c in range(NCH):
                mm(p[3][0:1, 0:512], ONESCOL, ch(A, c)[:, tau * 512:(tau + 1) * 512],
                   c == 0, c == NCH - 1)
            qV.wait(qT, qT.n)
            vop(lambda v, tau=tau: v.tensor_copy(rowp[0:1, tau * 512:(tau + 1) * 512], p[3][0:1, 0:512]))
            qT.wait(qV, qV.n)
        vop(lambda v: v.tensor_scalar(rowp[0:1, 0:T], rowp[0:1, 0:T], 1.0 / D, None, ALU.mult, ALU.bypass))
        qS.wait(qV, qV.n)
        sop(lambda s: s.activation(rowp[0:1, 0:T], rowp[0:1, 0:T], AF.Sqrt, bias=EPS6[0:1, :], scale=1.0))
        qV.wait(qS, qS.n)
        vop(lambda v: v.reciprocal(rowp[0:1, 2304:2304 + T], rowp[0:1, 0:T]))
        RMSR = rowp[0:1, 2304:2304 + T]
        qT.wait(qV, qV.n)
        rtok = [0, 0]
        for c in range(NCH):
            for tau in range(NTAU):
                bi = (c * NTAU + tau) % 2
                pb = p[4] if bi == 0 else p[6]
                if rtok[bi] > 0:
                    qT.wait(qV, rtok[bi])
                m5 = mm(pb[:, 0:512], ONESROW, RMSR[0:1, tau * 512:(tau + 1) * 512], True, True)
                qV.wait(qT, m5)
                dst = ch(A, c)[:, tau * 512:(tau + 1) * 512]
                rtok[bi] = vop(lambda v, dst=dst, c=c, tau=tau, pb=pb: v.tensor_tensor(
                    dst, ch(Bp, c)[:, tau * 512:(tau + 1) * 512], pb[:, 0:512], ALU.mult))
            vop(lambda v, c=c: v.tensor_scalar(ch(A, c), ch(A, c), colb[:, 32 + c:33 + c],
                                               None, ALU.mult, ALU.bypass))
        qT.wait(qV, qV.n)
        def proj_stream(W_src, act_pool, out_to_Bp=True, transposed_w=False):
            qV.wait(qG)
            qT.wait(qV, qV.n)
            qD.wait(qT, qT.n)
            ptok2 = [0] * 8
            ps_buf = (C[:, 0:1024], C[:, 1024:2048], C[:, 4096:5120], C[:, 5120:6144])
            ps_tok = [0, 0, 0, 0]
            gslot = [0]
            for tau in range(NTAU):
                for k in range(NCH):
                    slot = gslot[0] % 4
                    gslot[0] += 1
                    wb = ps_buf[slot]
                    if ps_tok[slot] > 0:
                        qD.wait(qT, ps_tok[slot])
                    lw = ld(wb, W_src[k * 128:(k + 1) * 128, :])
                    qT.wait(qD, lw)
                    for f in range(NCH):
                        if k == 0 and ptok2[f] > 0:
                            qT.wait(qV, ptok2[f])
                        ps_tok[slot] = mm(p[f][:, 0:512], wb[:, f * 128:(f + 1) * 128],
                                          ch(act_pool, k)[:, tau * 512:(tau + 1) * 512],
                                          k == 0, k == NCH - 1)
                qV.wait(qT, qT.n)
                for f in range(NCH):
                    ptok2[f] = vop(lambda v, f=f, tau=tau: v.tensor_copy(
                        ch(Bp, f)[:, tau * 512:(tau + 1) * 512], p[f][:, 0:512]))
            qT.wait(qV, qV.n)

        qT.wait(qV, qV.n)
        proj_stream(WoutT, A)
        for f in range(NCH):
            qG.wait(qV, qV.n)
            ld(s0_s[f * 128:(f + 1) * 128, :], ch(Bp, f), q=qG)

        barrier()
        qD.wait(qG)
        for tau in range(NTAU):
            for c in range(NCH):
                l1 = ld(sc0, xo_fm[c * 128:(c + 1) * 128, tau * 512:(tau + 1) * 512])
                l2_ = ld(sc1, g0_s[c * 128:(c + 1) * 128, tau * 512:(tau + 1) * 512])
                l3_ = ld(sc2, s0_s[c * 128:(c + 1) * 128, tau * 512:(tau + 1) * 512])
                qS.wait(qD, l2_)
                sop(lambda s: s.activation(sc1[:], sc1[:], AF.Silu))
                qV.wait(qS, qS.n)
                qV.wait(qD, l1)
                qV.wait(qD, l3_)
                dst = ch(Bp, c)[:, tau * 512:(tau + 1) * 512]
                vop(lambda v, dst=dst: v.tensor_tensor(dst, sc1[:], sc2, ALU.mult))
                vop(lambda v, dst=dst: v.tensor_tensor(dst, dst, sc0[:], ALU.add))
                vop(lambda v, dst=dst: v.tensor_tensor(sc0[:], dst, dst, ALU.mult))
                qT.wait(qV, qV.n)
                mm(p[5][0:1, 0:512], ONESCOL, dst, c == 0, c == NCH - 1)
                mm(p[6][0:1, 0:512], ONESCOL, sc0[:], c == 0, c == NCH - 1)
                qG.wait(qV, qV.n)
                ld(x1_s[c * 128:(c + 1) * 128, tau * 512:(tau + 1) * 512], dst, q=qG)
                qD.wait(qT, qT.n)
                qD.wait(qS, qS.n)
            qV.wait(qT, qT.n)
            vop(lambda v, tau=tau: v.tensor_copy(rowp[0:1, tau * 512:(tau + 1) * 512], p[5][0:1, 0:512]))
            vop(lambda v, tau=tau: v.tensor_copy(rowp[0:1, 2304 + tau * 512:2304 + (tau + 1) * 512], p[6][0:1, 0:512]))
            qT.wait(qV, qV.n)
        SU, SQ = rowp[0:1, 0:T], rowp[0:1, 2304:2304 + T]
        vop(lambda v: v.tensor_scalar(SU, SU, 1.0 / D, None, ALU.mult, ALU.bypass))
        vop(lambda v: v.tensor_scalar(SQ, SQ, 1.0 / D, None, ALU.mult, ALU.bypass))
        vop(lambda v: v.tensor_tensor(rowp[0:1, 4352:4352 + T], SU, SU, ALU.mult))
        vop(lambda v: v.tensor_tensor(SQ, SQ, rowp[0:1, 4352:4352 + T], ALU.subtract))
        qS.wait(qV, qV.n)
        sop(lambda s: s.activation(SQ, SQ, AF.Sqrt, bias=EPS5[0:1, :], scale=1.0))
        qV.wait(qS, qS.n)
        vop(lambda v: v.reciprocal(rowp[0:1, 4352:4352 + T], SQ))
        ISTD1 = rowp[0:1, 4352:4352 + T]
        qT.wait(qV, qV.n)
        xpair = ((p[5], p[6]), (p[1], p[2]))
        xtok = [0, 0]
        for c in range(NCH):
            for tau in range(NTAU):
                bi = (c * NTAU + tau) % 2
                pa_, pb_ = xpair[bi]
                if xtok[bi] > 0:
                    qT.wait(qV, xtok[bi])
                mm(pa_[:, 0:512], ONESROW, SU[0:1, tau * 512:(tau + 1) * 512], True, True)
                m7 = mm(pb_[:, 0:512], ONESROW, ISTD1[0:1, tau * 512:(tau + 1) * 512], True, True)
                qV.wait(qT, m7)
                dst = ch(A, c)[:, tau * 512:(tau + 1) * 512]
                src = ch(Bp, c)[:, tau * 512:(tau + 1) * 512]
                vop(lambda v, dst=dst, src=src, pa_=pa_: v.tensor_tensor(
                    dst, src, pa_[:, 0:512], ALU.subtract))
                xtok[bi] = vop(lambda v, dst=dst, pb_=pb_: v.tensor_tensor(
                    dst, dst, pb_[:, 0:512], ALU.mult))
            qG.wait(qV, qV.n)
            ld(xn1_s[c * 128:(c + 1) * 128, :], ch(A, c), q=qG)
        qT.wait(qV, qV.n)

        barrier()
        n_wb = ld(C[:, 0:128].rearrange("p (k e) -> p k e", e=16),
                  Wb_i.rearrange("(k p) e -> p k e", p=128))
        for c in range(NCH):
            ld(rowp[0:16, 4352 + c * 128:4352 + (c + 1) * 128], hbq[c], q=qD)
        HB = lambda c: rowp[0:16, 4352 + c * 128:4352 + (c + 1) * 128]
        qT.wait(qD)
        BETA = rowp[0:16, 0:T]
        for tau in range(NTAU):
            for k in range(NCH):
                mm(p[7][0:16, 0:512], C[:, k * 16:(k + 1) * 16],
                   ch(A, k)[:, tau * 512:(tau + 1) * 512], k == 0, k == NCH - 1)
            qS.wait(qT, qT.n)
            sop(lambda s, tau=tau: s.activation(BETA[:, tau * 512:(tau + 1) * 512],
                                                p[7][0:16, 0:512], AF.Sigmoid,
                                                bias=colb[0:16, 40:41], scale=1.0))
            qT.wait(qS, qS.n)
        qT.wait(qS, qS.n)
        NRM = rowp[0:16, 2304:2304 + T]

        def l2_apply(out_scr, fold_beta):
            for tau in range(NTAU):
                for c in range(NCH):
                    src = ch(Bp, c)[:, tau * 512:(tau + 1) * 512]
                    buf = sc0 if c % 2 == 0 else sc1
                    qV.wait(qT, qT.n)
                    vop(lambda v, src=src, buf=buf: v.tensor_tensor(buf[:], src, src, ALU.mult))
                    qT.wait(qV, qV.n)
                    mm(p[7][0:16, 0:512], SEL(c), buf[:], c == 0, c == NCH - 1)
                qV.wait(qT, qT.n)
                vop(lambda v, tau=tau: v.tensor_copy(NRM[:, tau * 512:(tau + 1) * 512],
                                                     p[7][0:16, 0:512]))
            qS.wait(qV, qV.n)
            sop(lambda s: s.activation(NRM, NRM, AF.Sqrt))
            qV.wait(qS, qS.n)
            vop(lambda v: v.tensor_scalar(NRM, NRM, 1e-12, None, ALU.max, ALU.bypass))
            vop(lambda v: v.reciprocal(NRM, NRM))
            if fold_beta:
                vop(lambda v: v.tensor_tensor(NRM, NRM, BETA, ALU.mult))
            qT.wait(qV, qV.n)
            for c in range(NCH):
                for tau in range(NTAU):
                    m8 = mm(p[7][:, 0:512], HB(c), NRM[:, tau * 512:(tau + 1) * 512], True, True)
                    qV.wait(qT, m8)
                    dst = ch(Bp, c)[:, tau * 512:(tau + 1) * 512]
                    vop(lambda v, dst=dst: v.tensor_tensor(dst, dst, p[7][:, 0:512], ALU.mult))
                    qT.wait(qV, qV.n)
                qG.wait(qV, qV.n)
                ld(out_scr[c * 128:(c + 1) * 128, :], ch(Bp, c), q=qG)

        proj_stream(Wq_i, A)
        l2_apply(qn_s, False)
        proj_stream(Wk_i, A)
        l2_apply(kb_s, True)
        proj_stream(Wv_i, A)
        qD.wait(qG)
        for c in range(NCH):
            for g4 in range(4):
                for jj in range(4):
                    i = g4 * 4 + jj
                    tr(p[7][:, jj * 128:(jj + 1) * 128], ch(Bp, c)[:, i * 128:(i + 1) * 128])
                qV.wait(qT, qT.n)
                qV.wait(qG)
                vop(lambda v: v.tensor_copy(sc0[:], p[7][:, 0:512]))
                qG.wait(qV, qV.n)
                ld(vT_s[g4 * 512:(g4 + 1) * 512, c * 128:(c + 1) * 128].rearrange(
                    "(i p) d -> p i d", p=128),
                   sc0[:].rearrange("p (i d) -> p i d", d=128), q=qG)
                qT.wait(qV, qV.n)

        barrier()
        QHp = A[:, 0:2048]
        KHp = A[:, 2048:4096]
        VTa = A[:, 4096:5120]
        VTb = A[:, 5120:6144]
        AOSp = A[:, 6144:8192]
        SC3 = C[:, 7680:8192]
        qD.wait(qG)
        qD.wait(qT, qT.n)
        qG.wait(qT, qT.n)
        qG.wait(qV, qV.n)
        ntl = [ld(TRIL(j), tril4[j], q=qG) for j in range(4)]
        qV.wait(qG, ntl[-1])
        for hp in range(H // 2):
            hA, hB = 2 * hp, 2 * hp + 1
            qV.wait(qG)
            qD.wait(qV, qV.n)
            ld(QHp[0:64, :], qn_s[hA * 64:(hA + 1) * 64, :])
            ld(QHp[64:128, :], qn_s[hB * 64:(hB + 1) * 64, :])
            ld(KHp[0:64, :], kb_s[hA * 64:(hA + 1) * 64, :])
            ld(KHp[64:128, :], kb_s[hB * 64:(hB + 1) * 64, :])
            ld(VTa.rearrange("p (i d) -> p i d", d=64),
               vT_s[:, hA * 64:(hA + 1) * 64].rearrange("(i p) d -> p i d", p=128))
            ld(VTb.rearrange("p (i d) -> p i d", d=64),
               vT_s[:, hB * 64:(hB + 1) * 64].rearrange("(i p) d -> p i d", p=128))
            qT.wait(qD)
            etok = [0, 0]
            for qtau in range(NTAU):
                jmax = qtau * 4 + 4

                def mm1p(j, qtau=qtau):
                    psA = p[0] if j % 2 == 0 else p[4]
                    psB = p[1] if j % 2 == 0 else p[5]
                    if etok[j % 2] > 0:
                        qT.wait(qV, etok[j % 2])
                    mm(psA[:, 0:512], KHp[0:64, j * 128:(j + 1) * 128],
                       QHp[0:64, qtau * 512:(qtau + 1) * 512], True, True, tp=(0, 0))
                    return mm(psB[:, 0:512], KHp[64:128, j * 128:(j + 1) * 128],
                              QHp[64:128, qtau * 512:(qtau + 1) * 512], True, True, tp=(64, 0))

                t1 = mm1p(0)
                for j in range(jmax):
                    psA = p[0] if j % 2 == 0 else p[4]
                    psB = p[1] if j % 2 == 0 else p[5]
                    scbA = sc0[:] if j % 2 == 0 else sc1[:]
                    scbB = sc2 if j % 2 == 0 else SC3
                    qV.wait(qT, t1)
                    o = j - qtau * 4
                    if o >= 0:
                        vop(lambda v, scbA=scbA, psA=psA, o=o: v.tensor_tensor(
                            scbA, psA[:, 0:512], TRIL(o), ALU.mult))
                        etok[j % 2] = vop(lambda v, scbB=scbB, psB=psB, o=o: v.tensor_tensor(
                            scbB, psB[:, 0:512], TRIL(o), ALU.mult))
                    else:
                        vop(lambda v, scbA=scbA, psA=psA: v.tensor_copy(
                            scbA, psA[:, 0:512]))
                        etok[j % 2] = vop(lambda v, scbB=scbB, psB=psB: v.tensor_copy(
                            scbB, psB[:, 0:512]))
                    if j + 1 < jmax:
                        t1 = mm1p(j + 1)
                    qT.wait(qV, etok[j % 2])
                    mm(p[2][0:64, 0:512], VTa[:, j * 64:(j + 1) * 64],
                       scbA,
                       j == 0, j == jmax - 1, tp=(0, 0))
                    mm(p[3][64:128, 0:512], VTb[:, j * 64:(j + 1) * 64],
                       scbB,
                       j == 0, j == jmax - 1, tp=(0, 64))
                qV.wait(qT, qT.n)
                vop(lambda v, qtau=qtau: v.tensor_copy(
                    AOSp[0:64, qtau * 512:(qtau + 1) * 512], p[2][0:64, 0:512]))
                vop(lambda v, qtau=qtau: v.tensor_copy(
                    AOSp[64:128, qtau * 512:(qtau + 1) * 512], p[3][64:128, 0:512]))
                qT.wait(qV, qV.n)
            qG.wait(qV, qV.n)
            ld(ao_s[hA * 64:(hA + 1) * 64, :], AOSp[0:64, :], q=qG)
            ld(ao_s[hB * 64:(hB + 1) * 64, :], AOSp[64:128, :], q=qG)
            qD.wait(qG)
        for k in range(NCH):
            ld(ch(A, k), ao_s[k * 128:(k + 1) * 128, :])
        qT.wait(qD)
        proj_stream(Wo_i, A)
        for f in range(NCH):
            qG.wait(qV, qV.n)
            ld(s1_s[f * 128:(f + 1) * 128, :], ch(Bp, f), q=qG)

        barrier()
        kst0 = C16[:, 0:2048]
        kst1 = C16[:, 2048:4096]
        keff = C16[:, 4096:6144]
        XNt = C[:, 3072:7168]
        OUTS = C[:, 3072:7168]
        GT = sc2
        XHt = A16[:, 0:4096]
        XLt = A16[:, 4096:8192]
        D1h = B16[:, 0:16384]
        D1l = B16[:, 16384:32768]
        D2h = A16[:, 0:16384]
        D2l = A16[:, 16384:32768]
        STAGE = A[:, 2048:3072]
        qD.wait(qG)
        qG.wait(qD)
        for tau in range(NTAU):
            qD.wait(qT, qT.n)
            qD.wait(qG)
            nl = []
            for c in range(NCH):
                nl.append(ld(XNt[:, c * 512:(c + 1) * 512],
                             xn1_s[c * 128:(c + 1) * 128, tau * 512:(tau + 1) * 512]))
            qV.wait(qD)
            for c in range(NCH):
                xs = slice(c * 512, (c + 1) * 512)
                vop(lambda v, xs=xs: v.tensor_copy(XHt[:, xs], XNt[:, xs]))
                vop(lambda v, xs=xs: v.tensor_tensor(sc0[:], XNt[:, xs], XHt[:, xs], ALU.subtract))
                vop(lambda v, xs=xs: v.tensor_copy(XLt[:, xs], sc0[:]))
            ft0 = [0, 0]
            splitdone0 = [0, 0]
            subtok = [0, 0]
            mmtok0 = [0, 0]
            ck0 = {}
            ck0[0] = ld(kst0[:, 0:2048], k0r[0].rearrange("k b f -> k (b f)"), q=qG)
            for fb in range(32):
                par = fb % 2
                kq = kst0 if par == 0 else kst1
                pp = p[3] if par == 0 else p[7]
                sa = sc0 if par == 0 else sc1
                sb = sc1 if par == 0 else sc0
                qV.wait(qG, ck0[fb])
                if mmtok0[par] > 0:
                    qV.wait(qT, mmtok0[par])
                ke = keff[:, 0:1024] if par == 0 else keff[:, 1024:2048]
                subtok[par] = vop(lambda v, kq=kq, ke=ke: v.tensor_tensor(
                    ke, kq[:, 0:1024], kq[:, 1024:2048], ALU.subtract))
                if fb + 1 < 32:
                    npar = (fb + 1) % 2
                    nkq = kst0 if npar == 0 else kst1
                    if subtok[npar] > 0:
                        qG.wait(qV, subtok[npar])
                    ck0[fb + 1] = ld(nkq[:, 0:2048],
                                     k0r[fb + 1].rearrange("k b f -> k (b f)"), q=qG)
                qT.wait(qV, subtok[par])
                if ft0[par] > 0:
                    qT.wait(qS, ft0[par])
                for kb in range(8):
                    stat = ke[:, kb * 128:(kb + 1) * 128]
                    mm(pp[:, 0:512], stat, XHt[:, kb * 512:(kb + 1) * 512], kb == 0, False)
                    mm(pp[:, 0:512], stat, XLt[:, kb * 512:(kb + 1) * 512], False, kb == 7)
                mmtok0[par] = qT.n
                qS.wait(qT, qT.n)
                if splitdone0[par] > 0:
                    qS.wait(qV, splitdone0[par])
                ft0[par] = sop(lambda s, pp=pp, sa=sa: s.activation(
                    sa[:], pp[:, 0:512], AF.Relu, bias=0.0, scale=S0C8))
                qV.wait(qS, ft0[par])
                fs = slice(fb * 512, (fb + 1) * 512)
                vop(lambda v, fs=fs, sa=sa: v.tensor_copy(D1h[:, fs], sa[:]))
                vop(lambda v, fs=fs, sa=sa, sb=sb: v.tensor_tensor(sb[:], sa[:], D1h[:, fs], ALU.subtract))
                splitdone0[par] = vop(lambda v, fs=fs, sb=sb: v.tensor_copy(D1l[:, fs], sb[:]))

            def rowsum_neg_half(Dh, Dl, nkb):
                qT.wait(qV, qV.n)
                qT.wait(qS, qS.n)
                for kb in range(nkb):
                    mm(p[4][0:1, 0:512], NEGH16, Dh[:, kb * 512:(kb + 1) * 512], kb == 0, False)
                    mm(p[4][0:1, 0:512], NEGH16, Dl[:, kb * 512:(kb + 1) * 512], False, kb == nkb - 1)
                qV.wait(qT, qT.n)
                vop(lambda v: v.tensor_copy(rowp[0:1, 0:512], p[4][0:1, 0:512]))
                qT.wait(qV, qV.n)
                m9 = mm(p[5][:, 0:512], ONESROW, rowp[0:1, 0:512], True, True)
                qV.wait(qT, m9)
                vop(lambda v: v.tensor_copy(rsb[:], p[5][:, 0:512]))
                qT.wait(qV, qV.n)

            rowsum_neg_half(D1h, D1l, 32)
            ft = [0, 0]
            kbufs = (kst0, kst1, keff)
            mm_done = [0, 0, 0]
            ck_tok = {}
            nhalf = 64
            qG.wait(qT, qT.n)
            qG.wait(qV, qV.n)
            for hidx in range(2):
                bslot = hidx % 3
                ck_tok[hidx] = ld(kbufs[bslot][:, 0:2048],
                                  k1r[0, :, hidx * 16:(hidx + 1) * 16, :].rearrange("k b f -> k (b f)"), q=qG)
            for fb in range(32):
                pp = p[3] if fb % 2 == 0 else p[7]
                sa = sc0 if fb % 2 == 0 else sc1
                sb = sc1 if fb % 2 == 0 else sc0
                if ft[fb % 2] > 0:
                    qT.wait(qV, ft[fb % 2])
                for half2 in range(2):
                    hidx = fb * 2 + half2
                    bslot = hidx % 3
                    kq = kbufs[bslot]
                    qT.wait(qG, ck_tok[hidx])
                    for kb in range(16):
                        gkb = half2 * 16 + kb
                        stat = kq[:, kb * 128:(kb + 1) * 128]
                        mm(pp[:, 0:512], stat, D1h[:, gkb * 512:(gkb + 1) * 512],
                           gkb == 0, False)
                        mm(pp[:, 0:512], stat, D1l[:, gkb * 512:(gkb + 1) * 512],
                           False, gkb == 31)
                    mm_done[bslot] = qT.n
                    nxt = hidx + 2
                    if nxt < nhalf:
                        nslot = nxt % 3
                        qG.wait(qT, mm_done[nslot])
                        ck_tok[nxt] = ld(kbufs[nslot][:, 0:2048],
                                         k1r[nxt // 2, :, (nxt % 2) * 16:(nxt % 2 + 1) * 16, :]
                                         .rearrange("k b f -> k (b f)"), q=qG)
                qV.wait(qT, qT.n)
                ft[fb % 2] = vop(lambda v, pp=pp, sa=sa: v.tensor_tensor(sa[:], pp[:, 0:512], rsb[:], ALU.add))
                qS.wait(qV, ft[fb % 2])
                stok = sop(lambda s, sa=sa: s.activation(sa[:], sa[:], AF.Relu, bias=0.0, scale=S1C4))
                qV.wait(qS, stok)
                fs = slice(fb * 512, (fb + 1) * 512)
                vop(lambda v, fs=fs, sa=sa: v.tensor_copy(D2h[:, fs], sa[:]))
                vop(lambda v, fs=fs, sa=sa, sb=sb: v.tensor_tensor(sb[:], sa[:], D2h[:, fs], ALU.subtract))
                vop(lambda v, fs=fs, sb=sb: v.tensor_copy(D2l[:, fs], sb[:]))
            rowsum_neg_half(D2h, D2l, 32)
            mm_done = [0, 0, 0]
            ck_tok = {}
            nhalf2 = 16
            qG.wait(qT, qT.n)
            for hidx in range(2):
                ck_tok[hidx] = ld(kbufs[hidx % 3][:, 0:2048],
                                  k2r[0, :, hidx * 16:(hidx + 1) * 16, :].rearrange("k b f -> k (b f)"), q=qG)
            for fb in range(8):
                pp = p[3] if fb % 2 == 0 else p[7]
                for half2 in range(2):
                    hidx = fb * 2 + half2
                    bslot = hidx % 3
                    kq = kbufs[bslot]
                    qT.wait(qG, ck_tok[hidx])
                    for kb in range(16):
                        gkb = half2 * 16 + kb
                        stat = kq[:, kb * 128:(kb + 1) * 128]
                        mm(pp[:, 0:512], stat, D2h[:, gkb * 512:(gkb + 1) * 512],
                           gkb == 0, False)
                        mm(pp[:, 0:512], stat, D2l[:, gkb * 512:(gkb + 1) * 512],
                           False, gkb == 31)
                    mm_done[bslot] = qT.n
                    nxt = hidx + 2
                    if nxt < nhalf2:
                        nslot = nxt % 3
                        qG.wait(qT, mm_done[nslot])
                        ck_tok[nxt] = ld(kbufs[nslot][:, 0:2048],
                                         k2r[nxt // 2, :, (nxt % 2) * 16:(nxt % 2 + 1) * 16, :]
                                         .rearrange("k b f -> k (b f)"), q=qG)
                qV.wait(qT, qT.n)
                vop(lambda v, pp=pp: v.tensor_tensor(sc0[:], pp[:, 0:512], rsb[:], ALU.add))
                qS.wait(qV, qV.n)
                sop(lambda s: s.activation(GT, sc0[:], AF.Silu, bias=0.0, scale=S2C))
                qD.wait(qV, qV.n)
                qD.wait(qS, qS.n)
                lx = ld(sc0, x1_s[fb * 128:(fb + 1) * 128, tau * 512:(tau + 1) * 512])
                ls = ld(sc1, s1_s[fb * 128:(fb + 1) * 128, tau * 512:(tau + 1) * 512])
                qV.wait(qS, qS.n)
                qV.wait(qD, lx)
                qV.wait(qD, ls)
                vop(lambda v: v.tensor_tensor(GT, GT, sc1[:], ALU.mult))
                vop(lambda v: v.tensor_tensor(GT, GT, sc0[:], ALU.add))
                vop(lambda v, fb=fb: v.tensor_copy(OUTS[:, fb * 512:(fb + 1) * 512], GT))
                qD.wait(qV, qV.n)
                qT.wait(qV, qV.n)
            for tt_i in range(4):
                qV.wait(qG)
                for c in range(NCH):
                    tr(p[6][:, (c % 4) * 128:((c % 4) + 1) * 128],
                       OUTS[:, c * 512 + tt_i * 128:c * 512 + (tt_i + 1) * 128])
                    if c % 4 == 3:
                        qV.wait(qT, qT.n)
                        vop(lambda v, c=c, tt_i=tt_i: v.tensor_copy(
                            STAGE[:, (c - 3) * 128:(c + 1) * 128], p[6][:, 0:512]))
                        qT.wait(qV, qV.n)
                qG.wait(qV, qV.n)
                ld(out[tau * 512 + tt_i * 128:tau * 512 + (tt_i + 1) * 128, :],
                   STAGE, q=qG)

        final_g = qG.n
        qD.wait(qG)
        qV.wait(qG)

        with nc.Block() as block:
            @block.tensor
            def _(t):
                qT.emit(t)

            @block.vector
            def _(v):
                qV.emit(v)

            @block.scalar
            def _(s):
                qS.emit(s)

            @block.sync
            def _(d):
                qD.emit(d)

            @block.gpsimd
            def _(g):
                qG.emit(g)

    return nc, dram


_NC_CACHE = {}


def _consts():
    ident = np.eye(128, dtype=np.float32)
    iota = np.broadcast_to(np.arange(8, dtype=np.float32), (128, 16, 8)).reshape(128, 128).copy()
    iota8 = np.arange(8, dtype=np.float32).reshape(8, 1)
    tril = np.zeros((4, 128, 512), np.float32)
    for o in range(4):
        kk = o * 128 + np.arange(128)[:, None]
        qq = np.arange(512)[None, :]
        tril[o] = (kk <= qq).astype(np.float32)
    sel = np.zeros((8, 128, 16), np.float32)
    for c in range(8):
        sel[c, 0:64, 2 * c] = 1.0
        sel[c, 64:128, 2 * c + 1] = 1.0
    hb = np.zeros((8, 16, 128), np.float32)
    for c in range(8):
        hb[c, 2 * c, 0:64] = 1.0
        hb[c, 2 * c + 1, 64:128] = 1.0
    ones = np.ones((1, 128), np.float32)
    return ident, iota, iota8, tril, sel, hb, ones


def _shared_inputs(inp):
    x = np.ascontiguousarray(inp["L0_attn_x"].astype(np.float32, copy=False))
    ident, iota, iota8, tril, sel, hb, ones = _consts()
    return x, {
        "x_tm": x,
        "x_fm": np.ascontiguousarray(x.transpose(0, 2, 1)),
        "WinT": np.ascontiguousarray(inp["L0_attn_W_in"].T),
        "WigT": np.ascontiguousarray(inp["L0_attn_W_ig"].T),
        "WrgT": np.ascontiguousarray(inp["L0_attn_W_rg"].T),
        "WoutT": np.ascontiguousarray(inp["L0_attn_W_out"].T),
        "Wr_fm": np.ascontiguousarray(
            inp["L0_ffn_Wr"].reshape(8, 128, 8).transpose(1, 0, 2).reshape(128, 64)),
        "br_row": np.ascontiguousarray(inp["L0_ffn_br"].reshape(1, 8)),
        "big_col": np.ascontiguousarray(inp["L0_attn_b_ig"].reshape(8, 128).T),
        "brg_col": np.ascontiguousarray(inp["L0_attn_b_rg"].reshape(8, 128).T),
        "lam_col": np.ascontiguousarray(inp["L0_attn_lam"].reshape(8, 128).T),
        "h0_col": np.ascontiguousarray(inp["L0_attn_h0"].reshape(8, 128).T),
        "nw_col": np.ascontiguousarray(inp["L0_attn_norm_w"].reshape(8, 128).T),
        "W1moe": np.ascontiguousarray(inp["L0_ffn_W1"].astype(np.float32, copy=False)),
        "W2moe": np.ascontiguousarray(inp["L0_ffn_W2"].astype(np.float32, copy=False)),
        "b1_col": np.ascontiguousarray(inp["L0_ffn_b1"].reshape(8, 8, 128).transpose(0, 2, 1)),
        "b2stat": np.ascontiguousarray(inp["L0_ffn_b2"].reshape(8, 8, 128)),
        "Wq": np.ascontiguousarray(inp["L1_attn_Wq"]),
        "Wk": np.ascontiguousarray(inp["L1_attn_Wk"]),
        "Wv": np.ascontiguousarray(inp["L1_attn_Wv"]),
        "Wo": np.ascontiguousarray(inp["L1_attn_Wo"]),
        "Wb": np.ascontiguousarray(inp["L1_attn_Wb"]),
        "bb_col": np.ascontiguousarray(inp["L1_attn_bb"].reshape(16, 1)),
        "k0r": np.ascontiguousarray(inp["L1_ffn_k0"].astype(np.uint8)
                                    .reshape(16, 128, 32, 128).transpose(2, 1, 0, 3)),
        "k1r": np.ascontiguousarray(inp["L1_ffn_k1"].astype(np.uint8)
                                    .reshape(32, 128, 32, 128).transpose(2, 1, 0, 3)),
        "k2r": np.ascontiguousarray(inp["L1_ffn_k2"].astype(np.uint8)
                                    .reshape(32, 128, 8, 128).transpose(2, 1, 0, 3)),
        "s_sc": np.array([[float(inp["L1_ffn_s0"]), float(inp["L1_ffn_s1"]),
                           float(inp["L1_ffn_s2"])]], np.float32),
        "ident_c": ident, "iota_e": iota, "iota8c": iota8, "tril4": tril,
        "selq": sel, "hbq": hb, "onesrow": ones,
        "e8sel": np.ascontiguousarray(
            np.stack([np.where(np.arange(8)[:, None] == e, 1.0, 0.0)
                      .repeat(128, axis=1)[:, :128] for e in range(8)], axis=1)
            .reshape(8, 1024).astype(np.float32)),
    }


def kernel(**inputs):
    inp = {k: np.asarray(v) for k, v in inputs.items()}
    x, shared = _shared_inputs(inp)

    if "nc" not in _NC_CACHE:
        _NC_CACHE["nc"] = build_nc(debug=False)
    nc, dram = _NC_CACHE["nc"]

    in_maps = []
    for b in range(B):
        m = dict(shared)
        m["xo_tm"] = np.ascontiguousarray(x[b])
        m["xo_fm"] = np.ascontiguousarray(x[b].T)
        oh = np.zeros((1, 8), np.float32)
        oh[0, b] = 1.0
        m["coreoh"] = oh
        in_maps.append(m)

    res = run_bass_kernel_spmd(nc, in_maps, core_ids=list(range(8)))
    outs = np.stack([res.results[b]["out"] for b in range(B)], axis=0)
    return outs
